# version 14
# baseline (speedup 1.0000x reference)
import numpy as np
try:
    import concourse.bass as bass
    import concourse.mybir as mybir
    from concourse.bass_utils import run_bass_kernel_spmd
except ImportError:
    import sys
    sys.path.insert(0, '/opt/trn_rl_repo')
    import concourse.bass as bass
    import concourse.mybir as mybir
    from concourse.bass_utils import run_bass_kernel_spmd

F32 = mybir.dt.float32
F32R = mybir.dt.float32r
BF16 = mybir.dt.bfloat16
U8 = mybir.dt.uint8
I32 = mybir.dt.int32
AF = mybir.ActivationFunctionType
ALU = mybir.AluOpType
AX = mybir.AxisListType

NS_SCHED = [
    (3.7111771581458646, -5.620608272237417, 2.1103471800461225),
    (3.329413233677866, -4.062595212811175, 1.2393093931296458),
    (2.9674567262212846, -2.86852510148617, 0.6778708762095051),
    (3.2380560803830135, -3.7372616312341555, 1.0783592811783235),
    (2.981962845278586, -2.911994792086269, 0.6975370488974846),
    (2.8932135288128827, -2.651149828439036, 0.5800612344955369),
    (3.033845782358891, -3.070206636005876, 0.7693889436987285),
    (3.0801300104202463, -3.2150378112771756, 0.8355372166767617),
    (3.1495997998475583, -3.439147089661531, 0.9385570822404736),
    (2.9131136983260744, -2.7085892242404913, 0.6058194688704685),
    (2.9479683094892326, -2.810645048711703, 0.6517416272859141),
    (2.7769556328409347, -2.3271824654484655, 0.43602047708272),
    (3.0105414639697745, -2.998607791517297, 0.7368168363836698),
    (3.4322970197419605, -4.451306469619516, 1.4440793216272883),
    (2.9562048202915693, -2.835034724402922, 0.6627439788262405),
    (3.0908765846627465, -4.007390087857139, 1.3222944767008617),
    (3.151104560640553, -4.25921535736201, 1.499665349660536),
    (2.7207321333327457, -2.7599487977921504, 0.7652923993101875),
    (1.875, -1.25, 0.375),
    (1.875, -1.25, 0.375),
    (1.875, -1.25, 0.375),
    (1.875, -1.25, 0.375),
]

B, T, C, DH = 4, 4096, 1024, 4096
PT = C // 128
HT = 2048
KB = 1024
RGROUPS = [[0, 1], [2, 3], [4, 5], [6, 7]]


class Sem:
    def __init__(self, h):
        self.h = h
        self.n = 0


class Ctx:
    def __init__(self, nc, eng_name, engines, sems, bufs, dbg):
        self.nc = nc
        self.eng = eng_name
        self.E = engines
        self.sems = sems
        self.b = bufs
        self.dbg = dbg

    def w(self, name, sem, val=None):
        if self.eng == name:
            self.E[name].wait_ge(sem.h, sem.n if val is None else val)

    def op(self, name, meth, *a, **kw):
        if self.eng == name:
            return getattr(self.E[name], meth)(*a, **kw)
        return None

    def do(self, name, sem, meth, *a, waits=(), inc_k=1, **kw):
        for (s, v) in waits:
            self.w(name, s, v)
        ins = self.op(name, meth, *a, **kw)
        if ins is not None:
            ins.then_inc(sem.h, inc_k)
        sem.n += inc_k
        return ins

    def sync_all(self, engs=('v', 'a', 't', 'sy', 'g')):
        for e in engs:
            for s in self.sems.values():
                self.w(e, s)


def emit_ns(ctx):
    b = ctx.b
    s_dma, s_pe, s_v, s_a = (ctx.sems[k] for k in ('sy_dma', 'pe', 'v', 'a'))
    X, XT, M, M2, S = b['nsA'], b['nsB'], b['nsC'], b['nsD'], b['nsE']
    psh = b['ps']
    fcol, sc_col = b['fcol'], b['sc_col']
    onesc, onesr, identf = b['onesc'], b['onesr'], b['identf']
    S32 = b['nsE32']

    s_gd0 = ctx.sems['g_dma']
    for ci in range(PT):
        ctx.do('g', s_gd0, 'dma_start', out=S[:, ci, :],
               in_=b['in_gsr'][ci * 128:(ci + 1) * 128, :], inc_k=16)

    for ci in range(PT):
        ctx.do('a', s_a, 'activation', out=psh[:, 6, 0:512], in_=S32[:, ci, 0:512],
               func=AF.Square, accum_out=fcol[:, ci:ci + 1],
               waits=[(s_gd0, None), (s_a, None)])
        ctx.do('a', s_a, 'activation', out=psh[:, 7, 0:512], in_=S32[:, ci, 512:1024],
               func=AF.Square, accum_out=fcol[:, 8 + ci:9 + ci],
               waits=[(s_a, None)])
    ctx.do('v', s_v, 'reduce_sum', out=sc_col[:, 1:2], in_=fcol[:, 0:16], axis=AX.X,
           waits=[(s_a, None)])
    ctx.do('t', s_pe, 'matmul', psh[0:1, 0, 0:1], onesc[:, 0:1], sc_col[:, 1:2],
           start=True, stop=True, waits=[(s_v, None)])
    ctx.do('a', s_a, 'activation', out=fcol[0:1, 0:1], in_=psh[0:1, 0, 0:1],
           func=AF.Sqrt, waits=[(s_pe, None), (s_a, None)])
    ctx.do('v', s_v, 'reciprocal', out=fcol[0:1, 1:2], in_=fcol[0:1, 0:1],
           waits=[(s_a, None), (s_v, None)])
    ctx.do('t', s_pe, 'matmul', psh[:, 0, 1:2], onesr[0:1, :], fcol[0:1, 1:2],
           start=True, stop=True, waits=[(s_v, None)])
    ctx.do('a', s_a, 'copy', out=sc_col[:, 0:1], in_=psh[:, 0, 1:2],
           waits=[(s_pe, None), (s_a, None)])

    for ci in range(PT):
        ctx.do('v', s_v, 'tensor_scalar_mul', out=X[:, ci, :], in0=S32[:, ci, :],
               scalar1=sc_col[:, 0:1], waits=[(s_a, None)] if ci == 0 else ())
    ctx.w('t', s_a)
    v_base, pe_base, ti = s_v.n, s_pe.n, 0
    for cj in range(PT):
        for ci in range(PT):
            bank = ti % 2
            ctx.do('t', s_pe, 'transpose',
                   psh[:, bank, 0:128], S32[:, ci, cj * 128:(cj + 1) * 128], identf[:, :],
                   waits=([(s_v, v_base + ti - 1)] if ti >= 2 else []))
            ctx.do('v', s_v, 'tensor_scalar_mul',
                   out=XT[:, cj, ci * 128:(ci + 1) * 128],
                   in0=psh[:, bank, 0:128], scalar1=sc_col[:, 0:1],
                   waits=[(s_pe, pe_base + ti + 1)])
            ti += 1

    lns = (ctx.sems['ln0'], ctx.sems['ln1'])
    lnc = b['lncols']
    psb = b['psb']
    xtv = b['xtl'][:, 0, :]
    xntl, xnful = b['xntl'], b['xnful']
    RL = {}
    lnst = {'tt': 0}

    def ln_tile():
        tt = lnst['tt']
        lnst['tt'] += 1
        par = tt % 2
        sp = lns[par]
        w = []
        if tt >= 1:
            w += [RL[('lnv', tt - 1)], RL[('lnaq', tt - 1)]]
        ctx.do('sy', sp, 'dma_start', out=xtv,
               in_=b['in_xh'][tt * 128:(tt + 1) * 128, :], waits=w, inc_k=16)
        ctx.do('a', s_a, 'activation', out=b['scs32'][:, :], in_=xtv[:, 0:512],
               func=AF.Square, accum_out=lnc[:, 0:1],
               waits=[(sp, None), (s_a, None)])
        ctx.do('a', s_a, 'activation', out=b['scs32'][:, :], in_=xtv[:, 512:1024],
               func=AF.Square, accum_out=lnc[:, 8:9], waits=[(s_a, None)])
        RL[('lnaq', tt)] = (s_a, s_a.n)
        ctx.do('v', s_v, 'reduce_sum', out=lnc[:, 2:3], in_=xtv, axis=AX.X,
               waits=[(sp, None), (s_v, None)])
        ctx.do('v', s_v, 'tensor_scalar_mul', out=lnc[:, 3:4], in0=lnc[:, 2:3],
               scalar1=-1.0 / 1024.0, waits=[(s_v, None)])
        ctx.do('v', s_v, 'tensor_mul', out=lnc[:, 4:5], in0=lnc[:, 3:4],
               in1=lnc[:, 3:4], waits=[(s_v, None)])
        ctx.do('v', s_v, 'tensor_add', out=lnc[:, 9:10], in0=lnc[:, 0:1],
               in1=lnc[:, 8:9], waits=[RL[('lnaq', tt)], (s_v, None)])
        ctx.do('v', s_v, 'scalar_tensor_tensor', out=lnc[:, 5:6], in0=lnc[:, 9:10],
               scalar=1.0 / 1024.0, in1=lnc[:, 4:5], op0=ALU.mult,
               op1=ALU.subtract, waits=[(s_v, None)])
        ctx.do('a', s_a, 'activation', out=lnc[:, 6:7], in_=lnc[:, 5:6],
               func=AF.Sqrt, bias=b['cc'][:, 0:1], waits=[(s_v, None), (s_a, None)])
        ctx.do('v', s_v, 'reciprocal', out=lnc[:, 7:8], in_=lnc[:, 6:7],
               waits=[(s_a, None), (s_v, None)])
        xw = [(s_v, None)]
        if tt >= 2:
            xw.append(RL[('lntr', tt - 2)])
        ctx.do('v', s_v, 'tensor_scalar', out=xntl[:, par, :], in0=xtv,
               scalar1=lnc[:, 3:4], scalar2=lnc[:, 7:8], op0=ALU.add,
               op1=ALU.mult, waits=xw)
        RL[('lnv', tt)] = (s_v, s_v.n)
        ctx.w('t', *RL[('lnv', tt)])
        for ci in range(PT):
            bank = 6 + ci % 2
            tw = []
            if ci >= 2 and ('lne', tt, ci - 2) in RL:
                tw.append(RL[('lne', tt, ci - 2)])
            elif ci < 2 and tt >= 1:
                tw.append(RL[('lne', tt - 1, ci + 6)])
            ctx.do('t', s_pe, 'transpose', psb[:, bank, 0:128],
                   xntl[:, par, ci * 128:(ci + 1) * 128], b['identb'][:, :],
                   waits=tw)
            ctx.do('a', s_a, 'copy',
                   out=xnful[:, ci, tt * 128:(tt + 1) * 128],
                   in_=psb[:, bank, 0:128],
                   waits=[(s_pe, None), (s_a, None)])
            RL[('lne', tt, ci)] = (s_a, s_a.n)
        RL[('lntr', tt)] = (s_pe, s_pe.n)

    def xn_transpose(src_b, dst, evrec):
        src32 = src_b.bitcast(F32)
        pe_b, a_b = s_pe.n, s_a.n
        ti = 0
        for cj in range(PT):
            ctx.w('t', *evrec[cj])
            for ci in range(PT):
                bank = ti % 2
                tw = [(s_a, a_b + ti - 1)] if ti >= 2 else []
                ctx.do('t', s_pe, 'transpose', psh[:, bank, 0:128],
                       src32[:, cj, ci * 128:(ci + 1) * 128], identf[:, :],
                       waits=tw)
                ctx.do('a', s_a, 'copy',
                       out=dst[:, ci, cj * 128:(cj + 1) * 128],
                       in_=psh[:, bank, 0:128],
                       waits=[(s_pe, pe_b + ti + 1), (s_a, None)])
                ti += 1

    bufs = [X, XT, M, M2, S]
    for itn, (ca, cb, cc) in enumerate(NS_SCHED):
        Xc, XTc, Mc, M2c, Sc = bufs
        ns_mm(ctx, out=Mc, lhsT=Xc, rhs=Xc, evac='copy', psh=psh,
              twaits=[(s_v, None)])
        ns_mm(ctx, out=M2c, lhsT=Mc, rhs=Mc, evac='copy', psh=psh)
        ctx.w('v', s_a)
        ctx.w('v', s_v)
        for ci in range(PT):
            ctx.do('v', s_v, 'tensor_scalar_mul', out=Sc[:, ci, :], in0=Mc[:, ci, :],
                   scalar1=float(cb))
        ctx.w('v', s_v)
        for ci in range(PT):
            ctx.do('v', s_v, 'scalar_tensor_tensor', out=Mc[:, ci, :], in0=M2c[:, ci, :],
                   scalar=float(cc), in1=Sc[:, ci, :], op0=ALU.mult, op1=ALU.add)
        evrec = ns_mm(ctx, out=Sc, lhsT=Mc, rhs=XTc, evac='axpy', psh=psh,
                      aval=float(ca), xbuf=XTc,
                      twaits=[(s_v, None)], evwaits=[(s_v, None)])
        xn_transpose(Sc, M2c, evrec)
        if itn < 16:
            ln_tile()
        bufs = [M2c, Sc, Xc, XTc, Mc]

    Xf = bufs[0]
    ctx.w('v', s_v)
    ctx.w('v', s_pe)
    ctx.w('v', s_a)
    for ci in range(PT):
        ctx.do('v', s_v, 'tensor_copy', out=b['Wbf'][:, ci, :], in_=Xf[:, ci, :])
    vwb = s_v.n
    if ctx.dbg == 'ns':
        sd = ctx.sems['g_dma']
        ctx.w('g', ctx.sems['v'])
        for ci in range(PT):
            ctx.do('g', sd, 'dma_start', out=b['out_dbg'][ci * 128:(ci + 1) * 128, 0:C],
                   in_=Xf[:, ci, :].bitcast(F32), inc_k=16)
        ctx.w('g', sd)
        return

    s_gd0b = ctx.sems['g_dma']
    ctx.w('g', s_v)
    ctx.w('g', s_pe)
    ctx.w('g', s_a)
    ctx.do('g', s_gd0b, 'dma_start', out=b['wupq'][:, :, :],
           in_=b['in_wupT'].rearrange("(a b) j -> b a j", b=128), inc_k=16)
    Xq = b['xscan']
    pe_b, a_b = s_pe.n, s_a.n
    gi = 0
    for jt in range(PT):
        for tc in range(4):
            bank = 2 + (gi % 4)
            if gi == 0:
                ctx.w('t', s_gd0b)
                ctx.w('t', s_a)
                ctx.w('t', s_v)
            if gi >= 4:
                ctx.w('t', s_a, a_b + (gi - 4) + 1)
            ins = None
            for kt in range(PT):
                ins = ctx.op('t', 'matmul', psh[:, bank, 0:512],
                             b['wupq'][:, kt, jt * 128:(jt + 1) * 128],
                             b['xnful'][:, kt, tc * 512:(tc + 1) * 512],
                             start=(kt == 0), stop=(kt == PT - 1))
            if ins is not None:
                ins.then_inc(s_pe.h, 1)
            s_pe.n += 1
            ctx.do('a', s_a, 'copy', out=Xq[:, jt, tc * 512:(tc + 1) * 512],
                   in_=psh[:, bank, 0:512],
                   waits=[(s_pe, pe_b + gi + 1), (s_a, None)] +
                         ([(s_v, vwb)] if gi == 0 else []))
            gi += 1


def ns_mm(ctx, out, lhsT, rhs, evac, psh, aval=None, xbuf=None, twaits=(), evwaits=()):
    s_pe, s_v, s_a = (ctx.sems[k] for k in ('pe', 'v', 'a'))
    s_ev = s_a if evac == 'copy' else s_v
    ev_eng = 'a' if evac == 'copy' else 'v'
    for (s, v) in twaits:
        ctx.w('t', s, v)
    ctx.w('t', s_a)
    ctx.w('t', s_v)
    for (s, v) in evwaits:
        ctx.w(ev_eng, s, v)
    pe_base, ev_base = s_pe.n, s_ev.n
    evrec = []
    gi = 0
    for it in range(PT):
        for jc in range(2):
            bank = 2 + (gi % 4)
            if gi >= 4:
                ctx.w('t', s_ev, ev_base + (gi - 4) + 1)
            ins = None
            for kt in range(PT):
                ins = ctx.op('t', 'matmul', psh[:, bank, 0:512],
                             lhsT[:, kt, it * 128:(it + 1) * 128],
                             rhs[:, kt, jc * 512:(jc + 1) * 512],
                             start=(kt == 0), stop=(kt == PT - 1))
            if ins is not None:
                ins.then_inc(s_pe.h, 1)
            s_pe.n += 1
            ctx.w(ev_eng, s_pe, pe_base + gi + 1)
            if evac == 'copy':
                ctx.do('a', s_a, 'copy', out=out[:, it, jc * 512:(jc + 1) * 512],
                       in_=psh[:, bank, 0:512], waits=[(s_a, None)])
            else:
                ctx.do('v', s_v, 'scalar_tensor_tensor',
                       out=out[:, it, jc * 512:(jc + 1) * 512],
                       in0=xbuf[:, it, jc * 512:(jc + 1) * 512],
                       scalar=aval, in1=psh[:, bank, 0:512],
                       op0=ALU.mult, op1=ALU.add)
            gi += 1
        evrec.append((s_ev, s_ev.n))
    return evrec


def emit_ln(ctx, src_hbm, src0, ntiles, xn_dst, dst_off, xt, xntok):
    b = ctx.b
    s_v, s_a, s_pe = (ctx.sems[k] for k in ('v', 'a', 'pe'))
    lns = (ctx.sems['ln0'], ctx.sems['ln1'])
    lnc = b['lncols']
    psh = b['ps']
    v_base, a_base = s_v.n, s_a.n
    for tt in range(ntiles):
        par = tt % 2
        sp = lns[par]
        ctx.do('sy', sp, 'dma_start', out=xt[:, par, :],
               in_=src_hbm[src0 + tt * 128:src0 + (tt + 1) * 128, :],
               waits=[(s_v, v_base + tt * 7)] if tt >= 2 else [(s_v, v_base)],
               inc_k=16)
        sq = lnc[:, par:par + 1]
        ctx.do('a', s_a, 'activation', out=psh[:, 6, 0:512], in_=xt[:, par, 0:512],
               func=AF.Square, accum_out=sq,
               waits=[(sp, None), (s_a, None)])
        ctx.do('a', s_a, 'activation', out=psh[:, 7, 0:512], in_=xt[:, par, 512:1024],
               func=AF.Square, accum_out=lnc[:, 8:9], waits=[(s_a, None)])
        ctx.do('v', s_v, 'reduce_sum', out=lnc[:, 2:3], in_=xt[:, par, :], axis=AX.X,
               waits=[(sp, None), (s_v, None)])
        ctx.do('v', s_v, 'tensor_scalar_mul', out=lnc[:, 3:4], in0=lnc[:, 2:3],
               scalar1=-1.0 / 1024.0, waits=[(s_v, None)])
        ctx.do('v', s_v, 'tensor_mul', out=lnc[:, 4:5], in0=lnc[:, 3:4], in1=lnc[:, 3:4],
               waits=[(s_v, None)])
        ctx.do('v', s_v, 'tensor_add', out=lnc[:, 9:10], in0=sq, in1=lnc[:, 8:9],
               waits=[(s_a, None), (s_v, None)])
        ctx.do('v', s_v, 'scalar_tensor_tensor', out=lnc[:, 5:6], in0=lnc[:, 9:10],
               scalar=1.0 / 1024.0, in1=lnc[:, 4:5], op0=ALU.mult, op1=ALU.subtract,
               waits=[(s_v, None)])
        ctx.do('a', s_a, 'activation', out=lnc[:, 6:7], in_=lnc[:, 5:6], func=AF.Sqrt,
               bias=b['cc'][:, 0:1], waits=[(s_v, None), (s_a, None)])
        ctx.do('v', s_v, 'reciprocal', out=lnc[:, 7:8], in_=lnc[:, 6:7],
               waits=[(s_a, None), (s_v, None)])
        ctx.do('v', s_v, 'tensor_scalar', out=xntok[:, par, :], in0=xt[:, par, :],
               scalar1=lnc[:, 3:4], scalar2=lnc[:, 7:8], op0=ALU.add, op1=ALU.mult,
               waits=[(s_v, None)])
        for ci in range(PT):
            bank = ci % 2
            tw = [(s_v, None)]
            if tt + ci > 0:
                tw.append((s_a, None))
            ctx.do('t', s_pe, 'transpose', b['psb'][:, bank, 0:128],
                   xntok[:, par, ci * 128:(ci + 1) * 128], b['identb'][:, :],
                   waits=tw)
            ctx.do('a', s_a, 'copy',
                   out=xn_dst[:, ci, dst_off + tt * 128:dst_off + (tt + 1) * 128],
                   in_=b['psb'][:, bank, 0:128], waits=[(s_pe, None), (s_a, None)])


def emit_p1(ctx):
    b = ctx.b
    s_gd = ctx.sems['g_dma']
    ctx.w('g', s_gd)
    for ci in range(PT):
        ctx.do('g', s_gd, 'dma_start', out=b['hz'][:, ci, :],
               in_=b['in_idh'][:, ci, :], inc_k=16)
    if ctx.dbg == 'q':
        X = b['xscan']
        ctx.w('g', ctx.sems['a'])
        for ci in range(PT):
            ctx.do('g', s_gd, 'dma_start', out=b['out_dbg'][ci * 128:(ci + 1) * 128, 0:HT],
                   in_=X[:, ci, 0:HT], inc_k=16)
        ctx.w('g', s_gd)


def identf_full(ctx):
    return ctx.b['identf'][:, :]


def emit_scan(ctx):
    b = ctx.b
    s2v, s2a, s2p = (ctx.sems[k] for k in ('v', 'a', 'pe'))
    s_gd, s_cc, s_hx = (ctx.sems[k] for k in ('g_dma', 'cc', 'hx'))
    spl = ctx.sems['pl']
    psh, psb = b['ps'], b['psb']
    X, HZ, Wb = b['xscan'], b['hz'], b['Wbf']
    sbb, prodb, vtok = b['sbb'], b['prodb'], b['vtok']
    prodb2 = b['prodb2']
    nmx, esr, eex, attb2 = b['nmx'], b['esr'], b['eex'], b['attb2']
    ssum, rcs = b['ssum'], b['rcs']
    zb1, zb2, zpg = b['zb1'], b['zb2'], b['zpg']
    msb, sdv, rs0, rsv = b['msb'], b['sdv'], b['rs0'], b['rsv']
    zTs, tmp8, junk256 = b['zTs'], b['tmp8'], b['junk256']
    ones32b, scs32 = b['ones32b'], b['scs32']
    nrounds = b.get('dbg_rounds', 12)
    NCH = 4
    R = {}
    hcond = [None]

    def get_cond(G):
        if hcond[0] is None:
            hcond[0] = G.compute_val(G.partition_id() % 2)
        return hcond[0]

    def seqi(rc):
        return rc[0] * NCH + (NCH - 1 - rc[1])

    def chunk_at(idx):
        if idx < 0 or idx >= nrounds * NCH:
            return None
        rr, cc2 = divmod(idx, NCH)
        return (rr, NCH - 1 - cc2)

    def geom(rc):
        r, c = rc
        d = 1 << r
        c0 = c * 512
        bd = min(max(d - c0, 0), 512)
        return d, c % 2, c0, bd

    def lslice(p, rc, lo, hi):
        d, par, c0, bd = geom(rc)
        if hi <= bd:
            h0 = HT + c0 - d
            return HZ[:, p, h0 + lo:h0 + hi], True
        return X[:, p, c0 + lo - d:c0 + hi - d], False

    def cond_land(r, lo, hi):
        d = 1 << r
        if ctx.eng == 'g':
            G = ctx.E['g']
            cond = get_cond(G)
            kw = {'allow_slow_non_contiguous': True} if hi - lo < 2 else {}
            ins = G.dma_start(out=HZ[:, :, HT - d + lo:HT - d + hi],
                              in_=b['hcout'][r][0, :, :, lo:hi], cond=cond, **kw)
            ins.then_inc(s_hx.h, 16)
        s_hx.n += 16

    def exchange_head(r, full):
        d = 1 << r
        ctx.w('g', s2v)
        ctx.w('g', s2a)
        ctx.w('g', spl)
        ctx.w('g', s_hx)
        ctx.w('g', s_gd)
        kw = {'allow_slow_non_contiguous': True} if d < 2 else {}
        ctx.do('g', s_gd, 'dma_start', out=b['hcin'][r][:, :, :],
               in_=X[:, :, HT - d:HT], inc_k=16, **kw)
        R[('xtail', r)] = (s_gd, s_gd.n)
        ctx.w('g', s_gd)
        ctx.do('g', s_cc, 'collective_compute', "AllGather", ALU.bypass,
               ins=[b['hcin'][r][:, :, :]], outs=[b['hcout'][r][:, :, :, :]],
               replica_groups=RGROUPS)
        ctx.w('g', s_cc)
        split = 0 if full else d - (d >> 1)
        cond_land(r, 0, d - split)
        if full:
            R[('hz', r)] = (s_hx, s_hx.n)

    def exchange_tail(r):
        d = 1 << r
        ctx.w('g', s2v)
        ctx.w('g', spl)
        cond_land(r, d - (d >> 1), d)
        R[('hz', r)] = (s_hx, s_hx.n)

    def front_s(rc):
        d, par, c0, bd = geom(rc)
        w = [(s2v, s2v.n)]
        pk = chunk_at(seqi(rc) - 1)
        if pk and ('vtokmm', pk) in R:
            w.append(R[('vtokmm', pk)])
        if bd > 0 and ('hz', rc[0]) in R:
            w.append(R[('hz', rc[0])])
        first = True
        for p in range(PT):
            if bd > 0:
                la, _ = lslice(p, rc, 0, bd)
                ctx.do('v', s2v, 'tensor_add',
                       out=sbb[:, p, 0:bd], in0=la, in1=X[:, p, c0:c0 + bd],
                       waits=w if first else ())
                first = False
            if bd < 512:
                la, _ = lslice(p, rc, bd, 512)
                ctx.do('v', s2v, 'tensor_add',
                       out=sbb[:, p, bd:512], in0=la,
                       in1=X[:, p, c0 + bd:c0 + 512],
                       waits=w if first else ())
                first = False
        R[('s', rc)] = (s2v, s2v.n)

    def front_vtok(rc):
        d, par, c0, bd = geom(rc)
        ctx.w('t', *R[('s', rc)])
        pk = chunk_at(seqi(rc) - 1)
        if pk and ('vtokevac', pk) in R:
            ctx.w('t', *R[('vtokevac', pk)])
        pk2 = chunk_at(seqi(rc) - 2)
        aw = []
        if pk2 and ('z', pk2) in R:
            aw.append(R[('z', pk2)])
        for g in range(4):
            gb0 = 2 * (g % 2)
            if g >= 2:
                ctx.w('t', *R[('vtokevacg', rc, g - 2)])
            for jc in range(2):
                ins = None
                for kt in range(PT):
                    ins = ctx.op('t', 'matmul', psh[:, gb0 + jc, 0:512],
                                 sbb[:, kt, g * 128:(g + 1) * 128],
                                 Wb[:, kt, jc * 512:(jc + 1) * 512],
                                 start=(kt == 0), stop=(kt == PT - 1))
                if ins is not None:
                    ins.then_inc(s2p.h, 1)
                s2p.n += 1
            R[('vtokg', rc, g)] = (s2p, s2p.n)
            ctx.w('a', *R[('vtokg', rc, g)])
            for jc in range(2):
                ctx.do('a', s2a, 'copy',
                       out=vtok[:, par, g, jc * 512:(jc + 1) * 512],
                       in_=psh[:, gb0 + jc, 0:512],
                       waits=(aw + [(s2a, s2a.n)]) if (g == 0 and jc == 0)
                       else [(s2a, s2a.n)])
            R[('vtokevacg', rc, g)] = (s2a, s2a.n)
        R[('vtokmm', rc)] = (s2p, s2p.n)
        R[('vtokevac', rc)] = (s2a, s2a.n)

    def front_prod(rc, bi):
        d, par, c0, bd = geom(rc)
        gbi = seqi(rc) * 4 + bi
        pb = (prodb, prodb2)[gbi % 2]
        l = bi
        w = []
        if ('scoremm', gbi - 2) in R:
            w.append(R[('scoremm', gbi - 2)])
        wp = list(w) + [(spl, spl.n), (s2v, s2v.n)]
        if bd > 0 and ('hz', rc[0]) in R:
            wp.append(R[('hz', rc[0])])
        firstv = firstp = True
        for m in range(4):
            for h in range(2):
                tix = m * 2 + h
                qp, kp = 2 * l + h, 2 * m + h
                eng, sem = ('v', s2v) if tix < 5 else ('g', spl)
                if eng == 'v':
                    myw, firstv = (w if firstv else ()), False
                else:
                    myw, firstp = (wp if firstp else ()), False
                if bd > 0:
                    la, _ = lslice(qp, rc, 0, bd)
                    ctx.do(eng, sem, 'tensor_mul',
                           out=pb[:, tix, 0:bd], in0=la,
                           in1=X[:, kp, c0:c0 + bd],
                           waits=myw)
                    myw = ()
                if bd < 512:
                    la, _ = lslice(qp, rc, bd, 512)
                    ctx.do(eng, sem, 'tensor_mul',
                           out=pb[:, tix, bd:512], in0=la,
                           in1=X[:, kp, c0 + bd:c0 + 512],
                           waits=myw)
        R[('prodv', gbi)] = (s2v, s2v.n)
        R[('prodp', gbi)] = (spl, spl.n)

    def front_score(rc, bi):
        gbi = seqi(rc) * 4 + bi
        pb = (prodb, prodb2)[gbi % 2]
        ctx.w('t', *R[('prodv', gbi)])
        ctx.w('t', *R[('prodp', gbi)])
        if ('scsevac', gbi - 1) in R:
            ctx.w('t', *R[('scsevac', gbi - 1)])
        for m in range(4):
            base = 32 * m
            ins = None
            for h in range(2):
                tix = m * 2 + h
                kw = {'tile_position': (0, 96)} if base == 96 else {}
                ins = ctx.op('t', 'matmul', psh[base:base + 32, 4, 0:512],
                             ones32b[:, :], pb[:, tix, 0:512],
                             start=(h == 0), stop=(h == 1), **kw)
            if ins is not None:
                ins.then_inc(s2p.h, 1)
            s2p.n += 1
        R[('scoremm', gbi)] = (s2p, s2p.n)
        ctx.w('a', *R[('scoremm', gbi)])
        if ('attT', gbi - 1) in R:
            ctx.w('a', *R[('attT', gbi - 1)])
        ctx.do('a', s2a, 'copy', out=scs32[:, :], in_=psh[:, 4, 0:512],
               waits=[(s2a, s2a.n)])
        R[('scsevac', gbi)] = (s2a, s2a.n)
        ctx.w('t', *R[('scsevac', gbi)])
        if ('smaxcp', gbi - 1) in R:
            ctx.w('t', *R[('smaxcp', gbi - 1)])
        for g in range(4):
            ctx.do('t', s2p, 'transpose', psh[:, 5, g * 128:(g + 1) * 128],
                   scs32[:, g * 128:(g + 1) * 128], b['identf'][:, :])
        R[('attT', gbi)] = (s2p, s2p.n)

    def front_smax(rc, bi):
        d, par, c0, bd = geom(rc)
        gbi = seqi(rc) * 4 + bi
        l = bi
        ps5v = psh[:, 5, 0:512].rearrange("p (g m rep) -> p g m rep",
                                          m=4, rep=32)[:, :, :, 0]
        e16 = esr[:, 0:16].rearrange("p (g m) -> p g m", m=4)
        ee16 = esr[:, 16:32].rearrange("p (g m) -> p g m", m=4)
        x16 = eex[:, 0:16].rearrange("p (g m) -> p g m", m=4)
        ctx.do('v', s2v, 'tensor_copy', out=esr[:, 0:16], in_=ps5v,
               waits=[R[('attT', gbi)], (s2v, s2v.n)])
        R[('smaxcp', gbi)] = (s2v, s2v.n)
        ctx.do('v', s2v, 'reduce_max', out=nmx[:, 0:4], in_=e16, axis=AX.X,
               negate=True, waits=[(s2v, s2v.n)])
        for m in range(4):
            ctx.do('v', s2v, 'tensor_add', out=x16[:, :, m], in0=e16[:, :, m],
                   in1=nmx[:, 0:4], waits=[(s2v, s2v.n)] if m == 0 else ())
        ctx.do('a', s2a, 'activation', out=esr[:, 16:32], in_=eex[:, 0:16],
               func=AF.Exp, scale=b['cc'][:, 2:3],
               waits=[(s2v, s2v.n), (s2a, s2a.n)])
        ctx.do('v', s2v, 'reduce_sum', out=ssum[:, 0:4], in_=ee16, axis=AX.X,
               waits=[(s2a, s2a.n), (s2v, s2v.n)])
        ctx.do('v', s2v, 'reciprocal', out=rcs[:, 0:4], in_=ssum[:, 0:4],
               waits=[(s2v, s2v.n)])
        av = attb2[:, par, :].rearrange("p (g w) -> p g w", w=16)
        for m in range(4):
            ctx.do('v', s2v, 'tensor_mul', out=av[:, :, 4 * l + m],
                   in0=ee16[:, :, m], in1=rcs[:, 0:4],
                   waits=[(s2v, s2v.n)] if m == 0 else ())
        if bi == 3:
            R[('smax', rc)] = (s2v, s2v.n)

    def back_group(rc, g):
        d, par, c0, bd = geom(rc)
        att = attb2[:, par, :]
        iw = [(s2v, s2v.n)]
        if g == 0:
            iw.append(R[('vtokevac', rc)])
        for l in range(4):
            cA = 16 * g + 4 * l
            ctx.do('v', s2v, 'tensor_scalar_mul',
                   out=zb1[:, l * 256:(l + 1) * 256],
                   in0=vtok[:, par, g, 0:256], scalar1=att[:, cA:cA + 1],
                   waits=iw if l == 0 else ())
        for mi, (src_, dst_) in enumerate(((zb1, zb2), (zb2, zb1), (zb1, zb2))):
            m = mi + 1
            ctx.w('v', s2v)
            for l in range(4):
                cA = 16 * g + 4 * l + m
                ctx.do('v', s2v, 'scalar_tensor_tensor',
                       out=dst_[:, l * 256:(l + 1) * 256],
                       in0=vtok[:, par, g, m * 256:(m + 1) * 256],
                       scalar=att[:, cA:cA + 1],
                       in1=src_[:, l * 256:(l + 1) * 256],
                       op0=ALU.mult, op1=ALU.add)
        R[('chainv',)] = (s2v, s2v.n)
        R[('z', rc)] = (s2v, s2v.n)
        for l in range(4):
            ctx.do('a', s2a, 'activation', out=junk256[:, :],
                   in_=zb2[:, l * 256:(l + 1) * 256], func=AF.Square,
                   accum_out=msb[:, 4 * g + l:4 * g + l + 1],
                   waits=[R[('chainv',)], (s2a, s2a.n)])
        ctx.do('a', s2a, 'activation', out=sdv[:, 4 * g:4 * g + 4],
               in_=msb[:, 4 * g:4 * g + 4], func=AF.Sqrt,
               scale=b['cc'][:, 3:4], bias=b['cc'][:, 1:2],
               waits=[(s2a, s2a.n)])
        ctx.do('v', s2v, 'reciprocal', out=rs0[:, 4 * g:4 * g + 4],
               in_=sdv[:, 4 * g:4 * g + 4], waits=[(s2a, s2a.n), (s2v, s2v.n)])
        for l in range(4):
            ctx.do('v', s2v, 'tensor_scalar_mul',
                   out=rsv[:, 4 * g + l:4 * g + l + 1],
                   in0=rs0[:, 4 * g + l:4 * g + l + 1],
                   scalar1=1.0 / (1.0 + l),
                   waits=[(s2v, s2v.n)] if l == 0 else ())
        ctx.w('v', s2v)
        for l in range(4):
            ctx.do('v', s2v, 'tensor_scalar_mul',
                   out=zpg[:, l * 256:(l + 1) * 256],
                   in0=zb2[:, l * 256:(l + 1) * 256],
                   scalar1=rsv[:, 4 * g + l:4 * g + l + 1])
        R[('zpg', rc, g)] = (s2v, s2v.n)
        ctx.w('t', *R[('zpg', rc, g)])
        if ('zTevacg', rc, g - 1) in R:
            ctx.w('t', *R[('zTevacg', rc, g - 1)])
        for p in range(PT):
            bank = 6 + (p // 4)
            ctx.do('t', s2p, 'transpose',
                   psb[:, bank, (p % 4) * 128:(p % 4 + 1) * 128],
                   zpg[:, p * 128:(p + 1) * 128], b['identb'][:, :])
        R[('zT', rc, g)] = (s2p, s2p.n)

    def back_group_evac(rc, g):
        ctx.w('a', *R[('zT', rc, g)])
        if g == 0 and ('finallast',) in R:
            ctx.w('a', *R[('finallast',)])
        for p in range(PT):
            bank = 6 + (p // 4)
            ctx.do('a', s2a, 'copy', out=zTs[:, p, g * 128:(g + 1) * 128],
                   in_=psb[:, bank, (p % 4) * 128:(p % 4 + 1) * 128],
                   waits=[(s2a, s2a.n)])
        R[('zTevacg', rc, g)] = (s2a, s2a.n)
        if g == 3:
            R[('zTevac', rc)] = (s2a, s2a.n)

    def back_final(rc):
        d, par, c0, bd = geom(rc)
        ctx.w('v', *R[('zTevac', rc)])
        ctx.w('v', s2v)
        ctx.w('v', *R[('prodp', seqi(rc) * 4 + 3)])
        ctx.w('v', s_gd)
        direct = (bd == 0 and d >= 512)
        if direct:
            for p in range(PT):
                la, _ = lslice(p, rc, 0, 512)
                ctx.do('v', s2v, 'tensor_add', out=X[:, p, c0:c0 + 512],
                       in0=zTs[:, p, 0:512], in1=la)
        else:
            for half in (1, 0):
                lo, hi = half * 256, half * 256 + 256
                lo2 = max(lo, bd)
                for p in range(PT):
                    if bd > lo:
                        e = min(bd, hi)
                        la, _ = lslice(p, rc, lo, e)
                        ctx.do('v', s2v, 'tensor_add',
                               out=tmp8[:, p, 0:e - lo], in0=zTs[:, p, lo:e],
                               in1=la)
                    if lo2 < hi:
                        la, _ = lslice(p, rc, lo2, hi)
                        ctx.do('v', s2v, 'tensor_add',
                               out=tmp8[:, p, lo2 - lo:hi - lo],
                               in0=zTs[:, p, lo2:hi], in1=la)
                ctx.w('v', s2v)
                for p in range(PT):
                    ctx.do('v', s2v, 'tensor_copy', out=X[:, p, c0 + lo:c0 + hi],
                           in_=tmp8[:, p, 0:256])
                ctx.w('v', s2v)
        R[('final', rc)] = (s2v, s2v.n)
        R[('finallast',)] = (s2v, s2v.n)

    def back_all(rc):
        for g in range(4):
            back_group(rc, g)
            back_group_evac(rc, g)
        back_final(rc)

    order = [chunk_at(i) for i in range(nrounds * NCH)]
    s_done = set()
    for i, rc in enumerate(order):
        new_round = (rc[1] == NCH - 1)
        r = rc[0]
        need_late = new_round and i >= 1 and (1 << r) > 512 * (NCH - 1)
        pv = order[i - 1] if i >= 1 else None
        if need_late:
            back_all(pv)
            pv = None
        if new_round:
            exchange_head(r, full=(r == 0 or need_late))
        if rc not in s_done:
            front_s(rc)
        front_vtok(rc)
        front_prod(rc, 0)
        front_prod(rc, 1)
        for bi in range(4):
            front_score(rc, bi)
            if pv is not None:
                back_group(pv, bi)
            if bi < 2:
                front_prod(rc, bi + 2)
            if bi == 2 and i + 1 < len(order):
                nxt = order[i + 1]
                if (nxt[0] == r or (nxt[0] == r + 1 and nxt[0] <= 9)) \
                        and nxt != (10, 2):
                    front_s(nxt)
                    s_done.add(nxt)
            front_smax(rc, bi)
            if pv is not None:
                back_group_evac(pv, bi)
        if pv is not None:
            back_final(pv)
        if new_round and not (r == 0 or need_late):
            exchange_tail(r)
    back_all(order[-1])

    if ctx.dbg == 'scan':
        sd = ctx.sems['g_dma']
        ctx.w('g', s2v)
        ctx.w('g', sd)
        for ci in range(PT):
            ctx.do('g', sd, 'dma_start', out=b['out_dbg'][ci * 128:(ci + 1) * 128, 0:HT],
                   in_=X[:, ci, 0:HT], inc_k=16)
        ctx.w('g', sd)


def emit_p3(ctx):
    b = ctx.b
    s_dma, s_gd, s_v, s_a, s_pe = (ctx.sems[k] for k in
                                   ('sy_dma', 'g_dma', 'v', 'a', 'pe'))
    X = b['xscan']

    emit_ln(ctx, b['in_xh'], 0, 16, b['xnh'], 0, b['xt3'], b['xnt3'])
    for ci in range(PT):
        ctx.do('g', s_gd, 'dma_start', out=b['wvbf'][:, ci, :],
               in_=b['in_wvT'][ci * 128:(ci + 1) * 128, :], inc_k=16)
    ctx.w('t', s_gd)
    ctx.w('t', s_a)
    ctx.w('t', s_v)
    pe_b = s_pe.n
    a_b = s_a.n
    gi = 0
    for jt in range(PT):
        for tc in range(4):
            bank = 2 + (gi % 4)
            if gi >= 4:
                ctx.w('t', s_a, a_b + gi - 4 + 1)
            ins = None
            for kt in range(PT):
                ins = ctx.op('t', 'matmul', b['ps'][:, bank, 0:512],
                             b['wvbf'][:, kt, jt * 128:(jt + 1) * 128],
                             b['xnh'][:, kt, tc * 512:(tc + 1) * 512],
                             start=(kt == 0), stop=(kt == PT - 1))
            if ins is not None:
                ins.then_inc(s_pe.h, 1)
            s_pe.n += 1
            ctx.do('a', s_a, 'copy', out=b['vbf'][:, jt, tc * 512:(tc + 1) * 512],
                   in_=b['ps'][:, bank, 0:512],
                   waits=[(s_pe, pe_b + gi + 1), (s_a, None)])
            gi += 1
    ctx.w('v', s_a)
    ctx.w('v', s_v)
    for ci in range(PT):
        ctx.do('v', s_v, 'tensor_mul', out=b['ubf'][:, ci, :], in0=X[:, ci, :],
               in1=b['vbf'][:, ci, :])
    ctx.w('g', s_v)
    for ci in range(PT):
        ctx.do('g', s_gd, 'dma_start', out=b['wprojbf'][:, ci, :],
               in_=b['in_wprojT'][ci * 128:(ci + 1) * 128, :], inc_k=16)
    ctx.w('t', s_gd)
    ctx.w('t', s_v)
    pe_b, v_b = s_pe.n, s_v.n
    for tg in range(16):
        par = tg % 2
        xgp = ctx.sems['xg0'] if par == 0 else ctx.sems['xg1']
        ctx.do('sy', xgp, 'dma_start', out=b['xtg'][:, par, :],
               in_=b['in_xh'][tg * 128:(tg + 1) * 128, :],
               waits=[(s_v, v_b + (tg - 2) * 2 + 2)] if tg >= 2 else (), inc_k=16)
        if tg >= 2:
            ctx.w('t', s_v, v_b + (tg - 2) * 2 + 2)
        for jc in range(2):
            ins = None
            for kt in range(PT):
                ins = ctx.op('t', 'matmul', b['ps'][:, 2 * par + jc, 0:512],
                             b['ubf'][:, kt, tg * 128:(tg + 1) * 128],
                             b['wprojbf'][:, kt, jc * 512:(jc + 1) * 512],
                             start=(kt == 0), stop=(kt == PT - 1))
            if ins is not None:
                ins.then_inc(s_pe.h, 1)
            s_pe.n += 1
        ctx.w('v', s_pe, pe_b + 2 * (tg + 1))
        ctx.w('v', xgp)
        ctx.w('v', s_gd)
        for jc in range(2):
            ctx.do('v', s_v, 'tensor_add', out=b['oat'][:, jc * 512:(jc + 1) * 512],
                   in0=b['ps'][:, 2 * par + jc, 0:512],
                   in1=b['xtg'][:, par, jc * 512:(jc + 1) * 512])
        ctx.do('g', s_gd, 'dma_start', out=b['oasp'][tg * 128:(tg + 1) * 128, :],
               in_=b['oat'][:, :], waits=[(s_v, None)], inc_k=16)
    ctx.w('g', s_a)
    ctx.do('g', s_gd, 'dma_start', out=b['xnsp'][:, :, :], in_=b['xnh'][:, :, :],
           inc_k=16)
    ctx.w('g', s_gd)


def emit_p4(ctx):
    b = ctx.b
    s_dma, s_gd, s_v, s_a, s_pe = (ctx.sems[k] for k in
                                   ('sy_dma', 'g_dma', 'v', 'a', 'pe'))
    s_out = ctx.sems['outd']
    psh = b['ps']
    abuf, bbuf, w3v = b['abuf'], b['bbuf'], b['w3bf']
    xns = b['xnhsub']
    ctx.w('sy', s_dma)
    ctx.do('sy', s_dma, 'dma_start', out=b['s1c'][:, :], in_=b['in_s1c'][:, :], inc_k=16)
    ctx.do('sy', s_dma, 'dma_start', out=b['s2c'][:, :], in_=b['in_s2c'][:, :], inc_k=16)
    ctx.do('sy', s_dma, 'dma_start', out=b['sh1c'][:, :], in_=b['in_sh1c'][:, :], inc_k=16)
    ctx.do('sy', s_dma, 'dma_start', out=b['sh2c'][:, :], in_=b['in_sh2c'][:, :], inc_k=16)
    for e in ('v', 'a', 't'):
        ctx.w(e, s_dma)
    R = {}

    for sh in range(2):
        t0 = sh * 1024
        ctx.w('sy', s_dma)
        if ('w12mm', sh - 1) in R:
            ctx.w('sy', *R[('w12mm', sh - 1)])
        for ci in range(PT):
            ctx.do('sy', s_dma, 'dma_start', out=xns[:, ci, :],
                   in_=b['xnsp'][:, ci, t0:t0 + 1024], inc_k=16)
        for js in range(8):
            par = js % 2
            stp = ctx.sems['st0'] if par == 0 else ctx.sems['st1']
            w = [R[('w12s', sh, js - 2)]] if ('w12s', sh, js - 2) in R else \
                ([R[('hmm_last',)]] if ('hmm_last',) in R else [])
            ctx.do('g', stp, 'dma_start', out=b['w1st'][:, par],
                   in_=b['in_w1s'][js].rearrange("(a b) j -> b a j", b=128),
                   waits=w, inc_k=16)
            ctx.do('g', stp, 'dma_start', out=b['w2st'][:, par],
                   in_=b['in_w2s'][js].rearrange("(a b) j -> b a j", b=128),
                   inc_k=16)
            ctx.w('t', stp)
            if js == 0:
                ctx.w('t', s_dma)
                ctx.w('t', s_a)
                ctx.w('t', s_v)
            gi0 = js * 8
            for jt in range(4):
                for tc in range(2):
                    gi = gi0 + jt * 2 + tc
                    b1, b2 = 2 * (gi % 2), 2 * (gi % 2) + 1
                    if ('aevac', sh, gi - 2) in R:
                        ctx.w('t', *R[('aevac', sh, gi - 2)])
                        ctx.w('t', *R[('vstt', sh, gi - 2)])
                    for (wst, bank) in ((b['w1st'], b1), (b['w2st'], b2)):
                        ins = None
                        for kt in range(PT):
                            ins = ctx.op('t', 'matmul', psh[:, bank, 0:512],
                                         wst[:, par, kt, jt * 128:(jt + 1) * 128],
                                         xns[:, kt, tc * 512:(tc + 1) * 512],
                                         start=(kt == 0), stop=(kt == PT - 1))
                        if ins is not None:
                            ins.then_inc(s_pe.h, 1)
                        s_pe.n += 1
                    R[('w12g', sh, gi)] = (s_pe, s_pe.n)
                    jg = js * 4 + jt
                    ctx.w('a', *R[('w12g', sh, gi)])
                    ta = b['tmpa'][:, (gi % 2) * 512:(gi % 2) * 512 + 512]
                    wl = [(s_a, s_a.n)]
                    if ('vstt', sh, gi - 2) in R:
                        wl.append(R[('vstt', sh, gi - 2)])
                    if gi == 0:
                        wl.append((s_v, s_v.n))
                    ctx.do('a', s_a, 'activation', out=ta,
                           in_=psh[:, b1, 0:512], func=AF.Silu,
                           scale=b['s1c'][:, jg:jg + 1], waits=wl)
                    R[('aevac', sh, gi)] = (s_a, s_a.n)
                    ctx.do('v', s_v, 'scalar_tensor_tensor',
                           out=abuf[:, jg, tc * 512:(tc + 1) * 512],
                           in0=psh[:, b2, 0:512], scalar=b['s2c'][:, jg:jg + 1],
                           in1=ta, op0=ALU.mult, op1=ALU.mult,
                           waits=[R[('aevac', sh, gi)], (s_v, s_v.n)])
                    R[('vstt', sh, gi)] = (s_v, s_v.n)
            R[('w12s', sh, js)] = (s_pe, s_pe.n)
        R[('w12mm', sh)] = (s_pe, s_pe.n)
        emit_h_layer(ctx, b['in_h1s'], b['h1st'], abuf, bbuf, b['sh1c'], R,
                     lastv=R[('vstt', sh, 63)])
        emit_h_layer(ctx, b['in_h2s'], b['h1st'], bbuf, abuf, b['sh2c'], R,
                     lastv=None)
        for kt2 in range(4):
            ctx.do('g', s_gd, 'dma_start', out=w3v[:, kt2 * 8:(kt2 + 1) * 8, :],
                   in_=b['in_w3T'].rearrange("(a b) j -> b a j", b=128)
                   [:, kt2 * 8:(kt2 + 1) * 8, :],
                   waits=[(s_gd, s_gd.n), (s_pe, s_pe.n), (s_a, s_a.n)]
                   if kt2 == 0 else (), inc_k=16)
        ctx.w('sy', s_dma)
        ctx.do('sy', s_dma, 'dma_start', out=b['s3r'][0:1, :], in_=b['in_s3r'][:, :],
               waits=[(s_pe, s_pe.n)] if sh == 0 else (), inc_k=16)
        ctx.do('t', s_pe, 'matmul', psh[:, 6, 0:512], b['onesr'][0:1, :],
               b['s3r'][0:1, 0:512], start=True, stop=True,
               waits=[(s_dma, s_dma.n), (s_a, s_a.n), (s_v, s_v.n)])
        ctx.do('t', s_pe, 'matmul', psh[:, 7, 0:512], b['onesr'][0:1, :],
               b['s3r'][0:1, 512:1024], start=True, stop=True)
        ctx.do('a', s_a, 'copy', out=b['s3bc'][:, 0:512], in_=psh[:, 6, 0:512],
               waits=[(s_pe, s_pe.n), (s_a, s_a.n)])
        ctx.do('a', s_a, 'copy', out=b['s3bc'][:, 512:1024], in_=psh[:, 7, 0:512],
               waits=[(s_a, s_a.n)])
        ctx.w('t', s_gd)
        ctx.w('t', s_a)
        pe_b3, v_b3 = s_pe.n, s_v.n
        for tg in range(8):
            par = tg % 2
            xgp = ctx.sems['xg0'] if par == 0 else ctx.sems['xg1']
            ctx.do('sy', xgp, 'dma_start', out=b['oatile'][:, par, :],
                   in_=b['oasp'][t0 + tg * 128:t0 + (tg + 1) * 128, :],
                   waits=([(s_v, v_b3 + (tg - 2) * 3 + 3)] if tg >= 2 else [])
                   + [(s_out, s_out.n)],
                   inc_k=16)
            if tg >= 2:
                ctx.w('t', s_v, v_b3 + (tg - 2) * 3 + 3)
            for jc in range(2):
                ins = None
                for kt in range(32):
                    ins = ctx.op('t', 'matmul', psh[:, 2 * par + jc, 0:512],
                                 abuf[:, kt, tg * 128:(tg + 1) * 128],
                                 w3v[:, kt, jc * 512:(jc + 1) * 512],
                                 start=(kt == 0), stop=(kt == 31))
                if ins is not None:
                    ins.then_inc(s_pe.h, 1)
                s_pe.n += 1
            ctx.w('v', s_pe, pe_b3 + 2 * (tg + 1))
            ctx.w('v', xgp)
            ctx.w('v', s_out)
            ctx.w('v', s_v)
            for jc in range(2):
                ctx.do('v', s_v, 'tensor_mul', out=b['tmpa'][:, jc * 512:(jc + 1) * 512],
                       in0=psh[:, 2 * par + jc, 0:512],
                       in1=b['s3bc'][:, jc * 512:(jc + 1) * 512])
            ctx.w('v', s_v)
            ctx.do('v', s_v, 'tensor_add', out=b['oatile'][:, par, :],
                   in0=b['tmpa'][:, :], in1=b['oatile'][:, par, :])
            ctx.do('g', s_out, 'dma_start',
                   out=b['out_h'][t0 + tg * 128:t0 + (tg + 1) * 128, :],
                   in_=b['oatile'][:, par, :], waits=[(s_v, s_v.n)], inc_k=16)
    ctx.w('g', s_out)
    ctx.w('sy', s_out)


def emit_h_layer(ctx, in_strips, stbuf, src, dst, scol, R, lastv=None):
    s_gd, s_v, s_a, s_pe = (ctx.sems[k] for k in ('g_dma', 'v', 'a', 'pe'))
    psh = ctx.b['ps']
    key = ctx.nc.next_id() if False else id(in_strips)
    for js in range(16):
        par = js % 2
        stp = ctx.sems['st0'] if par == 0 else ctx.sems['st1']
        w = [R[('hs', key, js - 2)]] if ('hs', key, js - 2) in R else [(s_pe, s_pe.n)]
        ctx.do('g', stp, 'dma_start', out=stbuf[:, par],
               in_=in_strips[js].rearrange("(a b) j -> b a j", b=128),
               waits=w, inc_k=16)
        ctx.w('t', stp)
        if js == 0:
            ctx.w('t', s_a)
            if lastv is not None:
                ctx.w('t', *lastv)
            else:
                ctx.w('t', s_v)
        for jt in range(2):
            for tc in range(2):
                gi = js * 4 + jt * 2 + tc
                bank = gi % 6
                if ('hg', key, gi - 6) in R:
                    ctx.w('t', *R[('hg', key, gi - 6)])
                ins = None
                for kt in range(32):
                    ins = ctx.op('t', 'matmul', psh[:, bank, 0:512],
                                 stbuf[:, par, kt, jt * 128:(jt + 1) * 128],
                                 src[:, kt, tc * 512:(tc + 1) * 512],
                                 start=(kt == 0), stop=(kt == 31))
                if ins is not None:
                    ins.then_inc(s_pe.h, 1)
                s_pe.n += 1
                jg = js * 2 + jt
                ctx.w('a', s_pe, s_pe.n)
                ctx.do('a', s_a, 'activation', out=dst[:, jg, tc * 512:(tc + 1) * 512],
                       in_=psh[:, bank, 0:512], func=AF.Gelu,
                       scale=scol[:, jg:jg + 1], waits=[(s_a, s_a.n)])
                R[('hg', key, gi)] = (s_a, s_a.n)
        R[('hs', key, js)] = (s_pe, s_pe.n)
    R[('hmm_last',)] = (s_pe, s_pe.n)


def build(dbg=None, dbg_rounds=12):
    nc = bass.Bass("TRN2", target_bir_lowering=False)
    din = {}
    D = nc.dram_tensor
    din['in_gsr'] = D("gsr", [C, C], F32, kind="ExternalInput").ap()
    din['in_xh'] = D("xh", [HT, C], F32, kind="ExternalInput").ap()
    din['in_idh'] = D("idh", [128, PT, HT], F32, kind="ExternalInput").ap()
    din['in_wupT'] = D("wupT", [C, C], F32, kind="ExternalInput").ap()
    din['in_wvT'] = D("wvT", [C, C], F32, kind="ExternalInput").ap()
    din['in_wprojT'] = D("wprojT", [C, C], F32, kind="ExternalInput").ap()
    din['in_identc'] = D("identc", [128, PT], F32, kind="ExternalInput").ap()
    din['in_eye'] = D("eye128", [128, 128], F32, kind="ExternalInput").ap()
    din['in_eyeb'] = D("eye128b", [128, 128], BF16, kind="ExternalInput").ap()
    din['in_onesc'] = D("onescol", [128, 1], F32, kind="ExternalInput").ap()
    din['in_onesb'] = D("onescolb", [128, 1], BF16, kind="ExternalInput").ap()
    din['in_onesr'] = D("onesrow", [1, 128], F32, kind="ExternalInput").ap()
    din['in_hoff'] = D("hoff", [1, 1], I32, kind="ExternalInput").ap()
    din['in_w1s'] = [D(f"w1s{j}", [C, 512], F32, kind="ExternalInput").ap() for j in range(8)]
    din['in_w2s'] = [D(f"w2s{j}", [C, 512], F32, kind="ExternalInput").ap() for j in range(8)]
    din['in_h1s'] = [D(f"h1s{j}", [DH, 256], F32, kind="ExternalInput").ap() for j in range(16)]
    din['in_h2s'] = [D(f"h2s{j}", [DH, 256], F32, kind="ExternalInput").ap() for j in range(16)]
    din['in_w3T'] = D("w3T", [DH, C], F32, kind="ExternalInput").ap()
    din['in_s1c'] = D("s1c_i", [128, 32], F32, kind="ExternalInput").ap()
    din['in_s2c'] = D("s2c_i", [128, 32], F32, kind="ExternalInput").ap()
    din['in_sh1c'] = D("sh1c_i", [128, 32], F32, kind="ExternalInput").ap()
    din['in_sh2c'] = D("sh2c_i", [128, 32], F32, kind="ExternalInput").ap()
    din['in_s3r'] = D("s3r_i", [1, C], F32, kind="ExternalInput").ap()
    out_h = D("out_h", [HT, C], F32, kind="ExternalOutput").ap()
    out_dbg = None
    if dbg in ('ns', 'q', 'scan'):
        out_dbg = D("dbgout", [C, 4096], F32, kind="ExternalOutput").ap()
    oasp = D("oasp", [HT, C], F32, kind="Internal").ap()
    hcin = [D(f"hcin{r}", [128, PT, 1 << r], BF16, kind="Internal").ap()
            for r in range(12)]
    hcout = [D(f"hcout{r}", [2, 128, PT, 1 << r], BF16, kind="Internal").ap()
             for r in range(12)]
    xnsp_h = D("xnsp", [128, PT, HT], BF16, kind="Internal")
    xnsp = xnsp_h.ap()

    st = {}

    def salloc(name, shape, dtype):
        st[name] = nc.alloc_sbuf_tensor(name, shape, dtype).ap()
        return st[name]

    identf = salloc("identf", [128, 128], F32)
    identb = salloc("identb", [128, 128], BF16)
    onesc = salloc("onesc", [128, 1], F32)
    onesb = salloc("onesb", [128, 1], BF16)
    onesr = salloc("onesr", [1, 128], F32)
    identcol = salloc("identcol", [128, PT], F32)
    lncols = salloc("lncols", [128, 12], F32)
    fcol = salloc("fcol", [128, 16], F32)
    sc_col = salloc("sc_col", [128, 2], F32)
    hoffs = salloc("hoffs", [1, 1], I32)
    s1c = salloc("s1c", [128, 32], F32)
    s2c = salloc("s2c", [128, 32], F32)
    sh1c = salloc("sh1c", [128, 32], F32)
    sh2c = salloc("sh2c", [128, 32], F32)
    nmx = salloc("nmx", [128, 16], F32)
    esr = salloc("esr", [128, 64], F32)
    eex = salloc("eex", [128, 64], F32)

    ssum = salloc("ssum", [128, 16], F32)
    rcs = salloc("rcs", [128, 16], F32)
    msb = salloc("msb", [128, 16], F32)
    sdv = salloc("sdv", [128, 16], F32)
    rs0 = salloc("rs0", [128, 16], F32)
    rsv = salloc("rsv", [128, 16], F32)
    cc = salloc("cc", [128, 4], F32)
    ones32b = salloc("ones32b", [128, 32], BF16)
    scs32 = salloc("scs32", [128, 512], F32)
    attb2 = salloc("attb2", [128, 2, 64], F32)
    junk256 = salloc("junk256", [128, 256], F32)

    arena_h = nc.alloc_sbuf_tensor("arena", [128, (nc.sbuf_bytes_remaining // 64) * 64 - 128], U8)
    ar = arena_h.ap()
    ARN = ar.shape[1]
    assert ARN >= 180 * KB, ARN

    def V2(off, dtype, d1, d2):
        esz = {F32: 4, F32R: 4, BF16: 2, I32: 4}[dtype]
        n = d1 * d2 * esz
        assert off % 64 == 0 and off + n <= ARN, (off, n, ARN)
        return ar[:, off:off + n].bitcast(dtype).rearrange("p (a b) -> p a b", b=d2)

    def V1(off, dtype, d1):
        esz = {F32: 4, F32R: 4, BF16: 2}[dtype]
        n = d1 * esz
        assert off % 64 == 0 and off + n <= ARN, (off, n, ARN)
        return ar[:, off:off + n].bitcast(dtype)

    KBy = 1024
    nsA = V2(0, F32R, PT, C)
    nsB = V2(32 * KBy, F32R, PT, C)
    nsC = V2(64 * KBy, F32R, PT, C)
    nsD = V2(96 * KBy, F32R, PT, C)
    nsE = V2(128 * KBy, F32R, PT, C)
    nsE32 = V2(128 * KBy, F32, PT, C)
    Wbf = V2(128 * KBy, BF16, PT, C)
    xscan = V2(0, F32, PT, HT)
    hz = V2(64 * KBy, BF16, PT, HT)
    xnfq = V2(96 * KBy, BF16, PT, 512)
    xt1 = V2(112 * KBy, F32, 2, 1024)
    xnt1 = V2(120 * KBy, BF16, 2, 1024)
    wupst = V2(124 * KBy, BF16, 2 * PT, 128).rearrange("p (a b) c -> p a b c", b=PT)
    xnful = V2(160 * KBy, BF16, PT, HT)
    xtl = V2(192 * KBy, F32, 1, 1024)
    xntl = V2(196 * KBy, BF16, 2, 1024)
    wupq = V2(96 * KBy, BF16, PT, C)
    sbb = V2(96 * KBy, BF16, PT, 512)
    vtok = V2(104 * KBy, BF16, 2 * 4, 1024).rearrange("p (a b) c -> p a b c", b=4)
    zTs = V2(120 * KBy, BF16, PT, 512)
    prodb = V2(144 * KBy, BF16, 8, 512)
    prodb2 = V2(166 * KBy, BF16, 8, 512)
    tmp8 = V2(152 * KBy, F32, PT, 256)
    zb1 = V1(160 * KBy, BF16, 1024)
    zb2 = V1(162 * KBy, BF16, 1024)
    zpg = V1(164 * KBy, BF16, 1024)
    vbf = V2(64 * KBy, BF16, PT, HT)
    ubf = V2(96 * KBy, BF16, PT, HT)
    xnh = V2(128 * KBy, BF16, PT, HT)
    wvbf = V2(160 * KBy, BF16, PT, C)
    wprojbf = V2(160 * KBy, BF16, PT, C)
    xt3 = V2(176 * KBy, F32, 2, 1024)
    xnt3 = V2(184 * KBy, BF16, 2, 1024)
    oat = V1(188 * KBy, F32, 1024)
    xtg = V2(192 * KBy, F32, 2, 1024)
    Wbf = V2(128 * KBy, BF16, PT, C)
    abuf = V2(0, BF16, 32, 1024)
    bbuf = V2(64 * KBy, BF16, 32, 1024)
    w3bf = V2(64 * KBy, BF16, 32, 1024)
    w1st = V2(128 * KBy, BF16, 2 * PT, 512).rearrange("p (a b) c -> p a b c", b=PT)
    w2st = V2(144 * KBy, BF16, 2 * PT, 512).rearrange("p (a b) c -> p a b c", b=PT)
    h1st = V2(128 * KBy, BF16, 2 * 32, 256).rearrange("p (a b) c -> p a b c", b=32)
    xnhsub = V2(160 * KBy, BF16, PT, 1024)
    s3bc = V1(176 * KBy, F32, 1024)
    tmpa = V1(180 * KBy, F32, 1024)
    s3r = V2(184 * KBy, F32, 1, 1024)[0:1, 0, :].rearrange("p a -> p a")
    oatile = V2(188 * KBy, F32, 2, 1024)

    with (
        nc.psum_tensor("ps", [128, PT, 512], F32) as ps,
        nc.semaphore("sy_dma") as h_sd,
        nc.semaphore("g_dma") as h_gd,
        nc.semaphore("pe") as h_pe,
        nc.semaphore("v") as h_v,
        nc.semaphore("a") as h_a,
        nc.semaphore("outd") as h_od,
        nc.semaphore("ln0") as h_l0,
        nc.semaphore("ln1") as h_l1,
        nc.semaphore("st0") as h_s0,
        nc.semaphore("st1") as h_s1,
        nc.semaphore("xg0") as h_x0,
        nc.semaphore("xg1") as h_x1,
        nc.semaphore("cc") as h_cc,
        nc.semaphore("hx") as h_hx,
        nc.semaphore("pl") as h_pl,
        nc.Block() as block,
    ):
        sems = {'sy_dma': Sem(h_sd), 'g_dma': Sem(h_gd), 'pe': Sem(h_pe),
                'v': Sem(h_v), 'a': Sem(h_a), 'outd': Sem(h_od),
                'ln0': Sem(h_l0), 'ln1': Sem(h_l1), 'st0': Sem(h_s0),
                'st1': Sem(h_s1), 'xg0': Sem(h_x0), 'xg1': Sem(h_x1),
                'cc': Sem(h_cc), 'hx': Sem(h_hx), 'pl': Sem(h_pl)}
        bufs = dict(
            nsA=nsA, nsB=nsB, nsC=nsC, nsD=nsD, nsE=nsE, nsE32=nsE32, Wbf=Wbf,
            xscan=xscan, hz=hz, xnfq=xnfq, xt1=xt1, xnt1=xnt1, wupst=wupst,
            xnful=xnful, xtl=xtl, xntl=xntl, wupq=wupq,
            sbb=sbb, vtok=vtok, tmp8=tmp8, zTs=zTs, prodb=prodb, prodb2=prodb2,
            zb1=zb1, zb2=zb2, zpg=zpg,
            vbf=vbf, ubf=ubf, xnh=xnh, wvbf=wvbf, wprojbf=wprojbf,
            xt3=xt3, xnt3=xnt3, oat=oat, xtg=xtg,
            abuf=abuf, bbuf=bbuf, w3bf=w3bf, w1st=w1st, w2st=w2st, h1st=h1st,
            xnhsub=xnhsub, s3bc=s3bc, tmpa=tmpa, s3r=s3r, oatile=oatile,
            ps=ps.ap(), psb=ps.ap().bitcast(BF16), oasp=oasp, xnsp=xnsp,
            hcin=hcin, hcout=hcout, out_h=out_h,
            out_dbg=out_dbg, dbg_rounds=dbg_rounds,
            **st, **din,
        )

        def emit_all(ctx):
            s_dma = ctx.sems['sy_dma']
            for (dst, src) in ((identf[:, :], din['in_eye'][:, :]),
                               (identb[:, :], din['in_eyeb'][:, :]),
                               (onesc[:, :], din['in_onesc'][:, :]),
                               (onesb[:, :], din['in_onesb'][:, :]),
                               (onesr[:, :], din['in_onesr'][:, :]),
                               (identcol[:, :], din['in_identc'][:, :]),
                               (hoffs[:, :], din['in_hoff'][:, :])):
                ctx.do('sy', s_dma, 'dma_start', out=dst, in_=src, inc_k=16)
            s_v0 = ctx.sems['v']
            for (col, val) in ((0, 1e-5), (1, 1e-8), (2, 1.0 / 16.0), (3, 1.0 / 256.0)):
                ctx.do('v', s_v0, 'memset', st['cc'][:, col:col + 1], val)
            ctx.do('v', s_v0, 'memset', st['ones32b'][:, :], 1.0)
            for e in ('v', 'a', 't', 'g'):
                ctx.w(e, s_dma)
            for e in ('a', 't', 'g'):
                ctx.w(e, s_v0)
            emit_ns(ctx)
            if ctx.dbg == 'ns':
                return
            ctx.sync_all()
            emit_p1(ctx)
            if ctx.dbg == 'q':
                return
            ctx.sync_all()
            emit_scan(ctx)
            if ctx.dbg == 'scan':
                return
            ctx.sync_all()
            emit_p3(ctx)
            ctx.sync_all()
            emit_p4(ctx)

        engines = {'v': nc.vector, 'a': nc.scalar, 't': nc.tensor,
                   'sy': nc.sync, 'g': nc.gpsimd}

        def begin():
            for s in sems.values():
                s.n = 0

        @block.sync
        def _(sync):
            begin(); emit_all(Ctx(nc, 'sy', engines, sems, bufs, dbg))

        @block.vector
        def _(vector):
            begin(); emit_all(Ctx(nc, 'v', engines, sems, bufs, dbg))

        @block.scalar
        def _(scalar):
            begin(); emit_all(Ctx(nc, 'a', engines, sems, bufs, dbg))

        @block.tensor
        def _(tensor):
            begin(); emit_all(Ctx(nc, 't', engines, sems, bufs, dbg))

        @block.gpsimd
        def _(gpsimd):
            begin(); emit_all(Ctx(nc, 'g', engines, sems, bufs, dbg))

    return nc


def host_inputs(z, core):
    import ml_dtypes
    seq, hh = core // 2, core % 2
    A = np.ascontiguousarray
    x = np.asarray(z['L0_attn_x'])
    identc = np.asarray(z['L0_attn_identity']).reshape(PT, 128).T
    d = {
        'gsr': A(np.asarray(z['L0_attn_W_scan_raw'])),
        'idh': A(np.broadcast_to(identc[:, :, None], (128, PT, HT)).astype(np.float32)),
        'xh': A(x[seq, hh * HT:(hh + 1) * HT]),
        'wupT': A(np.asarray(z['L0_attn_W_up']).T),
        'wvT': A(np.asarray(z['L0_attn_W_v']).T),
        'wprojT': A(np.asarray(z['L0_attn_W_proj']).T),
        'identc': A(np.asarray(z['L0_attn_identity']).reshape(PT, 128).T),
        'eye128': np.eye(128, dtype=np.float32),
        'eye128b': np.eye(128, dtype=np.float32).astype(ml_dtypes.bfloat16),
        'onescol': np.ones((128, 1), np.float32),
        'onescolb': np.ones((128, 1), np.float32).astype(ml_dtypes.bfloat16),
        'onesrow': np.ones((1, 128), np.float32),
        'hoff': np.array([[hh * HT]], np.int32),
        'w3T': A(np.asarray(z['L0_ffn_w3_w']).T),
        's1c_i': A(np.asarray(z['L0_ffn_w1_s']).reshape(32, 128).T),
        's2c_i': A(np.asarray(z['L0_ffn_w2_s']).reshape(32, 128).T),
        'sh1c_i': A(np.asarray(z['L0_ffn_h1_s']).reshape(32, 128).T),
        'sh2c_i': A(np.asarray(z['L0_ffn_h2_s']).reshape(32, 128).T),
        's3r_i': A(np.asarray(z['L0_ffn_w3_s']).reshape(1, C)),
    }
    w1T = np.asarray(z['L0_ffn_w1_w']).T
    w2T = np.asarray(z['L0_ffn_w2_w']).T
    h1T = np.asarray(z['L0_ffn_h1_w']).T
    h2T = np.asarray(z['L0_ffn_h2_w']).T
    for j in range(8):
        d[f'w1s{j}'] = A(w1T[:, j * 512:(j + 1) * 512])
        d[f'w2s{j}'] = A(w2T[:, j * 512:(j + 1) * 512])
    for j in range(16):
        d[f'h1s{j}'] = A(h1T[:, j * 256:(j + 1) * 256])
        d[f'h2s{j}'] = A(h2T[:, j * 256:(j + 1) * 256])
    return d


def _patch_walrus_verifier():
    import concourse.bass_utils as _bu

    def bir_verify_and_optimise(tmpdir, inp="bir.json", outp="file.neff",
                                arch=None, *, dve_root=None):
        from pathlib import Path as _P
        cmd = [
            _bu.get_walrus_driver(),
            "--pass",
            ",".join([
                "runtime_memory_reservation",
                "lower_act",
                "lower_dve",
                "lower_ap_offset",
                "codegen",
                "neff_packager",
            ]),
            "-i", inp,
        ]
        if _bu.checkenv("CONCOURSE_WALRUS_GENERATE_MANIFEST"):
            cmd += ["-o", "walrus_bir.out.json"]
        cmd += [
            "--neff-output-filename", outp,
            "--enable-birsim=true",
            "--mem-mode=physical",
            "--policy=0",
            "--enable-ldw-opt=false",
            "--assign-static-dmas-to-sp=false",
            f"--dram-page-size={_bu.aot_getenv('NEURON_SCRATCHPAD_PAGE_SIZE', '256')}",
            f"--enable-neff-debug-info={'false' if _bu.aot_checkenv('CONCOURSE_SCRUB_NEFF_DEBUG_INFO') else 'true'}",
            "--jobs", "8",
            *_bu.get_walrus_args(
                _bu.get_bir_arch(tmpdir, inp) if arch is None else arch,
                tmpdir, dve_root=dve_root),
        ]
        result = _bu.run_command(cmd, cwd=tmpdir)
        if result is not None:
            (_P(tmpdir) / "log.txt").write_text(result.stdout)
        return f"{tmpdir}/{outp}"

    _bu.bir_verify_and_optimise = bir_verify_and_optimise


_patch_walrus_verifier()


_NC_CACHE = {}


def kernel(**inputs):
    if 'nc' not in _NC_CACHE:
        _NC_CACHE['nc'] = build(dbg=None)
    nc = _NC_CACHE['nc']
    in_maps = [host_inputs(inputs, core) for core in range(8)]
    res = run_bass_kernel_spmd(nc, in_maps, core_ids=list(range(8)))
    out = np.empty((B, T, C), np.float32)
    for core in range(8):
        seq, hh = core // 2, core % 2
        out[seq, hh * HT:(hh + 1) * HT] = np.asarray(res.results[core]['out_h'])
    return out


# version 15
# speedup vs baseline: 1.0251x; 1.0251x over previous
import numpy as np
try:
    import concourse.bass as bass
    import concourse.mybir as mybir
    from concourse.bass_utils import run_bass_kernel_spmd
except ImportError:
    import sys
    sys.path.insert(0, '/opt/trn_rl_repo')
    import concourse.bass as bass
    import concourse.mybir as mybir
    from concourse.bass_utils import run_bass_kernel_spmd

F32 = mybir.dt.float32
F32R = mybir.dt.float32r
BF16 = mybir.dt.bfloat16
U8 = mybir.dt.uint8
I32 = mybir.dt.int32
AF = mybir.ActivationFunctionType
ALU = mybir.AluOpType
AX = mybir.AxisListType

NS_SCHED = [
    (3.7111771581458646, -5.620608272237417, 2.1103471800461225),
    (3.329413233677866, -4.062595212811175, 1.2393093931296458),
    (2.9674567262212846, -2.86852510148617, 0.6778708762095051),
    (3.2380560803830135, -3.7372616312341555, 1.0783592811783235),
    (2.981962845278586, -2.911994792086269, 0.6975370488974846),
    (2.8932135288128827, -2.651149828439036, 0.5800612344955369),
    (3.033845782358891, -3.070206636005876, 0.7693889436987285),
    (3.0801300104202463, -3.2150378112771756, 0.8355372166767617),
    (3.1495997998475583, -3.439147089661531, 0.9385570822404736),
    (2.9131136983260744, -2.7085892242404913, 0.6058194688704685),
    (2.9479683094892326, -2.810645048711703, 0.6517416272859141),
    (2.7769556328409347, -2.3271824654484655, 0.43602047708272),
    (3.0105414639697745, -2.998607791517297, 0.7368168363836698),
    (3.4322970197419605, -4.451306469619516, 1.4440793216272883),
    (2.9562048202915693, -2.835034724402922, 0.6627439788262405),
    (3.0908765846627465, -4.007390087857139, 1.3222944767008617),
    (3.151104560640553, -4.25921535736201, 1.499665349660536),
    (2.7207321333327457, -2.7599487977921504, 0.7652923993101875),
    (1.875, -1.25, 0.375),
    (1.875, -1.25, 0.375),
    (1.875, -1.25, 0.375),
    (1.875, -1.25, 0.375),
]

B, T, C, DH = 4, 4096, 1024, 4096
PT = C // 128
HT = 2048
KB = 1024
RGROUPS = [[0, 1], [2, 3], [4, 5], [6, 7]]


class Sem:
    def __init__(self, h):
        self.h = h
        self.n = 0


class Ctx:
    def __init__(self, nc, eng_name, engines, sems, bufs, dbg):
        self.nc = nc
        self.eng = eng_name
        self.E = engines
        self.sems = sems
        self.b = bufs
        self.dbg = dbg

    def w(self, name, sem, val=None):
        if self.eng == name:
            self.E[name].wait_ge(sem.h, sem.n if val is None else val)

    def op(self, name, meth, *a, **kw):
        if self.eng == name:
            return getattr(self.E[name], meth)(*a, **kw)
        return None

    def do(self, name, sem, meth, *a, waits=(), inc_k=1, **kw):
        for (s, v) in waits:
            self.w(name, s, v)
        ins = self.op(name, meth, *a, **kw)
        if ins is not None:
            ins.then_inc(sem.h, inc_k)
        sem.n += inc_k
        return ins

    def sync_all(self, engs=('v', 'a', 't', 'sy', 'g')):
        for e in engs:
            for s in self.sems.values():
                self.w(e, s)


def emit_ns(ctx):
    b = ctx.b
    s_dma, s_pe, s_v, s_a = (ctx.sems[k] for k in ('sy_dma', 'pe', 'v', 'a'))
    X, XT, M, M2, S = b['nsA'], b['nsB'], b['nsC'], b['nsD'], b['nsE']
    psh = b['ps']
    fcol, sc_col = b['fcol'], b['sc_col']
    onesc, onesr, identf = b['onesc'], b['onesr'], b['identf']
    S32 = b['nsE32']

    s_gd0 = ctx.sems['g_dma']
    for ci in range(PT):
        ctx.do('g', s_gd0, 'dma_start', out=S[:, ci, :],
               in_=b['in_gsr'][ci * 128:(ci + 1) * 128, :], inc_k=16)

    for ci in range(PT):
        ctx.do('a', s_a, 'activation', out=psh[:, 6, 0:512], in_=S32[:, ci, 0:512],
               func=AF.Square, accum_out=fcol[:, ci:ci + 1],
               waits=[(s_gd0, None), (s_a, None)])
        ctx.do('a', s_a, 'activation', out=psh[:, 7, 0:512], in_=S32[:, ci, 512:1024],
               func=AF.Square, accum_out=fcol[:, 8 + ci:9 + ci],
               waits=[(s_a, None)])
    ctx.do('v', s_v, 'reduce_sum', out=sc_col[:, 1:2], in_=fcol[:, 0:16], axis=AX.X,
           waits=[(s_a, None)])
    ctx.do('t', s_pe, 'matmul', psh[0:1, 0, 0:1], onesc[:, 0:1], sc_col[:, 1:2],
           start=True, stop=True, waits=[(s_v, None)])
    ctx.do('a', s_a, 'activation', out=fcol[0:1, 0:1], in_=psh[0:1, 0, 0:1],
           func=AF.Sqrt, waits=[(s_pe, None), (s_a, None)])
    ctx.do('v', s_v, 'reciprocal', out=fcol[0:1, 1:2], in_=fcol[0:1, 0:1],
           waits=[(s_a, None), (s_v, None)])
    ctx.do('t', s_pe, 'matmul', psh[:, 0, 1:2], onesr[0:1, :], fcol[0:1, 1:2],
           start=True, stop=True, waits=[(s_v, None)])
    ctx.do('a', s_a, 'copy', out=sc_col[:, 0:1], in_=psh[:, 0, 1:2],
           waits=[(s_pe, None), (s_a, None)])

    for ci in range(PT):
        ctx.do('v', s_v, 'tensor_scalar_mul', out=X[:, ci, :], in0=S32[:, ci, :],
               scalar1=sc_col[:, 0:1], waits=[(s_a, None)] if ci == 0 else ())
    ctx.w('t', s_a)
    v_base, pe_base, ti = s_v.n, s_pe.n, 0
    for cj in range(PT):
        for ci in range(PT):
            bank = ti % 2
            ctx.do('t', s_pe, 'transpose',
                   psh[:, bank, 0:128], S32[:, ci, cj * 128:(cj + 1) * 128], identf[:, :],
                   waits=([(s_v, v_base + ti - 1)] if ti >= 2 else []))
            ctx.do('v', s_v, 'tensor_scalar_mul',
                   out=XT[:, cj, ci * 128:(ci + 1) * 128],
                   in0=psh[:, bank, 0:128], scalar1=sc_col[:, 0:1],
                   waits=[(s_pe, pe_base + ti + 1)])
            ti += 1

    lns = (ctx.sems['ln0'], ctx.sems['ln1'])
    lnc = b['lncols']
    psb = b['psb']
    xtv = b['xtl'][:, 0, :]
    xntl, xnful = b['xntl'], b['xnful']
    RL = {}
    lnst = {'tt': 0}

    def ln_tile():
        tt = lnst['tt']
        lnst['tt'] += 1
        par = tt % 2
        sp = lns[par]
        w = []
        if tt >= 1:
            w += [RL[('lnv', tt - 1)], RL[('lnaq', tt - 1)]]
        ctx.do('sy', sp, 'dma_start', out=xtv,
               in_=b['in_xh'][tt * 128:(tt + 1) * 128, :], waits=w, inc_k=16)
        ctx.do('a', s_a, 'activation', out=b['scs32'][:, :], in_=xtv[:, 0:512],
               func=AF.Square, accum_out=lnc[:, 0:1],
               waits=[(sp, None), (s_a, None)])
        ctx.do('a', s_a, 'activation', out=b['scs32'][:, :], in_=xtv[:, 512:1024],
               func=AF.Square, accum_out=lnc[:, 8:9], waits=[(s_a, None)])
        RL[('lnaq', tt)] = (s_a, s_a.n)
        ctx.do('v', s_v, 'reduce_sum', out=lnc[:, 2:3], in_=xtv, axis=AX.X,
               waits=[(sp, None), (s_v, None)])
        ctx.do('v', s_v, 'tensor_scalar_mul', out=lnc[:, 3:4], in0=lnc[:, 2:3],
               scalar1=-1.0 / 1024.0, waits=[(s_v, None)])
        ctx.do('v', s_v, 'tensor_mul', out=lnc[:, 4:5], in0=lnc[:, 3:4],
               in1=lnc[:, 3:4], waits=[(s_v, None)])
        ctx.do('v', s_v, 'tensor_add', out=lnc[:, 9:10], in0=lnc[:, 0:1],
               in1=lnc[:, 8:9], waits=[RL[('lnaq', tt)], (s_v, None)])
        ctx.do('v', s_v, 'scalar_tensor_tensor', out=lnc[:, 5:6], in0=lnc[:, 9:10],
               scalar=1.0 / 1024.0, in1=lnc[:, 4:5], op0=ALU.mult,
               op1=ALU.subtract, waits=[(s_v, None)])
        ctx.do('a', s_a, 'activation', out=lnc[:, 6:7], in_=lnc[:, 5:6],
               func=AF.Sqrt, bias=b['cc'][:, 0:1], waits=[(s_v, None), (s_a, None)])
        ctx.do('v', s_v, 'reciprocal', out=lnc[:, 7:8], in_=lnc[:, 6:7],
               waits=[(s_a, None), (s_v, None)])
        xw = [(s_v, None)]
        if tt >= 2:
            xw.append(RL[('lntr', tt - 2)])
        ctx.do('v', s_v, 'tensor_scalar', out=xntl[:, par, :], in0=xtv,
               scalar1=lnc[:, 3:4], scalar2=lnc[:, 7:8], op0=ALU.add,
               op1=ALU.mult, waits=xw)
        RL[('lnv', tt)] = (s_v, s_v.n)
        ctx.w('t', *RL[('lnv', tt)])
        for ci in range(PT):
            bank = 6 + ci % 2
            tw = []
            if ci >= 2 and ('lne', tt, ci - 2) in RL:
                tw.append(RL[('lne', tt, ci - 2)])
            elif ci < 2 and tt >= 1:
                tw.append(RL[('lne', tt - 1, ci + 6)])
            ctx.do('t', s_pe, 'transpose', psb[:, bank, 0:128],
                   xntl[:, par, ci * 128:(ci + 1) * 128], b['identb'][:, :],
                   waits=tw)
            ctx.do('a', s_a, 'copy',
                   out=xnful[:, ci, tt * 128:(tt + 1) * 128],
                   in_=psb[:, bank, 0:128],
                   waits=[(s_pe, None), (s_a, None)])
            RL[('lne', tt, ci)] = (s_a, s_a.n)
        RL[('lntr', tt)] = (s_pe, s_pe.n)

    def xn_transpose(src_b, dst, evrec):
        src32 = src_b.bitcast(F32)
        pe_b, a_b = s_pe.n, s_a.n
        ti = 0
        for cj in range(PT):
            ctx.w('t', *evrec[cj])
            for ci in range(PT):
                bank = ti % 2
                tw = [(s_a, a_b + ti - 1)] if ti >= 2 else []
                ctx.do('t', s_pe, 'transpose', psh[:, bank, 0:128],
                       src32[:, cj, ci * 128:(ci + 1) * 128], identf[:, :],
                       waits=tw)
                ctx.do('a', s_a, 'copy',
                       out=dst[:, ci, cj * 128:(cj + 1) * 128],
                       in_=psh[:, bank, 0:128],
                       waits=[(s_pe, pe_b + ti + 1), (s_a, None)])
                ti += 1

    bufs = [X, XT, M, M2, S]
    for itn, (ca, cb, cc) in enumerate(NS_SCHED):
        Xc, XTc, Mc, M2c, Sc = bufs
        ns_mm_sym(ctx, out=Mc, lhsT=Xc, rhs=Xc, psh=psh,
                  twaits=[(s_v, None)])
        ns_mm_sym(ctx, out=M2c, lhsT=Mc, rhs=Mc, psh=psh)
        ctx.w('v', s_a)
        ctx.w('v', s_v)
        for ci in range(PT):
            ctx.do('v', s_v, 'tensor_scalar_mul', out=Sc[:, ci, :], in0=Mc[:, ci, :],
                   scalar1=float(cb))
        ctx.w('v', s_v)
        for ci in range(PT):
            ctx.do('v', s_v, 'scalar_tensor_tensor', out=Mc[:, ci, :], in0=M2c[:, ci, :],
                   scalar=float(cc), in1=Sc[:, ci, :], op0=ALU.mult, op1=ALU.add)
        evrec = ns_mm(ctx, out=Sc, lhsT=Mc, rhs=XTc, evac='axpy', psh=psh,
                      aval=float(ca), xbuf=XTc,
                      twaits=[(s_v, None)], evwaits=[(s_v, None)])
        xn_transpose(Sc, M2c, evrec)
        if itn < 16:
            ln_tile()
        bufs = [M2c, Sc, Xc, XTc, Mc]

    Xf = bufs[0]
    ctx.w('v', s_v)
    ctx.w('v', s_pe)
    ctx.w('v', s_a)
    for ci in range(PT):
        ctx.do('v', s_v, 'tensor_copy', out=b['Wbf'][:, ci, :], in_=Xf[:, ci, :])
    vwb = s_v.n
    if ctx.dbg == 'ns':
        sd = ctx.sems['g_dma']
        ctx.w('g', ctx.sems['v'])
        for ci in range(PT):
            ctx.do('g', sd, 'dma_start', out=b['out_dbg'][ci * 128:(ci + 1) * 128, 0:C],
                   in_=Xf[:, ci, :].bitcast(F32), inc_k=16)
        ctx.w('g', sd)
        return

    s_gd0b = ctx.sems['g_dma']
    ctx.w('g', s_v)
    ctx.w('g', s_pe)
    ctx.w('g', s_a)
    ctx.do('g', s_gd0b, 'dma_start', out=b['wupq'][:, :, :],
           in_=b['in_wupT'].rearrange("(a b) j -> b a j", b=128), inc_k=16)
    Xq = b['xscan']
    pe_b, a_b = s_pe.n, s_a.n
    gi = 0
    for jt in range(PT):
        for tc in range(4):
            bank = 2 + (gi % 4)
            if gi == 0:
                ctx.w('t', s_gd0b)
                ctx.w('t', s_a)
                ctx.w('t', s_v)
            if gi >= 4:
                ctx.w('t', s_a, a_b + (gi - 4) + 1)
            ins = None
            for kt in range(PT):
                ins = ctx.op('t', 'matmul', psh[:, bank, 0:512],
                             b['wupq'][:, kt, jt * 128:(jt + 1) * 128],
                             b['xnful'][:, kt, tc * 512:(tc + 1) * 512],
                             start=(kt == 0), stop=(kt == PT - 1))
            if ins is not None:
                ins.then_inc(s_pe.h, 1)
            s_pe.n += 1
            ctx.do('a', s_a, 'copy', out=Xq[:, jt, tc * 512:(tc + 1) * 512],
                   in_=psh[:, bank, 0:512],
                   waits=[(s_pe, pe_b + gi + 1), (s_a, None)] +
                         ([(s_v, vwb)] if gi == 0 else []))
            gi += 1


def ns_mm(ctx, out, lhsT, rhs, evac, psh, aval=None, xbuf=None, twaits=(), evwaits=()):
    s_pe, s_v, s_a = (ctx.sems[k] for k in ('pe', 'v', 'a'))
    s_ev = s_a if evac == 'copy' else s_v
    ev_eng = 'a' if evac == 'copy' else 'v'
    for (s, v) in twaits:
        ctx.w('t', s, v)
    ctx.w('t', s_a)
    ctx.w('t', s_v)
    for (s, v) in evwaits:
        ctx.w(ev_eng, s, v)
    pe_base, ev_base = s_pe.n, s_ev.n
    evrec = []
    gi = 0
    for it in range(PT):
        for jc in range(2):
            bank = 2 + (gi % 4)
            if gi >= 4:
                ctx.w('t', s_ev, ev_base + (gi - 4) + 1)
            ins = None
            for kt in range(PT):
                ins = ctx.op('t', 'matmul', psh[:, bank, 0:512],
                             lhsT[:, kt, it * 128:(it + 1) * 128],
                             rhs[:, kt, jc * 512:(jc + 1) * 512],
                             start=(kt == 0), stop=(kt == PT - 1))
            if ins is not None:
                ins.then_inc(s_pe.h, 1)
            s_pe.n += 1
            ctx.w(ev_eng, s_pe, pe_base + gi + 1)
            if evac == 'copy':
                ctx.do('a', s_a, 'copy', out=out[:, it, jc * 512:(jc + 1) * 512],
                       in_=psh[:, bank, 0:512], waits=[(s_a, None)])
            else:
                ctx.do('v', s_v, 'scalar_tensor_tensor',
                       out=out[:, it, jc * 512:(jc + 1) * 512],
                       in0=xbuf[:, it, jc * 512:(jc + 1) * 512],
                       scalar=aval, in1=psh[:, bank, 0:512],
                       op0=ALU.mult, op1=ALU.add)
            gi += 1
        evrec.append((s_ev, s_ev.n))
    return evrec


def ns_mm_sym(ctx, out, lhsT, rhs, psh, twaits=()):
    s_pe, s_v, s_a = (ctx.sems[k] for k in ('pe', 'v', 'a'))
    for (s, v) in twaits:
        ctx.w('t', s, v)
    ctx.w('t', s_a)
    ctx.w('t', s_v)
    pe_base, a_base = s_pe.n, s_a.n
    erec = {}
    gi = 0
    for it in range(PT):
        c0 = it * 128
        pieces = []
        if c0 < 512:
            pieces.append((c0, 512))
            pieces.append((512, 1024))
        else:
            pieces.append((c0, 1024))
        for (lo, hi) in pieces:
            bank = 2 + (gi % 4)
            if gi >= 4:
                ctx.w('t', s_a, a_base + (gi - 4) + 1)
            ins = None
            for kt in range(PT):
                ins = ctx.op('t', 'matmul', psh[:, bank, 0:hi - lo],
                             lhsT[:, kt, it * 128:(it + 1) * 128],
                             rhs[:, kt, lo:hi],
                             start=(kt == 0), stop=(kt == PT - 1))
            if ins is not None:
                ins.then_inc(s_pe.h, 1)
            s_pe.n += 1
            ctx.do('a', s_a, 'copy', out=out[:, it, lo:hi],
                   in_=psh[:, bank, 0:hi - lo],
                   waits=[(s_pe, s_pe.n), (s_a, None)])
            gi += 1
        erec[it] = (s_a, s_a.n)
    out32 = out.bitcast(F32)
    identf = ctx.b['identf']
    mp_base, ma_base = s_pe.n, s_a.n
    ti = 0
    for it in range(PT):
        for jb in range(it + 1, PT):
            bank = ti % 2
            tw = [erec[it]] if jb == it + 1 else []
            if ti >= 2:
                tw.append((s_a, ma_base + ti - 1))
            ctx.do('t', s_pe, 'transpose', psh[:, bank, 0:128],
                   out32[:, it, jb * 128:(jb + 1) * 128], identf[:, :],
                   waits=tw)
            ctx.do('a', s_a, 'copy',
                   out=out[:, jb, it * 128:(it + 1) * 128],
                   in_=psh[:, bank, 0:128],
                   waits=[(s_pe, mp_base + ti + 1), (s_a, None)])
            ti += 1


def emit_ln(ctx, src_hbm, src0, ntiles, xn_dst, dst_off, xt, xntok):
    b = ctx.b
    s_v, s_a, s_pe = (ctx.sems[k] for k in ('v', 'a', 'pe'))
    lns = (ctx.sems['ln0'], ctx.sems['ln1'])
    lnc = b['lncols']
    psh = b['ps']
    v_base, a_base = s_v.n, s_a.n
    for tt in range(ntiles):
        par = tt % 2
        sp = lns[par]
        ctx.do('sy', sp, 'dma_start', out=xt[:, par, :],
               in_=src_hbm[src0 + tt * 128:src0 + (tt + 1) * 128, :],
               waits=[(s_v, v_base + tt * 7)] if tt >= 2 else [(s_v, v_base)],
               inc_k=16)
        sq = lnc[:, par:par + 1]
        ctx.do('a', s_a, 'activation', out=psh[:, 6, 0:512], in_=xt[:, par, 0:512],
               func=AF.Square, accum_out=sq,
               waits=[(sp, None), (s_a, None)])
        ctx.do('a', s_a, 'activation', out=psh[:, 7, 0:512], in_=xt[:, par, 512:1024],
               func=AF.Square, accum_out=lnc[:, 8:9], waits=[(s_a, None)])
        ctx.do('v', s_v, 'reduce_sum', out=lnc[:, 2:3], in_=xt[:, par, :], axis=AX.X,
               waits=[(sp, None), (s_v, None)])
        ctx.do('v', s_v, 'tensor_scalar_mul', out=lnc[:, 3:4], in0=lnc[:, 2:3],
               scalar1=-1.0 / 1024.0, waits=[(s_v, None)])
        ctx.do('v', s_v, 'tensor_mul', out=lnc[:, 4:5], in0=lnc[:, 3:4], in1=lnc[:, 3:4],
               waits=[(s_v, None)])
        ctx.do('v', s_v, 'tensor_add', out=lnc[:, 9:10], in0=sq, in1=lnc[:, 8:9],
               waits=[(s_a, None), (s_v, None)])
        ctx.do('v', s_v, 'scalar_tensor_tensor', out=lnc[:, 5:6], in0=lnc[:, 9:10],
               scalar=1.0 / 1024.0, in1=lnc[:, 4:5], op0=ALU.mult, op1=ALU.subtract,
               waits=[(s_v, None)])
        ctx.do('a', s_a, 'activation', out=lnc[:, 6:7], in_=lnc[:, 5:6], func=AF.Sqrt,
               bias=b['cc'][:, 0:1], waits=[(s_v, None), (s_a, None)])
        ctx.do('v', s_v, 'reciprocal', out=lnc[:, 7:8], in_=lnc[:, 6:7],
               waits=[(s_a, None), (s_v, None)])
        ctx.do('v', s_v, 'tensor_scalar', out=xntok[:, par, :], in0=xt[:, par, :],
               scalar1=lnc[:, 3:4], scalar2=lnc[:, 7:8], op0=ALU.add, op1=ALU.mult,
               waits=[(s_v, None)])
        for ci in range(PT):
            bank = ci % 2
            tw = [(s_v, None)]
            if tt + ci > 0:
                tw.append((s_a, None))
            ctx.do('t', s_pe, 'transpose', b['psb'][:, bank, 0:128],
                   xntok[:, par, ci * 128:(ci + 1) * 128], b['identb'][:, :],
                   waits=tw)
            ctx.do('a', s_a, 'copy',
                   out=xn_dst[:, ci, dst_off + tt * 128:dst_off + (tt + 1) * 128],
                   in_=b['psb'][:, bank, 0:128], waits=[(s_pe, None), (s_a, None)])


def emit_p1(ctx):
    b = ctx.b
    s_gd = ctx.sems['g_dma']
    ctx.w('g', s_gd)
    for ci in range(PT):
        ctx.do('g', s_gd, 'dma_start', out=b['hz'][:, ci, :],
               in_=b['in_idh'][:, ci, :], inc_k=16)
    if ctx.dbg == 'q':
        X = b['xscan']
        ctx.w('g', ctx.sems['a'])
        for ci in range(PT):
            ctx.do('g', s_gd, 'dma_start', out=b['out_dbg'][ci * 128:(ci + 1) * 128, 0:HT],
                   in_=X[:, ci, 0:HT], inc_k=16)
        ctx.w('g', s_gd)


def identf_full(ctx):
    return ctx.b['identf'][:, :]


def emit_scan(ctx):
    b = ctx.b
    s2v, s2a, s2p = (ctx.sems[k] for k in ('v', 'a', 'pe'))
    s_gd, s_cc, s_hx = (ctx.sems[k] for k in ('g_dma', 'cc', 'hx'))
    psh, psb = b['ps'], b['psb']
    X, HZ, Wb = b['xscan'], b['hz'], b['Wbf']
    sbb, prodb, vtok = b['sbb'], b['prodb'], b['vtok']
    prodb2 = b['prodb2']
    nmx, esr, eex, attb2 = b['nmx'], b['esr'], b['eex'], b['attb2']
    ssum, rcs = b['ssum'], b['rcs']
    zb1, zb2, zpg = b['zb1'], b['zb2'], b['zpg']
    msb, sdv, rs0, rsv = b['msb'], b['sdv'], b['rs0'], b['rsv']
    zTs, tmp8, junk256 = b['zTs'], b['tmp8'], b['junk256']
    ones32b, scs32 = b['ones32b'], b['scs32']
    nrounds = b.get('dbg_rounds', 12)
    NCH = 4
    R = {}
    hcond = [None]

    def get_cond(G):
        if hcond[0] is None:
            hcond[0] = G.compute_val(G.partition_id() % 2)
        return hcond[0]

    def seqi(rc):
        return rc[0] * NCH + (NCH - 1 - rc[1])

    def chunk_at(idx):
        if idx < 0 or idx >= nrounds * NCH:
            return None
        rr, cc2 = divmod(idx, NCH)
        return (rr, NCH - 1 - cc2)

    def geom(rc):
        r, c = rc
        d = 1 << r
        c0 = c * 512
        bd = min(max(d - c0, 0), 512)
        return d, c % 2, c0, bd

    def lslice(p, rc, lo, hi):
        d, par, c0, bd = geom(rc)
        if hi <= bd:
            h0 = HT + c0 - d
            return HZ[:, p, h0 + lo:h0 + hi], True
        return X[:, p, c0 + lo - d:c0 + hi - d], False

    def cond_land(r, lo, hi):
        d = 1 << r
        if ctx.eng == 'g':
            G = ctx.E['g']
            cond = get_cond(G)
            kw = {'allow_slow_non_contiguous': True} if hi - lo < 2 else {}
            ins = G.dma_start(out=HZ[:, :, HT - d + lo:HT - d + hi],
                              in_=b['hcout'][r][0, :, :, lo:hi], cond=cond, **kw)
            ins.then_inc(s_hx.h, 16)
        s_hx.n += 16

    def exchange_head(r, full):
        d = 1 << r
        ctx.w('g', s2v)
        ctx.w('g', s2a)
        ctx.w('g', s_hx)
        ctx.w('g', s_gd)
        kw = {'allow_slow_non_contiguous': True} if d < 2 else {}
        ctx.do('g', s_gd, 'dma_start', out=b['hcin'][r][:, :, :],
               in_=X[:, :, HT - d:HT], inc_k=16, **kw)
        R[('xtail', r)] = (s_gd, s_gd.n)
        ctx.w('g', s_gd)
        ctx.do('g', s_cc, 'collective_compute', "AllGather", ALU.bypass,
               ins=[b['hcin'][r][:, :, :]], outs=[b['hcout'][r][:, :, :, :]],
               replica_groups=RGROUPS)
        ctx.w('g', s_cc)
        split = 0 if full else d - (d >> 1)
        cond_land(r, 0, d - split)
        if full:
            R[('hz', r)] = (s_hx, s_hx.n)

    def exchange_tail(r):
        d = 1 << r
        ctx.w('g', s2v)
        cond_land(r, d - (d >> 1), d)
        R[('hz', r)] = (s_hx, s_hx.n)

    def front_s(rc):
        d, par, c0, bd = geom(rc)
        w = [(s2v, s2v.n)]
        pk = chunk_at(seqi(rc) - 1)
        if pk and ('vtokmm', pk) in R:
            w.append(R[('vtokmm', pk)])
        if bd > 0 and ('hz', rc[0]) in R:
            w.append(R[('hz', rc[0])])
        first = True
        for p in range(PT):
            if bd > 0:
                la, _ = lslice(p, rc, 0, bd)
                ctx.do('v', s2v, 'tensor_add',
                       out=sbb[:, p, 0:bd], in0=la, in1=X[:, p, c0:c0 + bd],
                       waits=w if first else ())
                first = False
            if bd < 512:
                la, _ = lslice(p, rc, bd, 512)
                ctx.do('v', s2v, 'tensor_add',
                       out=sbb[:, p, bd:512], in0=la,
                       in1=X[:, p, c0 + bd:c0 + 512],
                       waits=w if first else ())
                first = False
        R[('s', rc)] = (s2v, s2v.n)

    def front_vtok(rc):
        d, par, c0, bd = geom(rc)
        ctx.w('t', *R[('s', rc)])
        pk = chunk_at(seqi(rc) - 1)
        if pk and ('vtokevac', pk) in R:
            ctx.w('t', *R[('vtokevac', pk)])
        pk2 = chunk_at(seqi(rc) - 2)
        aw = []
        if pk2 and ('z', pk2) in R:
            aw.append(R[('z', pk2)])
        for g in range(4):
            gb0 = 2 * (g % 2)
            if g >= 2:
                ctx.w('t', *R[('vtokevacg', rc, g - 2)])
            for jc in range(2):
                ins = None
                for kt in range(PT):
                    ins = ctx.op('t', 'matmul', psh[:, gb0 + jc, 0:512],
                                 sbb[:, kt, g * 128:(g + 1) * 128],
                                 Wb[:, kt, jc * 512:(jc + 1) * 512],
                                 start=(kt == 0), stop=(kt == PT - 1))
                if ins is not None:
                    ins.then_inc(s2p.h, 1)
                s2p.n += 1
            R[('vtokg', rc, g)] = (s2p, s2p.n)
            ctx.w('a', *R[('vtokg', rc, g)])
            for jc in range(2):
                ctx.do('a', s2a, 'copy',
                       out=vtok[:, par, g, jc * 512:(jc + 1) * 512],
                       in_=psh[:, gb0 + jc, 0:512],
                       waits=(aw + [(s2a, s2a.n)]) if (g == 0 and jc == 0)
                       else [(s2a, s2a.n)])
            R[('vtokevacg', rc, g)] = (s2a, s2a.n)
        R[('vtokmm', rc)] = (s2p, s2p.n)
        R[('vtokevac', rc)] = (s2a, s2a.n)

    def front_prod(rc, bi):
        d, par, c0, bd = geom(rc)
        gbi = seqi(rc) * 4 + bi
        pb = (prodb, prodb2)[gbi % 2]
        l = bi
        w = []
        if ('scoremm', gbi - 2) in R:
            w.append(R[('scoremm', gbi - 2)])
        first = True
        for m in range(4):
            for h in range(2):
                tix = m * 2 + h
                qp, kp = 2 * l + h, 2 * m + h
                if bd > 0:
                    la, _ = lslice(qp, rc, 0, bd)
                    ctx.do('v', s2v, 'tensor_mul',
                           out=pb[:, tix, 0:bd], in0=la,
                           in1=X[:, kp, c0:c0 + bd],
                           waits=w if first else ())
                    first = False
                if bd < 512:
                    la, _ = lslice(qp, rc, bd, 512)
                    ctx.do('v', s2v, 'tensor_mul',
                           out=pb[:, tix, bd:512], in0=la,
                           in1=X[:, kp, c0 + bd:c0 + 512],
                           waits=w if first else ())
                    first = False
        R[('prodv', gbi)] = (s2v, s2v.n)

    def front_score(rc, bi):
        gbi = seqi(rc) * 4 + bi
        pb = (prodb, prodb2)[gbi % 2]
        ctx.w('t', *R[('prodv', gbi)])
        if ('scsevac', gbi - 1) in R:
            ctx.w('t', *R[('scsevac', gbi - 1)])
        for m in range(4):
            base = 32 * m
            ins = None
            for h in range(2):
                tix = m * 2 + h
                kw = {'tile_position': (0, 96)} if base == 96 else {}
                ins = ctx.op('t', 'matmul', psh[base:base + 32, 4, 0:512],
                             ones32b[:, :], pb[:, tix, 0:512],
                             start=(h == 0), stop=(h == 1), **kw)
            if ins is not None:
                ins.then_inc(s2p.h, 1)
            s2p.n += 1
        R[('scoremm', gbi)] = (s2p, s2p.n)
        ctx.w('a', *R[('scoremm', gbi)])
        if ('attT', gbi - 1) in R:
            ctx.w('a', *R[('attT', gbi - 1)])
        ctx.do('a', s2a, 'copy', out=scs32[:, :], in_=psh[:, 4, 0:512],
               waits=[(s2a, s2a.n)])
        R[('scsevac', gbi)] = (s2a, s2a.n)
        ctx.w('t', *R[('scsevac', gbi)])
        if ('smaxcp', gbi - 1) in R:
            ctx.w('t', *R[('smaxcp', gbi - 1)])
        for g in range(4):
            ctx.do('t', s2p, 'transpose', psh[:, 5, g * 128:(g + 1) * 128],
                   scs32[:, g * 128:(g + 1) * 128], b['identf'][:, :])
        R[('attT', gbi)] = (s2p, s2p.n)

    def front_smax(rc, bi):
        d, par, c0, bd = geom(rc)
        gbi = seqi(rc) * 4 + bi
        l = bi
        ps5v = psh[:, 5, 0:512].rearrange("p (g m rep) -> p g m rep",
                                          m=4, rep=32)[:, :, :, 0]
        e16 = esr[:, 0:16].rearrange("p (g m) -> p g m", m=4)
        ee16 = esr[:, 16:32].rearrange("p (g m) -> p g m", m=4)
        x16 = eex[:, 0:16].rearrange("p (g m) -> p g m", m=4)
        ctx.do('v', s2v, 'tensor_copy', out=esr[:, 0:16], in_=ps5v,
               waits=[R[('attT', gbi)], (s2v, s2v.n)])
        R[('smaxcp', gbi)] = (s2v, s2v.n)
        ctx.do('v', s2v, 'reduce_max', out=nmx[:, 0:4], in_=e16, axis=AX.X,
               negate=True, waits=[(s2v, s2v.n)])
        for m in range(4):
            ctx.do('v', s2v, 'tensor_add', out=x16[:, :, m], in0=e16[:, :, m],
                   in1=nmx[:, 0:4], waits=[(s2v, s2v.n)] if m == 0 else ())
        ctx.do('a', s2a, 'activation', out=esr[:, 16:32], in_=eex[:, 0:16],
               func=AF.Exp, scale=b['cc'][:, 2:3],
               waits=[(s2v, s2v.n), (s2a, s2a.n)])
        ctx.do('v', s2v, 'reduce_sum', out=ssum[:, 0:4], in_=ee16, axis=AX.X,
               waits=[(s2a, s2a.n), (s2v, s2v.n)])
        ctx.do('v', s2v, 'reciprocal', out=rcs[:, 0:4], in_=ssum[:, 0:4],
               waits=[(s2v, s2v.n)])
        av = attb2[:, par, :].rearrange("p (g w) -> p g w", w=16)
        for m in range(4):
            ctx.do('v', s2v, 'tensor_mul', out=av[:, :, 4 * l + m],
                   in0=ee16[:, :, m], in1=rcs[:, 0:4],
                   waits=[(s2v, s2v.n)] if m == 0 else ())
        if bi == 3:
            R[('smax', rc)] = (s2v, s2v.n)

    def back_group(rc, g):
        d, par, c0, bd = geom(rc)
        att = attb2[:, par, :]
        iw = [(s2v, s2v.n)]
        if g == 0:
            iw.append(R[('vtokevac', rc)])
        for l in range(4):
            cA = 16 * g + 4 * l
            ctx.do('v', s2v, 'tensor_scalar_mul',
                   out=zb1[:, l * 256:(l + 1) * 256],
                   in0=vtok[:, par, g, 0:256], scalar1=att[:, cA:cA + 1],
                   waits=iw if l == 0 else ())
        for mi, (src_, dst_) in enumerate(((zb1, zb2), (zb2, zb1), (zb1, zb2))):
            m = mi + 1
            ctx.w('v', s2v)
            for l in range(4):
                cA = 16 * g + 4 * l + m
                ctx.do('v', s2v, 'scalar_tensor_tensor',
                       out=dst_[:, l * 256:(l + 1) * 256],
                       in0=vtok[:, par, g, m * 256:(m + 1) * 256],
                       scalar=att[:, cA:cA + 1],
                       in1=src_[:, l * 256:(l + 1) * 256],
                       op0=ALU.mult, op1=ALU.add)
        R[('chainv',)] = (s2v, s2v.n)
        R[('z', rc)] = (s2v, s2v.n)
        for l in range(4):
            ctx.do('a', s2a, 'activation', out=junk256[:, :],
                   in_=zb2[:, l * 256:(l + 1) * 256], func=AF.Square,
                   accum_out=msb[:, 4 * g + l:4 * g + l + 1],
                   waits=[R[('chainv',)], (s2a, s2a.n)])
        ctx.do('a', s2a, 'activation', out=sdv[:, 4 * g:4 * g + 4],
               in_=msb[:, 4 * g:4 * g + 4], func=AF.Sqrt,
               scale=b['cc'][:, 3:4], bias=b['cc'][:, 1:2],
               waits=[(s2a, s2a.n)])
        ctx.do('v', s2v, 'reciprocal', out=rs0[:, 4 * g:4 * g + 4],
               in_=sdv[:, 4 * g:4 * g + 4], waits=[(s2a, s2a.n), (s2v, s2v.n)])
        for l in range(4):
            ctx.do('v', s2v, 'tensor_scalar_mul',
                   out=rsv[:, 4 * g + l:4 * g + l + 1],
                   in0=rs0[:, 4 * g + l:4 * g + l + 1],
                   scalar1=1.0 / (1.0 + l),
                   waits=[(s2v, s2v.n)] if l == 0 else ())
        ctx.w('v', s2v)
        for l in range(4):
            ctx.do('v', s2v, 'tensor_scalar_mul',
                   out=zpg[:, l * 256:(l + 1) * 256],
                   in0=zb2[:, l * 256:(l + 1) * 256],
                   scalar1=rsv[:, 4 * g + l:4 * g + l + 1])
        R[('zpg', rc, g)] = (s2v, s2v.n)
        ctx.w('t', *R[('zpg', rc, g)])
        if ('zTevacg', rc, g - 1) in R:
            ctx.w('t', *R[('zTevacg', rc, g - 1)])
        for p in range(PT):
            bank = 6 + (p // 4)
            ctx.do('t', s2p, 'transpose',
                   psb[:, bank, (p % 4) * 128:(p % 4 + 1) * 128],
                   zpg[:, p * 128:(p + 1) * 128], b['identb'][:, :])
        R[('zT', rc, g)] = (s2p, s2p.n)

    def back_group_evac(rc, g):
        ctx.w('a', *R[('zT', rc, g)])
        if g == 0 and ('finallast',) in R:
            ctx.w('a', *R[('finallast',)])
        for p in range(PT):
            bank = 6 + (p // 4)
            ctx.do('a', s2a, 'copy', out=zTs[:, p, g * 128:(g + 1) * 128],
                   in_=psb[:, bank, (p % 4) * 128:(p % 4 + 1) * 128],
                   waits=[(s2a, s2a.n)])
        R[('zTevacg', rc, g)] = (s2a, s2a.n)
        if g == 3:
            R[('zTevac', rc)] = (s2a, s2a.n)

    def back_final(rc):
        d, par, c0, bd = geom(rc)
        ctx.w('v', *R[('zTevac', rc)])
        ctx.w('v', s2v)
        ctx.w('v', s_gd)
        direct = (bd == 0 and d >= 512)
        if direct:
            for p in range(PT):
                la, _ = lslice(p, rc, 0, 512)
                ctx.do('v', s2v, 'tensor_add', out=X[:, p, c0:c0 + 512],
                       in0=zTs[:, p, 0:512], in1=la)
        else:
            for half in (1, 0):
                lo, hi = half * 256, half * 256 + 256
                lo2 = max(lo, bd)
                for p in range(PT):
                    if bd > lo:
                        e = min(bd, hi)
                        la, _ = lslice(p, rc, lo, e)
                        ctx.do('v', s2v, 'tensor_add',
                               out=tmp8[:, p, 0:e - lo], in0=zTs[:, p, lo:e],
                               in1=la)
                    if lo2 < hi:
                        la, _ = lslice(p, rc, lo2, hi)
                        ctx.do('v', s2v, 'tensor_add',
                               out=tmp8[:, p, lo2 - lo:hi - lo],
                               in0=zTs[:, p, lo2:hi], in1=la)
                ctx.w('v', s2v)
                for p in range(PT):
                    ctx.do('v', s2v, 'tensor_copy', out=X[:, p, c0 + lo:c0 + hi],
                           in_=tmp8[:, p, 0:256])
                ctx.w('v', s2v)
        R[('final', rc)] = (s2v, s2v.n)
        R[('finallast',)] = (s2v, s2v.n)

    def back_all(rc):
        for g in range(4):
            back_group(rc, g)
            back_group_evac(rc, g)
        back_final(rc)

    order = [chunk_at(i) for i in range(nrounds * NCH)]
    s_done = set()
    for i, rc in enumerate(order):
        new_round = (rc[1] == NCH - 1)
        r = rc[0]
        need_late = new_round and i >= 1 and (1 << r) > 512 * (NCH - 1)
        pv = order[i - 1] if i >= 1 else None
        if need_late:
            back_all(pv)
            pv = None
        if new_round:
            exchange_head(r, full=(r == 0 or need_late))
        if rc not in s_done:
            front_s(rc)
        front_vtok(rc)
        front_prod(rc, 0)
        front_prod(rc, 1)
        for bi in range(4):
            front_score(rc, bi)
            if pv is not None:
                back_group(pv, bi)
            if bi < 2:
                front_prod(rc, bi + 2)
            if bi == 2 and i + 1 < len(order):
                nxt = order[i + 1]
                if (nxt[0] == r or (nxt[0] == r + 1 and nxt[0] <= 9)) \
                        and nxt != (10, 2):
                    front_s(nxt)
                    s_done.add(nxt)
            front_smax(rc, bi)
            if pv is not None:
                back_group_evac(pv, bi)
        if pv is not None:
            back_final(pv)
        if new_round and not (r == 0 or need_late):
            exchange_tail(r)
    back_all(order[-1])

    if ctx.dbg == 'scan':
        sd = ctx.sems['g_dma']
        ctx.w('g', s2v)
        ctx.w('g', sd)
        for ci in range(PT):
            ctx.do('g', sd, 'dma_start', out=b['out_dbg'][ci * 128:(ci + 1) * 128, 0:HT],
                   in_=X[:, ci, 0:HT], inc_k=16)
        ctx.w('g', sd)


def emit_p3(ctx):
    b = ctx.b
    s_dma, s_gd, s_v, s_a, s_pe = (ctx.sems[k] for k in
                                   ('sy_dma', 'g_dma', 'v', 'a', 'pe'))
    X = b['xscan']

    emit_ln(ctx, b['in_xh'], 0, 16, b['xnh'], 0, b['xt3'], b['xnt3'])
    for ci in range(PT):
        ctx.do('g', s_gd, 'dma_start', out=b['wvbf'][:, ci, :],
               in_=b['in_wvT'][ci * 128:(ci + 1) * 128, :], inc_k=16)
    ctx.w('t', s_gd)
    ctx.w('t', s_a)
    ctx.w('t', s_v)
    pe_b = s_pe.n
    a_b = s_a.n
    gi = 0
    for jt in range(PT):
        for tc in range(4):
            bank = 2 + (gi % 4)
            if gi >= 4:
                ctx.w('t', s_a, a_b + gi - 4 + 1)
            ins = None
            for kt in range(PT):
                ins = ctx.op('t', 'matmul', b['ps'][:, bank, 0:512],
                             b['wvbf'][:, kt, jt * 128:(jt + 1) * 128],
                             b['xnh'][:, kt, tc * 512:(tc + 1) * 512],
                             start=(kt == 0), stop=(kt == PT - 1))
            if ins is not None:
                ins.then_inc(s_pe.h, 1)
            s_pe.n += 1
            ctx.do('a', s_a, 'copy', out=b['vbf'][:, jt, tc * 512:(tc + 1) * 512],
                   in_=b['ps'][:, bank, 0:512],
                   waits=[(s_pe, pe_b + gi + 1), (s_a, None)])
            gi += 1
    ctx.w('v', s_a)
    ctx.w('v', s_v)
    for ci in range(PT):
        ctx.do('v', s_v, 'tensor_mul', out=b['ubf'][:, ci, :], in0=X[:, ci, :],
               in1=b['vbf'][:, ci, :])
    ctx.w('g', s_v)
    for ci in range(PT):
        ctx.do('g', s_gd, 'dma_start', out=b['wprojbf'][:, ci, :],
               in_=b['in_wprojT'][ci * 128:(ci + 1) * 128, :], inc_k=16)
    ctx.w('t', s_gd)
    ctx.w('t', s_v)
    pe_b, v_b = s_pe.n, s_v.n
    for tg in range(16):
        par = tg % 2
        xgp = ctx.sems['xg0'] if par == 0 else ctx.sems['xg1']
        ctx.do('sy', xgp, 'dma_start', out=b['xtg'][:, par, :],
               in_=b['in_xh'][tg * 128:(tg + 1) * 128, :],
               waits=[(s_v, v_b + (tg - 2) * 2 + 2)] if tg >= 2 else (), inc_k=16)
        if tg >= 2:
            ctx.w('t', s_v, v_b + (tg - 2) * 2 + 2)
        for jc in range(2):
            ins = None
            for kt in range(PT):
                ins = ctx.op('t', 'matmul', b['ps'][:, 2 * par + jc, 0:512],
                             b['ubf'][:, kt, tg * 128:(tg + 1) * 128],
                             b['wprojbf'][:, kt, jc * 512:(jc + 1) * 512],
                             start=(kt == 0), stop=(kt == PT - 1))
            if ins is not None:
                ins.then_inc(s_pe.h, 1)
            s_pe.n += 1
        ctx.w('v', s_pe, pe_b + 2 * (tg + 1))
        ctx.w('v', xgp)
        ctx.w('v', s_gd)
        for jc in range(2):
            ctx.do('v', s_v, 'tensor_add', out=b['oat'][:, jc * 512:(jc + 1) * 512],
                   in0=b['ps'][:, 2 * par + jc, 0:512],
                   in1=b['xtg'][:, par, jc * 512:(jc + 1) * 512])
        ctx.do('g', s_gd, 'dma_start', out=b['oasp'][tg * 128:(tg + 1) * 128, :],
               in_=b['oat'][:, :], waits=[(s_v, None)], inc_k=16)
    ctx.w('g', s_a)
    ctx.do('g', s_gd, 'dma_start', out=b['xnsp'][:, :, :], in_=b['xnh'][:, :, :],
           inc_k=16)
    ctx.w('g', s_gd)


def emit_p4(ctx):
    b = ctx.b
    s_dma, s_gd, s_v, s_a, s_pe = (ctx.sems[k] for k in
                                   ('sy_dma', 'g_dma', 'v', 'a', 'pe'))
    s_out = ctx.sems['outd']
    psh = b['ps']
    abuf, bbuf, w3v = b['abuf'], b['bbuf'], b['w3bf']
    xns = b['xnhsub']
    ctx.w('sy', s_dma)
    ctx.do('sy', s_dma, 'dma_start', out=b['s1c'][:, :], in_=b['in_s1c'][:, :], inc_k=16)
    ctx.do('sy', s_dma, 'dma_start', out=b['s2c'][:, :], in_=b['in_s2c'][:, :], inc_k=16)
    ctx.do('sy', s_dma, 'dma_start', out=b['sh1c'][:, :], in_=b['in_sh1c'][:, :], inc_k=16)
    ctx.do('sy', s_dma, 'dma_start', out=b['sh2c'][:, :], in_=b['in_sh2c'][:, :], inc_k=16)
    for e in ('v', 'a', 't'):
        ctx.w(e, s_dma)
    R = {}

    for sh in range(2):
        t0 = sh * 1024
        ctx.w('sy', s_dma)
        if ('w12mm', sh - 1) in R:
            ctx.w('sy', *R[('w12mm', sh - 1)])
        for ci in range(PT):
            ctx.do('sy', s_dma, 'dma_start', out=xns[:, ci, :],
                   in_=b['xnsp'][:, ci, t0:t0 + 1024], inc_k=16)
        for js in range(8):
            par = js % 2
            stp = ctx.sems['st0'] if par == 0 else ctx.sems['st1']
            w = [R[('w12s', sh, js - 2)]] if ('w12s', sh, js - 2) in R else \
                ([R[('hmm_last',)]] if ('hmm_last',) in R else [])
            ctx.do('g', stp, 'dma_start', out=b['w1st'][:, par],
                   in_=b['in_w1s'][js].rearrange("(a b) j -> b a j", b=128),
                   waits=w, inc_k=16)
            ctx.do('g', stp, 'dma_start', out=b['w2st'][:, par],
                   in_=b['in_w2s'][js].rearrange("(a b) j -> b a j", b=128),
                   inc_k=16)
            ctx.w('t', stp)
            if js == 0:
                ctx.w('t', s_dma)
                ctx.w('t', s_a)
                ctx.w('t', s_v)
            gi0 = js * 8
            for jt in range(4):
                for tc in range(2):
                    gi = gi0 + jt * 2 + tc
                    b1, b2 = 2 * (gi % 2), 2 * (gi % 2) + 1
                    if ('aevac', sh, gi - 2) in R:
                        ctx.w('t', *R[('aevac', sh, gi - 2)])
                        ctx.w('t', *R[('vstt', sh, gi - 2)])
                    for (wst, bank) in ((b['w1st'], b1), (b['w2st'], b2)):
                        ins = None
                        for kt in range(PT):
                            ins = ctx.op('t', 'matmul', psh[:, bank, 0:512],
                                         wst[:, par, kt, jt * 128:(jt + 1) * 128],
                                         xns[:, kt, tc * 512:(tc + 1) * 512],
                                         start=(kt == 0), stop=(kt == PT - 1))
                        if ins is not None:
                            ins.then_inc(s_pe.h, 1)
                        s_pe.n += 1
                    R[('w12g', sh, gi)] = (s_pe, s_pe.n)
                    jg = js * 4 + jt
                    ctx.w('a', *R[('w12g', sh, gi)])
                    ta = b['tmpa'][:, (gi % 2) * 512:(gi % 2) * 512 + 512]
                    wl = [(s_a, s_a.n)]
                    if ('vstt', sh, gi - 2) in R:
                        wl.append(R[('vstt', sh, gi - 2)])
                    if gi == 0:
                        wl.append((s_v, s_v.n))
                    ctx.do('a', s_a, 'activation', out=ta,
                           in_=psh[:, b1, 0:512], func=AF.Silu,
                           scale=b['s1c'][:, jg:jg + 1], waits=wl)
                    R[('aevac', sh, gi)] = (s_a, s_a.n)
                    ctx.do('v', s_v, 'scalar_tensor_tensor',
                           out=abuf[:, jg, tc * 512:(tc + 1) * 512],
                           in0=psh[:, b2, 0:512], scalar=b['s2c'][:, jg:jg + 1],
                           in1=ta, op0=ALU.mult, op1=ALU.mult,
                           waits=[R[('aevac', sh, gi)], (s_v, s_v.n)])
                    R[('vstt', sh, gi)] = (s_v, s_v.n)
            R[('w12s', sh, js)] = (s_pe, s_pe.n)
        R[('w12mm', sh)] = (s_pe, s_pe.n)
        emit_h_layer(ctx, b['in_h1s'], b['h1st'], abuf, bbuf, b['sh1c'], R,
                     lastv=R[('vstt', sh, 63)])
        emit_h_layer(ctx, b['in_h2s'], b['h1st'], bbuf, abuf, b['sh2c'], R,
                     lastv=None)
        for kt2 in range(4):
            ctx.do('g', s_gd, 'dma_start', out=w3v[:, kt2 * 8:(kt2 + 1) * 8, :],
                   in_=b['in_w3T'].rearrange("(a b) j -> b a j", b=128)
                   [:, kt2 * 8:(kt2 + 1) * 8, :],
                   waits=[(s_gd, s_gd.n), (s_pe, s_pe.n), (s_a, s_a.n)]
                   if kt2 == 0 else (), inc_k=16)
        ctx.w('sy', s_dma)
        ctx.do('sy', s_dma, 'dma_start', out=b['s3r'][0:1, :], in_=b['in_s3r'][:, :],
               waits=[(s_pe, s_pe.n)] if sh == 0 else (), inc_k=16)
        ctx.do('t', s_pe, 'matmul', psh[:, 6, 0:512], b['onesr'][0:1, :],
               b['s3r'][0:1, 0:512], start=True, stop=True,
               waits=[(s_dma, s_dma.n), (s_a, s_a.n), (s_v, s_v.n)])
        ctx.do('t', s_pe, 'matmul', psh[:, 7, 0:512], b['onesr'][0:1, :],
               b['s3r'][0:1, 512:1024], start=True, stop=True)
        ctx.do('a', s_a, 'copy', out=b['s3bc'][:, 0:512], in_=psh[:, 6, 0:512],
               waits=[(s_pe, s_pe.n), (s_a, s_a.n)])
        ctx.do('a', s_a, 'copy', out=b['s3bc'][:, 512:1024], in_=psh[:, 7, 0:512],
               waits=[(s_a, s_a.n)])
        ctx.w('t', s_gd)
        ctx.w('t', s_a)
        pe_b3, v_b3 = s_pe.n, s_v.n
        for tg in range(8):
            par = tg % 2
            xgp = ctx.sems['xg0'] if par == 0 else ctx.sems['xg1']
            ctx.do('sy', xgp, 'dma_start', out=b['oatile'][:, par, :],
                   in_=b['oasp'][t0 + tg * 128:t0 + (tg + 1) * 128, :],
                   waits=([(s_v, v_b3 + (tg - 2) * 3 + 3)] if tg >= 2 else [])
                   + [(s_out, s_out.n)],
                   inc_k=16)
            if tg >= 2:
                ctx.w('t', s_v, v_b3 + (tg - 2) * 3 + 3)
            for jc in range(2):
                ins = None
                for kt in range(32):
                    ins = ctx.op('t', 'matmul', psh[:, 2 * par + jc, 0:512],
                                 abuf[:, kt, tg * 128:(tg + 1) * 128],
                                 w3v[:, kt, jc * 512:(jc + 1) * 512],
                                 start=(kt == 0), stop=(kt == 31))
                if ins is not None:
                    ins.then_inc(s_pe.h, 1)
                s_pe.n += 1
            ctx.w('v', s_pe, pe_b3 + 2 * (tg + 1))
            ctx.w('v', xgp)
            ctx.w('v', s_out)
            ctx.w('v', s_v)
            for jc in range(2):
                ctx.do('v', s_v, 'tensor_mul', out=b['tmpa'][:, jc * 512:(jc + 1) * 512],
                       in0=psh[:, 2 * par + jc, 0:512],
                       in1=b['s3bc'][:, jc * 512:(jc + 1) * 512])
            ctx.w('v', s_v)
            ctx.do('v', s_v, 'tensor_add', out=b['oatile'][:, par, :],
                   in0=b['tmpa'][:, :], in1=b['oatile'][:, par, :])
            ctx.do('g', s_out, 'dma_start',
                   out=b['out_h'][t0 + tg * 128:t0 + (tg + 1) * 128, :],
                   in_=b['oatile'][:, par, :], waits=[(s_v, s_v.n)], inc_k=16)
    ctx.w('g', s_out)
    ctx.w('sy', s_out)


def emit_h_layer(ctx, in_strips, stbuf, src, dst, scol, R, lastv=None):
    s_gd, s_v, s_a, s_pe = (ctx.sems[k] for k in ('g_dma', 'v', 'a', 'pe'))
    psh = ctx.b['ps']
    key = ctx.nc.next_id() if False else id(in_strips)
    for js in range(16):
        par = js % 2
        stp = ctx.sems['st0'] if par == 0 else ctx.sems['st1']
        w = [R[('hs', key, js - 2)]] if ('hs', key, js - 2) in R else [(s_pe, s_pe.n)]
        ctx.do('g', stp, 'dma_start', out=stbuf[:, par],
               in_=in_strips[js].rearrange("(a b) j -> b a j", b=128),
               waits=w, inc_k=16)
        ctx.w('t', stp)
        if js == 0:
            ctx.w('t', s_a)
            if lastv is not None:
                ctx.w('t', *lastv)
            else:
                ctx.w('t', s_v)
        for jt in range(2):
            for tc in range(2):
                gi = js * 4 + jt * 2 + tc
                bank = gi % 6
                if ('hg', key, gi - 6) in R:
                    ctx.w('t', *R[('hg', key, gi - 6)])
                ins = None
                for kt in range(32):
                    ins = ctx.op('t', 'matmul', psh[:, bank, 0:512],
                                 stbuf[:, par, kt, jt * 128:(jt + 1) * 128],
                                 src[:, kt, tc * 512:(tc + 1) * 512],
                                 start=(kt == 0), stop=(kt == 31))
                if ins is not None:
                    ins.then_inc(s_pe.h, 1)
                s_pe.n += 1
                jg = js * 2 + jt
                ctx.w('a', s_pe, s_pe.n)
                ctx.do('a', s_a, 'activation', out=dst[:, jg, tc * 512:(tc + 1) * 512],
                       in_=psh[:, bank, 0:512], func=AF.Gelu,
                       scale=scol[:, jg:jg + 1], waits=[(s_a, s_a.n)])
                R[('hg', key, gi)] = (s_a, s_a.n)
        R[('hs', key, js)] = (s_pe, s_pe.n)
    R[('hmm_last',)] = (s_pe, s_pe.n)


def build(dbg=None, dbg_rounds=12):
    nc = bass.Bass("TRN2", target_bir_lowering=False)
    din = {}
    D = nc.dram_tensor
    din['in_gsr'] = D("gsr", [C, C], F32, kind="ExternalInput").ap()
    din['in_xh'] = D("xh", [HT, C], F32, kind="ExternalInput").ap()
    din['in_idh'] = D("idh", [128, PT, HT], F32, kind="ExternalInput").ap()
    din['in_wupT'] = D("wupT", [C, C], F32, kind="ExternalInput").ap()
    din['in_wvT'] = D("wvT", [C, C], F32, kind="ExternalInput").ap()
    din['in_wprojT'] = D("wprojT", [C, C], F32, kind="ExternalInput").ap()
    din['in_identc'] = D("identc", [128, PT], F32, kind="ExternalInput").ap()
    din['in_eye'] = D("eye128", [128, 128], F32, kind="ExternalInput").ap()
    din['in_eyeb'] = D("eye128b", [128, 128], BF16, kind="ExternalInput").ap()
    din['in_onesc'] = D("onescol", [128, 1], F32, kind="ExternalInput").ap()
    din['in_onesb'] = D("onescolb", [128, 1], BF16, kind="ExternalInput").ap()
    din['in_onesr'] = D("onesrow", [1, 128], F32, kind="ExternalInput").ap()
    din['in_hoff'] = D("hoff", [1, 1], I32, kind="ExternalInput").ap()
    din['in_w1s'] = [D(f"w1s{j}", [C, 512], F32, kind="ExternalInput").ap() for j in range(8)]
    din['in_w2s'] = [D(f"w2s{j}", [C, 512], F32, kind="ExternalInput").ap() for j in range(8)]
    din['in_h1s'] = [D(f"h1s{j}", [DH, 256], F32, kind="ExternalInput").ap() for j in range(16)]
    din['in_h2s'] = [D(f"h2s{j}", [DH, 256], F32, kind="ExternalInput").ap() for j in range(16)]
    din['in_w3T'] = D("w3T", [DH, C], F32, kind="ExternalInput").ap()
    din['in_s1c'] = D("s1c_i", [128, 32], F32, kind="ExternalInput").ap()
    din['in_s2c'] = D("s2c_i", [128, 32], F32, kind="ExternalInput").ap()
    din['in_sh1c'] = D("sh1c_i", [128, 32], F32, kind="ExternalInput").ap()
    din['in_sh2c'] = D("sh2c_i", [128, 32], F32, kind="ExternalInput").ap()
    din['in_s3r'] = D("s3r_i", [1, C], F32, kind="ExternalInput").ap()
    out_h = D("out_h", [HT, C], F32, kind="ExternalOutput").ap()
    out_dbg = None
    if dbg in ('ns', 'q', 'scan'):
        out_dbg = D("dbgout", [C, 4096], F32, kind="ExternalOutput").ap()
    oasp = D("oasp", [HT, C], F32, kind="Internal").ap()
    hcin = [D(f"hcin{r}", [128, PT, 1 << r], BF16, kind="Internal").ap()
            for r in range(12)]
    hcout = [D(f"hcout{r}", [2, 128, PT, 1 << r], BF16, kind="Internal").ap()
             for r in range(12)]
    xnsp_h = D("xnsp", [128, PT, HT], BF16, kind="Internal")
    xnsp = xnsp_h.ap()

    st = {}

    def salloc(name, shape, dtype):
        st[name] = nc.alloc_sbuf_tensor(name, shape, dtype).ap()
        return st[name]

    identf = salloc("identf", [128, 128], F32)
    identb = salloc("identb", [128, 128], BF16)
    onesc = salloc("onesc", [128, 1], F32)
    onesb = salloc("onesb", [128, 1], BF16)
    onesr = salloc("onesr", [1, 128], F32)
    identcol = salloc("identcol", [128, PT], F32)
    lncols = salloc("lncols", [128, 12], F32)
    fcol = salloc("fcol", [128, 16], F32)
    sc_col = salloc("sc_col", [128, 2], F32)
    hoffs = salloc("hoffs", [1, 1], I32)
    s1c = salloc("s1c", [128, 32], F32)
    s2c = salloc("s2c", [128, 32], F32)
    sh1c = salloc("sh1c", [128, 32], F32)
    sh2c = salloc("sh2c", [128, 32], F32)
    nmx = salloc("nmx", [128, 16], F32)
    esr = salloc("esr", [128, 64], F32)
    eex = salloc("eex", [128, 64], F32)

    ssum = salloc("ssum", [128, 16], F32)
    rcs = salloc("rcs", [128, 16], F32)
    msb = salloc("msb", [128, 16], F32)
    sdv = salloc("sdv", [128, 16], F32)
    rs0 = salloc("rs0", [128, 16], F32)
    rsv = salloc("rsv", [128, 16], F32)
    cc = salloc("cc", [128, 4], F32)
    ones32b = salloc("ones32b", [128, 32], BF16)
    scs32 = salloc("scs32", [128, 512], F32)
    attb2 = salloc("attb2", [128, 2, 64], F32)
    junk256 = salloc("junk256", [128, 256], F32)

    arena_h = nc.alloc_sbuf_tensor("arena", [128, (nc.sbuf_bytes_remaining // 64) * 64 - 128], U8)
    ar = arena_h.ap()
    ARN = ar.shape[1]
    assert ARN >= 180 * KB, ARN

    def V2(off, dtype, d1, d2):
        esz = {F32: 4, F32R: 4, BF16: 2, I32: 4}[dtype]
        n = d1 * d2 * esz
        assert off % 64 == 0 and off + n <= ARN, (off, n, ARN)
        return ar[:, off:off + n].bitcast(dtype).rearrange("p (a b) -> p a b", b=d2)

    def V1(off, dtype, d1):
        esz = {F32: 4, F32R: 4, BF16: 2}[dtype]
        n = d1 * esz
        assert off % 64 == 0 and off + n <= ARN, (off, n, ARN)
        return ar[:, off:off + n].bitcast(dtype)

    KBy = 1024
    nsA = V2(0, F32R, PT, C)
    nsB = V2(32 * KBy, F32R, PT, C)
    nsC = V2(64 * KBy, F32R, PT, C)
    nsD = V2(96 * KBy, F32R, PT, C)
    nsE = V2(128 * KBy, F32R, PT, C)
    nsE32 = V2(128 * KBy, F32, PT, C)
    Wbf = V2(128 * KBy, BF16, PT, C)
    xscan = V2(0, F32, PT, HT)
    hz = V2(64 * KBy, BF16, PT, HT)
    xnfq = V2(96 * KBy, BF16, PT, 512)
    xt1 = V2(112 * KBy, F32, 2, 1024)
    xnt1 = V2(120 * KBy, BF16, 2, 1024)
    wupst = V2(124 * KBy, BF16, 2 * PT, 128).rearrange("p (a b) c -> p a b c", b=PT)
    xnful = V2(160 * KBy, BF16, PT, HT)
    xtl = V2(192 * KBy, F32, 1, 1024)
    xntl = V2(196 * KBy, BF16, 2, 1024)
    wupq = V2(96 * KBy, BF16, PT, C)
    sbb = V2(96 * KBy, BF16, PT, 512)
    vtok = V2(104 * KBy, BF16, 2 * 4, 1024).rearrange("p (a b) c -> p a b c", b=4)
    zTs = V2(120 * KBy, BF16, PT, 512)
    prodb = V2(144 * KBy, BF16, 8, 512)
    prodb2 = V2(166 * KBy, BF16, 8, 512)
    tmp8 = V2(152 * KBy, F32, PT, 256)
    zb1 = V1(160 * KBy, BF16, 1024)
    zb2 = V1(162 * KBy, BF16, 1024)
    zpg = V1(164 * KBy, BF16, 1024)
    vbf = V2(64 * KBy, BF16, PT, HT)
    ubf = V2(96 * KBy, BF16, PT, HT)
    xnh = V2(128 * KBy, BF16, PT, HT)
    wvbf = V2(160 * KBy, BF16, PT, C)
    wprojbf = V2(160 * KBy, BF16, PT, C)
    xt3 = V2(176 * KBy, F32, 2, 1024)
    xnt3 = V2(184 * KBy, BF16, 2, 1024)
    oat = V1(188 * KBy, F32, 1024)
    xtg = V2(192 * KBy, F32, 2, 1024)
    Wbf = V2(128 * KBy, BF16, PT, C)
    abuf = V2(0, BF16, 32, 1024)
    bbuf = V2(64 * KBy, BF16, 32, 1024)
    w3bf = V2(64 * KBy, BF16, 32, 1024)
    w1st = V2(128 * KBy, BF16, 2 * PT, 512).rearrange("p (a b) c -> p a b c", b=PT)
    w2st = V2(144 * KBy, BF16, 2 * PT, 512).rearrange("p (a b) c -> p a b c", b=PT)
    h1st = V2(128 * KBy, BF16, 2 * 32, 256).rearrange("p (a b) c -> p a b c", b=32)
    xnhsub = V2(160 * KBy, BF16, PT, 1024)
    s3bc = V1(176 * KBy, F32, 1024)
    tmpa = V1(180 * KBy, F32, 1024)
    s3r = V2(184 * KBy, F32, 1, 1024)[0:1, 0, :].rearrange("p a -> p a")
    oatile = V2(188 * KBy, F32, 2, 1024)

    with (
        nc.psum_tensor("ps", [128, PT, 512], F32) as ps,
        nc.semaphore("sy_dma") as h_sd,
        nc.semaphore("g_dma") as h_gd,
        nc.semaphore("pe") as h_pe,
        nc.semaphore("v") as h_v,
        nc.semaphore("a") as h_a,
        nc.semaphore("outd") as h_od,
        nc.semaphore("ln0") as h_l0,
        nc.semaphore("ln1") as h_l1,
        nc.semaphore("st0") as h_s0,
        nc.semaphore("st1") as h_s1,
        nc.semaphore("xg0") as h_x0,
        nc.semaphore("xg1") as h_x1,
        nc.semaphore("cc") as h_cc,
        nc.semaphore("hx") as h_hx,
        nc.Block() as block,
    ):
        sems = {'sy_dma': Sem(h_sd), 'g_dma': Sem(h_gd), 'pe': Sem(h_pe),
                'v': Sem(h_v), 'a': Sem(h_a), 'outd': Sem(h_od),
                'ln0': Sem(h_l0), 'ln1': Sem(h_l1), 'st0': Sem(h_s0),
                'st1': Sem(h_s1), 'xg0': Sem(h_x0), 'xg1': Sem(h_x1),
                'cc': Sem(h_cc), 'hx': Sem(h_hx)}
        bufs = dict(
            nsA=nsA, nsB=nsB, nsC=nsC, nsD=nsD, nsE=nsE, nsE32=nsE32, Wbf=Wbf,
            xscan=xscan, hz=hz, xnfq=xnfq, xt1=xt1, xnt1=xnt1, wupst=wupst,
            xnful=xnful, xtl=xtl, xntl=xntl, wupq=wupq,
            sbb=sbb, vtok=vtok, tmp8=tmp8, zTs=zTs, prodb=prodb, prodb2=prodb2,
            zb1=zb1, zb2=zb2, zpg=zpg,
            vbf=vbf, ubf=ubf, xnh=xnh, wvbf=wvbf, wprojbf=wprojbf,
            xt3=xt3, xnt3=xnt3, oat=oat, xtg=xtg,
            abuf=abuf, bbuf=bbuf, w3bf=w3bf, w1st=w1st, w2st=w2st, h1st=h1st,
            xnhsub=xnhsub, s3bc=s3bc, tmpa=tmpa, s3r=s3r, oatile=oatile,
            ps=ps.ap(), psb=ps.ap().bitcast(BF16), oasp=oasp, xnsp=xnsp,
            hcin=hcin, hcout=hcout, out_h=out_h,
            out_dbg=out_dbg, dbg_rounds=dbg_rounds,
            **st, **din,
        )

        def emit_all(ctx):
            s_dma = ctx.sems['sy_dma']
            for (dst, src) in ((identf[:, :], din['in_eye'][:, :]),
                               (identb[:, :], din['in_eyeb'][:, :]),
                               (onesc[:, :], din['in_onesc'][:, :]),
                               (onesb[:, :], din['in_onesb'][:, :]),
                               (onesr[:, :], din['in_onesr'][:, :]),
                               (identcol[:, :], din['in_identc'][:, :]),
                               (hoffs[:, :], din['in_hoff'][:, :])):
                ctx.do('sy', s_dma, 'dma_start', out=dst, in_=src, inc_k=16)
            s_v0 = ctx.sems['v']
            for (col, val) in ((0, 1e-5), (1, 1e-8), (2, 1.0 / 16.0), (3, 1.0 / 256.0)):
                ctx.do('v', s_v0, 'memset', st['cc'][:, col:col + 1], val)
            ctx.do('v', s_v0, 'memset', st['ones32b'][:, :], 1.0)
            for e in ('v', 'a', 't', 'g'):
                ctx.w(e, s_dma)
            for e in ('a', 't', 'g'):
                ctx.w(e, s_v0)
            emit_ns(ctx)
            if ctx.dbg == 'ns':
                return
            ctx.sync_all()
            emit_p1(ctx)
            if ctx.dbg == 'q':
                return
            ctx.sync_all()
            emit_scan(ctx)
            if ctx.dbg == 'scan':
                return
            ctx.sync_all()
            emit_p3(ctx)
            ctx.sync_all()
            emit_p4(ctx)

        engines = {'v': nc.vector, 'a': nc.scalar, 't': nc.tensor,
                   'sy': nc.sync, 'g': nc.gpsimd}

        def begin():
            for s in sems.values():
                s.n = 0

        @block.sync
        def _(sync):
            begin(); emit_all(Ctx(nc, 'sy', engines, sems, bufs, dbg))

        @block.vector
        def _(vector):
            begin(); emit_all(Ctx(nc, 'v', engines, sems, bufs, dbg))

        @block.scalar
        def _(scalar):
            begin(); emit_all(Ctx(nc, 'a', engines, sems, bufs, dbg))

        @block.tensor
        def _(tensor):
            begin(); emit_all(Ctx(nc, 't', engines, sems, bufs, dbg))

        @block.gpsimd
        def _(gpsimd):
            begin(); emit_all(Ctx(nc, 'g', engines, sems, bufs, dbg))

    return nc


def host_inputs(z, core):
    import ml_dtypes
    seq, hh = core // 2, core % 2
    A = np.ascontiguousarray
    x = np.asarray(z['L0_attn_x'])
    identc = np.asarray(z['L0_attn_identity']).reshape(PT, 128).T
    d = {
        'gsr': A(np.asarray(z['L0_attn_W_scan_raw'])),
        'idh': A(np.broadcast_to(identc[:, :, None], (128, PT, HT)).astype(np.float32)),
        'xh': A(x[seq, hh * HT:(hh + 1) * HT]),
        'wupT': A(np.asarray(z['L0_attn_W_up']).T),
        'wvT': A(np.asarray(z['L0_attn_W_v']).T),
        'wprojT': A(np.asarray(z['L0_attn_W_proj']).T),
        'identc': A(np.asarray(z['L0_attn_identity']).reshape(PT, 128).T),
        'eye128': np.eye(128, dtype=np.float32),
        'eye128b': np.eye(128, dtype=np.float32).astype(ml_dtypes.bfloat16),
        'onescol': np.ones((128, 1), np.float32),
        'onescolb': np.ones((128, 1), np.float32).astype(ml_dtypes.bfloat16),
        'onesrow': np.ones((1, 128), np.float32),
        'hoff': np.array([[hh * HT]], np.int32),
        'w3T': A(np.asarray(z['L0_ffn_w3_w']).T),
        's1c_i': A(np.asarray(z['L0_ffn_w1_s']).reshape(32, 128).T),
        's2c_i': A(np.asarray(z['L0_ffn_w2_s']).reshape(32, 128).T),
        'sh1c_i': A(np.asarray(z['L0_ffn_h1_s']).reshape(32, 128).T),
        'sh2c_i': A(np.asarray(z['L0_ffn_h2_s']).reshape(32, 128).T),
        's3r_i': A(np.asarray(z['L0_ffn_w3_s']).reshape(1, C)),
    }
    w1T = np.asarray(z['L0_ffn_w1_w']).T
    w2T = np.asarray(z['L0_ffn_w2_w']).T
    h1T = np.asarray(z['L0_ffn_h1_w']).T
    h2T = np.asarray(z['L0_ffn_h2_w']).T
    for j in range(8):
        d[f'w1s{j}'] = A(w1T[:, j * 512:(j + 1) * 512])
        d[f'w2s{j}'] = A(w2T[:, j * 512:(j + 1) * 512])
    for j in range(16):
        d[f'h1s{j}'] = A(h1T[:, j * 256:(j + 1) * 256])
        d[f'h2s{j}'] = A(h2T[:, j * 256:(j + 1) * 256])
    return d


def _patch_walrus_verifier():
    import concourse.bass_utils as _bu

    def bir_verify_and_optimise(tmpdir, inp="bir.json", outp="file.neff",
                                arch=None, *, dve_root=None):
        from pathlib import Path as _P
        cmd = [
            _bu.get_walrus_driver(),
            "--pass",
            ",".join([
                "runtime_memory_reservation",
                "lower_act",
                "lower_dve",
                "lower_ap_offset",
                "codegen",
                "neff_packager",
            ]),
            "-i", inp,
        ]
        if _bu.checkenv("CONCOURSE_WALRUS_GENERATE_MANIFEST"):
            cmd += ["-o", "walrus_bir.out.json"]
        cmd += [
            "--neff-output-filename", outp,
            "--enable-birsim=true",
            "--mem-mode=physical",
            "--policy=0",
            "--enable-ldw-opt=false",
            "--assign-static-dmas-to-sp=false",
            f"--dram-page-size={_bu.aot_getenv('NEURON_SCRATCHPAD_PAGE_SIZE', '256')}",
            f"--enable-neff-debug-info={'false' if _bu.aot_checkenv('CONCOURSE_SCRUB_NEFF_DEBUG_INFO') else 'true'}",
            "--jobs", "8",
            *_bu.get_walrus_args(
                _bu.get_bir_arch(tmpdir, inp) if arch is None else arch,
                tmpdir, dve_root=dve_root),
        ]
        result = _bu.run_command(cmd, cwd=tmpdir)
        if result is not None:
            (_P(tmpdir) / "log.txt").write_text(result.stdout)
        return f"{tmpdir}/{outp}"

    _bu.bir_verify_and_optimise = bir_verify_and_optimise


_patch_walrus_verifier()


_NC_CACHE = {}


def kernel(**inputs):
    if 'nc' not in _NC_CACHE:
        _NC_CACHE['nc'] = build(dbg=None)
    nc = _NC_CACHE['nc']
    in_maps = [host_inputs(inputs, core) for core in range(8)]
    res = run_bass_kernel_spmd(nc, in_maps, core_ids=list(range(8)))
    out = np.empty((B, T, C), np.float32)
    for core in range(8):
        seq, hh = core // 2, core % 2
        out[seq, hh * HT:(hh + 1) * HT] = np.asarray(res.results[core]['out_h'])
    return out


# version 16 (final)
# speedup vs baseline: 1.0893x; 1.0626x over previous
import numpy as np
try:
    import concourse.bass as bass
    import concourse.mybir as mybir
    from concourse.bass_utils import run_bass_kernel_spmd
except ImportError:
    import sys
    sys.path.insert(0, '/opt/trn_rl_repo')
    import concourse.bass as bass
    import concourse.mybir as mybir
    from concourse.bass_utils import run_bass_kernel_spmd

F32 = mybir.dt.float32
F32R = mybir.dt.float32r
BF16 = mybir.dt.bfloat16
U8 = mybir.dt.uint8
I32 = mybir.dt.int32
AF = mybir.ActivationFunctionType
ALU = mybir.AluOpType
AX = mybir.AxisListType

NS_SCHED = [
    (3.7111771581458646, -5.620608272237417, 2.1103471800461225),
    (3.329413233677866, -4.062595212811175, 1.2393093931296458),
    (2.9674567262212846, -2.86852510148617, 0.6778708762095051),
    (3.2380560803830135, -3.7372616312341555, 1.0783592811783235),
    (2.981962845278586, -2.911994792086269, 0.6975370488974846),
    (2.8932135288128827, -2.651149828439036, 0.5800612344955369),
    (3.033845782358891, -3.070206636005876, 0.7693889436987285),
    (3.0801300104202463, -3.2150378112771756, 0.8355372166767617),
    (3.1495997998475583, -3.439147089661531, 0.9385570822404736),
    (2.9131136983260744, -2.7085892242404913, 0.6058194688704685),
    (2.9479683094892326, -2.810645048711703, 0.6517416272859141),
    (2.7769556328409347, -2.3271824654484655, 0.43602047708272),
    (3.0105414639697745, -2.998607791517297, 0.7368168363836698),
    (3.4322970197419605, -4.451306469619516, 1.4440793216272883),
    (2.9562048202915693, -2.835034724402922, 0.6627439788262405),
    (3.0908765846627465, -4.007390087857139, 1.3222944767008617),
    (3.151104560640553, -4.25921535736201, 1.499665349660536),
    (2.7207321333327457, -2.7599487977921504, 0.7652923993101875),
    (1.875, -1.25, 0.375),
    (1.875, -1.25, 0.375),
    (1.875, -1.25, 0.375),
    (1.875, -1.25, 0.375),
]

B, T, C, DH = 4, 4096, 1024, 4096
PT = C // 128
HT = 2048
KB = 1024
RGROUPS = [[0, 1], [2, 3], [4, 5], [6, 7]]


class Sem:
    def __init__(self, h):
        self.h = h
        self.n = 0


class Ctx:
    def __init__(self, nc, eng_name, engines, sems, bufs, dbg):
        self.nc = nc
        self.eng = eng_name
        self.E = engines
        self.sems = sems
        self.b = bufs
        self.dbg = dbg

    def w(self, name, sem, val=None):
        if self.eng == name:
            self.E[name].wait_ge(sem.h, sem.n if val is None else val)

    def op(self, name, meth, *a, **kw):
        if self.eng == name:
            return getattr(self.E[name], meth)(*a, **kw)
        return None

    def do(self, name, sem, meth, *a, waits=(), inc_k=1, **kw):
        for (s, v) in waits:
            self.w(name, s, v)
        ins = self.op(name, meth, *a, **kw)
        if ins is not None:
            ins.then_inc(sem.h, inc_k)
        sem.n += inc_k
        return ins

    def sync_all(self, engs=('v', 'a', 't', 'sy', 'g')):
        for e in engs:
            for s in self.sems.values():
                self.w(e, s)


def emit_ns(ctx):
    b = ctx.b
    s_dma, s_pe, s_v, s_a = (ctx.sems[k] for k in ('sy_dma', 'pe', 'v', 'a'))
    X, XT, M, M2, S = b['nsA'], b['nsB'], b['nsC'], b['nsD'], b['nsE']
    psh = b['ps']
    fcol, sc_col = b['fcol'], b['sc_col']
    onesc, onesr, identf = b['onesc'], b['onesr'], b['identf']
    S32 = b['nsE32']

    s_gd0 = ctx.sems['g_dma']
    for ci in range(PT):
        ctx.do('g', s_gd0, 'dma_start', out=S[:, ci, :],
               in_=b['in_gsr'][ci * 128:(ci + 1) * 128, :], inc_k=16)

    for ci in range(PT):
        ctx.do('a', s_a, 'activation', out=psh[:, 6, 0:512], in_=S32[:, ci, 0:512],
               func=AF.Square, accum_out=fcol[:, ci:ci + 1],
               waits=[(s_gd0, None), (s_a, None)])
        ctx.do('a', s_a, 'activation', out=psh[:, 7, 0:512], in_=S32[:, ci, 512:1024],
               func=AF.Square, accum_out=fcol[:, 8 + ci:9 + ci],
               waits=[(s_a, None)])
    ctx.do('v', s_v, 'reduce_sum', out=sc_col[:, 1:2], in_=fcol[:, 0:16], axis=AX.X,
           waits=[(s_a, None)])
    ctx.do('t', s_pe, 'matmul', psh[0:1, 0, 0:1], onesc[:, 0:1], sc_col[:, 1:2],
           start=True, stop=True, waits=[(s_v, None)])
    ctx.do('a', s_a, 'activation', out=fcol[0:1, 0:1], in_=psh[0:1, 0, 0:1],
           func=AF.Sqrt, waits=[(s_pe, None), (s_a, None)])
    ctx.do('v', s_v, 'reciprocal', out=fcol[0:1, 1:2], in_=fcol[0:1, 0:1],
           waits=[(s_a, None), (s_v, None)])
    ctx.do('t', s_pe, 'matmul', psh[:, 0, 1:2], onesr[0:1, :], fcol[0:1, 1:2],
           start=True, stop=True, waits=[(s_v, None)])
    ctx.do('a', s_a, 'copy', out=sc_col[:, 0:1], in_=psh[:, 0, 1:2],
           waits=[(s_pe, None), (s_a, None)])

    for ci in range(PT):
        ctx.do('v', s_v, 'tensor_scalar_mul', out=X[:, ci, :], in0=S32[:, ci, :],
               scalar1=sc_col[:, 0:1], waits=[(s_a, None)] if ci == 0 else ())
    ctx.w('t', s_a)
    v_base, pe_base, ti = s_v.n, s_pe.n, 0
    for cj in range(PT):
        for ci in range(PT):
            bank = ti % 2
            ctx.do('t', s_pe, 'transpose',
                   psh[:, bank, 0:128], S32[:, ci, cj * 128:(cj + 1) * 128], identf[:, :],
                   waits=([(s_v, v_base + ti - 1)] if ti >= 2 else []))
            ctx.do('v', s_v, 'tensor_scalar_mul',
                   out=XT[:, cj, ci * 128:(ci + 1) * 128],
                   in0=psh[:, bank, 0:128], scalar1=sc_col[:, 0:1],
                   waits=[(s_pe, pe_base + ti + 1)])
            ti += 1

    lns = (ctx.sems['ln0'], ctx.sems['ln1'])
    lnc = b['lncols']
    psb = b['psb']
    xtv = b['xtl'][:, 0, :]
    xntl, xnful = b['xntl'], b['xnful']
    RL = {}
    lnst = {'tt': 0}

    def ln_tile():
        tt = lnst['tt']
        lnst['tt'] += 1
        par = tt % 2
        sp = lns[par]
        w = []
        if tt >= 1:
            w += [RL[('lnv', tt - 1)], RL[('lnaq', tt - 1)]]
        ctx.do('sy', sp, 'dma_start', out=xtv,
               in_=b['in_xh'][tt * 128:(tt + 1) * 128, :], waits=w, inc_k=16)
        ctx.do('a', s_a, 'activation', out=b['scs32'][:, :], in_=xtv[:, 0:512],
               func=AF.Square, accum_out=lnc[:, 0:1],
               waits=[(sp, None), (s_a, None)])
        ctx.do('a', s_a, 'activation', out=b['scs32'][:, :], in_=xtv[:, 512:1024],
               func=AF.Square, accum_out=lnc[:, 8:9], waits=[(s_a, None)])
        RL[('lnaq', tt)] = (s_a, s_a.n)
        ctx.do('v', s_v, 'reduce_sum', out=lnc[:, 2:3], in_=xtv, axis=AX.X,
               waits=[(sp, None), (s_v, None)])
        ctx.do('v', s_v, 'tensor_scalar_mul', out=lnc[:, 3:4], in0=lnc[:, 2:3],
               scalar1=-1.0 / 1024.0, waits=[(s_v, None)])
        ctx.do('v', s_v, 'tensor_mul', out=lnc[:, 4:5], in0=lnc[:, 3:4],
               in1=lnc[:, 3:4], waits=[(s_v, None)])
        ctx.do('v', s_v, 'tensor_add', out=lnc[:, 9:10], in0=lnc[:, 0:1],
               in1=lnc[:, 8:9], waits=[RL[('lnaq', tt)], (s_v, None)])
        ctx.do('v', s_v, 'scalar_tensor_tensor', out=lnc[:, 5:6], in0=lnc[:, 9:10],
               scalar=1.0 / 1024.0, in1=lnc[:, 4:5], op0=ALU.mult,
               op1=ALU.subtract, waits=[(s_v, None)])
        ctx.do('a', s_a, 'activation', out=lnc[:, 6:7], in_=lnc[:, 5:6],
               func=AF.Sqrt, bias=b['cc'][:, 0:1], waits=[(s_v, None), (s_a, None)])
        ctx.do('v', s_v, 'reciprocal', out=lnc[:, 7:8], in_=lnc[:, 6:7],
               waits=[(s_a, None), (s_v, None)])
        xw = [(s_v, None)]
        if tt >= 2:
            xw.append(RL[('lntr', tt - 2)])
        ctx.do('v', s_v, 'tensor_scalar', out=xntl[:, par, :], in0=xtv,
               scalar1=lnc[:, 3:4], scalar2=lnc[:, 7:8], op0=ALU.add,
               op1=ALU.mult, waits=xw)
        RL[('lnv', tt)] = (s_v, s_v.n)
        ctx.w('t', *RL[('lnv', tt)])
        for ci in range(PT):
            bank = 6 + ci % 2
            tw = []
            if ci >= 2 and ('lne', tt, ci - 2) in RL:
                tw.append(RL[('lne', tt, ci - 2)])
            elif ci < 2 and tt >= 1:
                tw.append(RL[('lne', tt - 1, ci + 6)])
            ctx.do('t', s_pe, 'transpose', psb[:, bank, 0:128],
                   xntl[:, par, ci * 128:(ci + 1) * 128], b['identb'][:, :],
                   waits=tw)
            ctx.do('a', s_a, 'copy',
                   out=xnful[:, ci, tt * 128:(tt + 1) * 128],
                   in_=psb[:, bank, 0:128],
                   waits=[(s_pe, None), (s_a, None)])
            RL[('lne', tt, ci)] = (s_a, s_a.n)
        RL[('lntr', tt)] = (s_pe, s_pe.n)

    def xn_transpose(src_b, dst, evrec):
        src32 = src_b.bitcast(F32)
        pe_b, a_b = s_pe.n, s_a.n
        ti = 0
        for cj in range(PT):
            ctx.w('t', *evrec[cj])
            for ci in range(PT):
                bank = ti % 2
                tw = [(s_a, a_b + ti - 1)] if ti >= 2 else []
                ctx.do('t', s_pe, 'transpose', psh[:, bank, 0:128],
                       src32[:, cj, ci * 128:(ci + 1) * 128], identf[:, :],
                       waits=tw)
                ctx.do('a', s_a, 'copy',
                       out=dst[:, ci, cj * 128:(cj + 1) * 128],
                       in_=psh[:, bank, 0:128],
                       waits=[(s_pe, pe_b + ti + 1), (s_a, None)])
                ti += 1

    bufs = [X, XT, M, M2, S]
    for itn, (ca, cb, cc) in enumerate(NS_SCHED):
        Xc, XTc, Mc, M2c, Sc = bufs
        ns_mm_sym(ctx, out=Mc, lhsT=Xc, rhs=Xc, psh=psh,
                  twaits=[(s_v, None)])
        ns_mm_sym(ctx, out=M2c, lhsT=Mc, rhs=Mc, psh=psh)
        ctx.w('v', s_a)
        ctx.w('v', s_v)
        for ci in range(PT):
            ctx.do('v', s_v, 'tensor_scalar_mul', out=Sc[:, ci, :], in0=Mc[:, ci, :],
                   scalar1=float(cb))
        ctx.w('v', s_v)
        for ci in range(PT):
            ctx.do('v', s_v, 'scalar_tensor_tensor', out=Mc[:, ci, :], in0=M2c[:, ci, :],
                   scalar=float(cc), in1=Sc[:, ci, :], op0=ALU.mult, op1=ALU.add)
        evrec = ns_mm(ctx, out=Sc, lhsT=Mc, rhs=XTc, evac='axpy', psh=psh,
                      aval=float(ca), xbuf=XTc,
                      twaits=[(s_v, None)], evwaits=[(s_v, None)])
        xn_transpose(Sc, M2c, evrec)
        if itn < 16:
            ln_tile()
        bufs = [M2c, Sc, Xc, XTc, Mc]

    Xf = bufs[0]
    ctx.w('v', s_v)
    ctx.w('v', s_pe)
    ctx.w('v', s_a)
    for ci in range(PT):
        ctx.do('v', s_v, 'tensor_copy', out=b['Wbf'][:, ci, :], in_=Xf[:, ci, :])
    vwb = s_v.n
    if ctx.dbg == 'ns':
        sd = ctx.sems['g_dma']
        ctx.w('g', ctx.sems['v'])
        for ci in range(PT):
            ctx.do('g', sd, 'dma_start', out=b['out_dbg'][ci * 128:(ci + 1) * 128, 0:C],
                   in_=Xf[:, ci, :].bitcast(F32), inc_k=16)
        ctx.w('g', sd)
        return

    s_gd0b = ctx.sems['g_dma']
    ctx.w('g', s_v)
    ctx.w('g', s_pe)
    ctx.w('g', s_a)
    ctx.do('g', s_gd0b, 'dma_start', out=b['wupq'][:, :, :],
           in_=b['in_wupT'].rearrange("(a b) j -> b a j", b=128), inc_k=16)
    Xq = b['xscan']
    pe_b, a_b = s_pe.n, s_a.n
    gi = 0
    for jt in range(PT):
        for tc in range(4):
            bank = 2 + (gi % 4)
            if gi == 0:
                ctx.w('t', s_gd0b)
                ctx.w('t', s_a)
                ctx.w('t', s_v)
            if gi >= 4:
                ctx.w('t', s_a, a_b + (gi - 4) + 1)
            ins = None
            for kt in range(PT):
                ins = ctx.op('t', 'matmul', psh[:, bank, 0:512],
                             b['wupq'][:, kt, jt * 128:(jt + 1) * 128],
                             b['xnful'][:, kt, tc * 512:(tc + 1) * 512],
                             start=(kt == 0), stop=(kt == PT - 1))
            if ins is not None:
                ins.then_inc(s_pe.h, 1)
            s_pe.n += 1
            ctx.do('a', s_a, 'copy', out=Xq[:, jt, tc * 512:(tc + 1) * 512],
                   in_=psh[:, bank, 0:512],
                   waits=[(s_pe, pe_b + gi + 1), (s_a, None)] +
                         ([(s_v, vwb)] if gi == 0 else []))
            gi += 1


def ns_mm(ctx, out, lhsT, rhs, evac, psh, aval=None, xbuf=None, twaits=(), evwaits=()):
    s_pe, s_v, s_a = (ctx.sems[k] for k in ('pe', 'v', 'a'))
    s_ev = s_a if evac == 'copy' else s_v
    ev_eng = 'a' if evac == 'copy' else 'v'
    for (s, v) in twaits:
        ctx.w('t', s, v)
    ctx.w('t', s_a)
    ctx.w('t', s_v)
    for (s, v) in evwaits:
        ctx.w(ev_eng, s, v)
    pe_base, ev_base = s_pe.n, s_ev.n
    evrec = []
    gi = 0
    for it in range(PT):
        for jc in range(2):
            bank = 2 + (gi % 4)
            if gi >= 4:
                ctx.w('t', s_ev, ev_base + (gi - 4) + 1)
            ins = None
            for kt in range(PT):
                ins = ctx.op('t', 'matmul', psh[:, bank, 0:512],
                             lhsT[:, kt, it * 128:(it + 1) * 128],
                             rhs[:, kt, jc * 512:(jc + 1) * 512],
                             start=(kt == 0), stop=(kt == PT - 1))
            if ins is not None:
                ins.then_inc(s_pe.h, 1)
            s_pe.n += 1
            ctx.w(ev_eng, s_pe, pe_base + gi + 1)
            if evac == 'copy':
                ctx.do('a', s_a, 'copy', out=out[:, it, jc * 512:(jc + 1) * 512],
                       in_=psh[:, bank, 0:512], waits=[(s_a, None)])
            else:
                ctx.do('v', s_v, 'scalar_tensor_tensor',
                       out=out[:, it, jc * 512:(jc + 1) * 512],
                       in0=xbuf[:, it, jc * 512:(jc + 1) * 512],
                       scalar=aval, in1=psh[:, bank, 0:512],
                       op0=ALU.mult, op1=ALU.add)
            gi += 1
        evrec.append((s_ev, s_ev.n))
    return evrec


def ns_mm_sym(ctx, out, lhsT, rhs, psh, twaits=()):
    s_pe, s_v, s_a = (ctx.sems[k] for k in ('pe', 'v', 'a'))
    for (s, v) in twaits:
        ctx.w('t', s, v)
    ctx.w('t', s_a)
    ctx.w('t', s_v)
    pe_base, a_base = s_pe.n, s_a.n
    erec = {}
    gi = 0
    for it in range(PT):
        c0 = it * 128
        pieces = []
        if c0 < 512:
            pieces.append((c0, 512))
            pieces.append((512, 1024))
        else:
            pieces.append((c0, 1024))
        for (lo, hi) in pieces:
            bank = 2 + (gi % 4)
            if gi >= 4:
                ctx.w('t', s_a, a_base + (gi - 4) + 1)
            ins = None
            for kt in range(PT):
                ins = ctx.op('t', 'matmul', psh[:, bank, 0:hi - lo],
                             lhsT[:, kt, it * 128:(it + 1) * 128],
                             rhs[:, kt, lo:hi],
                             start=(kt == 0), stop=(kt == PT - 1))
            if ins is not None:
                ins.then_inc(s_pe.h, 1)
            s_pe.n += 1
            ctx.do('a', s_a, 'copy', out=out[:, it, lo:hi],
                   in_=psh[:, bank, 0:hi - lo],
                   waits=[(s_pe, s_pe.n), (s_a, None)])
            gi += 1
        erec[it] = (s_a, s_a.n)
    out32 = out.bitcast(F32)
    identf = ctx.b['identf']
    mp_base, ma_base = s_pe.n, s_a.n
    ti = 0
    for it in range(PT):
        for jb in range(it + 1, PT):
            bank = ti % 2
            tw = [erec[it]] if jb == it + 1 else []
            if ti >= 2:
                tw.append((s_a, ma_base + ti - 1))
            ctx.do('t', s_pe, 'transpose', psh[:, bank, 0:128],
                   out32[:, it, jb * 128:(jb + 1) * 128], identf[:, :],
                   waits=tw)
            ctx.do('a', s_a, 'copy',
                   out=out[:, jb, it * 128:(it + 1) * 128],
                   in_=psh[:, bank, 0:128],
                   waits=[(s_pe, mp_base + ti + 1), (s_a, None)])
            ti += 1


def emit_ln(ctx, src_hbm, src0, ntiles, xn_dst, dst_off, xt, xntok):
    b = ctx.b
    s_v, s_a, s_pe = (ctx.sems[k] for k in ('v', 'a', 'pe'))
    lns = (ctx.sems['ln0'], ctx.sems['ln1'])
    lnc = b['lncols']
    psh = b['ps']
    v_base, a_base = s_v.n, s_a.n
    for tt in range(ntiles):
        par = tt % 2
        sp = lns[par]
        ctx.do('sy', sp, 'dma_start', out=xt[:, par, :],
               in_=src_hbm[src0 + tt * 128:src0 + (tt + 1) * 128, :],
               waits=[(s_v, v_base + tt * 7)] if tt >= 2 else [(s_v, v_base)],
               inc_k=16)
        sq = lnc[:, par:par + 1]
        ctx.do('a', s_a, 'activation', out=psh[:, 6, 0:512], in_=xt[:, par, 0:512],
               func=AF.Square, accum_out=sq,
               waits=[(sp, None), (s_a, None)])
        ctx.do('a', s_a, 'activation', out=psh[:, 7, 0:512], in_=xt[:, par, 512:1024],
               func=AF.Square, accum_out=lnc[:, 8:9], waits=[(s_a, None)])
        ctx.do('v', s_v, 'reduce_sum', out=lnc[:, 2:3], in_=xt[:, par, :], axis=AX.X,
               waits=[(sp, None), (s_v, None)])
        ctx.do('v', s_v, 'tensor_scalar_mul', out=lnc[:, 3:4], in0=lnc[:, 2:3],
               scalar1=-1.0 / 1024.0, waits=[(s_v, None)])
        ctx.do('v', s_v, 'tensor_mul', out=lnc[:, 4:5], in0=lnc[:, 3:4], in1=lnc[:, 3:4],
               waits=[(s_v, None)])
        ctx.do('v', s_v, 'tensor_add', out=lnc[:, 9:10], in0=sq, in1=lnc[:, 8:9],
               waits=[(s_a, None), (s_v, None)])
        ctx.do('v', s_v, 'scalar_tensor_tensor', out=lnc[:, 5:6], in0=lnc[:, 9:10],
               scalar=1.0 / 1024.0, in1=lnc[:, 4:5], op0=ALU.mult, op1=ALU.subtract,
               waits=[(s_v, None)])
        ctx.do('a', s_a, 'activation', out=lnc[:, 6:7], in_=lnc[:, 5:6], func=AF.Sqrt,
               bias=b['cc'][:, 0:1], waits=[(s_v, None), (s_a, None)])
        ctx.do('v', s_v, 'reciprocal', out=lnc[:, 7:8], in_=lnc[:, 6:7],
               waits=[(s_a, None), (s_v, None)])
        ctx.do('v', s_v, 'tensor_scalar', out=xntok[:, par, :], in0=xt[:, par, :],
               scalar1=lnc[:, 3:4], scalar2=lnc[:, 7:8], op0=ALU.add, op1=ALU.mult,
               waits=[(s_v, None)])
        for ci in range(PT):
            bank = ci % 2
            tw = [(s_v, None)]
            if tt + ci > 0:
                tw.append((s_a, None))
            ctx.do('t', s_pe, 'transpose', b['psb'][:, bank, 0:128],
                   xntok[:, par, ci * 128:(ci + 1) * 128], b['identb'][:, :],
                   waits=tw)
            ctx.do('a', s_a, 'copy',
                   out=xn_dst[:, ci, dst_off + tt * 128:dst_off + (tt + 1) * 128],
                   in_=b['psb'][:, bank, 0:128], waits=[(s_pe, None), (s_a, None)])


def emit_p1(ctx):
    b = ctx.b
    s_gd = ctx.sems['g_dma']
    ctx.w('g', s_gd)
    for ci in range(PT):
        ctx.do('g', s_gd, 'dma_start', out=b['hz'][:, ci, :],
               in_=b['in_idh'][:, ci, :], inc_k=16)
    if ctx.dbg == 'q':
        X = b['xscan']
        ctx.w('g', ctx.sems['a'])
        for ci in range(PT):
            ctx.do('g', s_gd, 'dma_start', out=b['out_dbg'][ci * 128:(ci + 1) * 128, 0:HT],
                   in_=X[:, ci, 0:HT], inc_k=16)
        ctx.w('g', s_gd)


def identf_full(ctx):
    return ctx.b['identf'][:, :]


def emit_scan(ctx):
    b = ctx.b
    s2v, s2a, s2p = (ctx.sems[k] for k in ('v', 'a', 'pe'))
    s_gd, s_cc, s_hx = (ctx.sems[k] for k in ('g_dma', 'cc', 'hx'))
    psh, psb = b['ps'], b['psb']
    X, HZ, Wb = b['xscan'], b['hz'], b['Wbf']
    sbb, prodb, vtok = b['sbb'], b['prodb'], b['vtok']
    prodb2 = b['prodb2']
    nmx, esr, eex, attb2 = b['nmx'], b['esr'], b['eex'], b['attb2']
    ssum, rcs = b['ssum'], b['rcs']
    zb1, zb2, zpg = b['zb1'], b['zb2'], b['zpg']
    msb, sdv, rs0, rsv = b['msb'], b['sdv'], b['rs0'], b['rsv']
    zTs, tmp8, junk256 = b['zTs'], b['tmp8'], b['junk256']
    ones32b, scs32 = b['ones32b'], b['scs32']
    nrounds = b.get('dbg_rounds', 12)
    NCH = 4
    R = {}
    hcond = [None]

    def get_cond(G):
        if hcond[0] is None:
            hcond[0] = G.compute_val(G.partition_id() % 2)
        return hcond[0]

    def seqi(rc):
        return rc[0] * NCH + (NCH - 1 - rc[1])

    def chunk_at(idx):
        if idx < 0 or idx >= nrounds * NCH:
            return None
        rr, cc2 = divmod(idx, NCH)
        return (rr, NCH - 1 - cc2)

    def geom(rc):
        r, c = rc
        d = 1 << r
        c0 = c * 512
        bd = min(max(d - c0, 0), 512)
        return d, c % 2, c0, bd

    def lslice(p, rc, lo, hi):
        d, par, c0, bd = geom(rc)
        if hi <= bd:
            h0 = HT + c0 - d
            return HZ[:, p, h0 + lo:h0 + hi], True
        return X[:, p, c0 + lo - d:c0 + hi - d], False

    def cond_land(r, lo, hi):
        d = 1 << r
        if ctx.eng == 'g':
            G = ctx.E['g']
            cond = get_cond(G)
            kw = {'allow_slow_non_contiguous': True} if hi - lo < 2 else {}
            ins = G.dma_start(out=HZ[:, :, HT - d + lo:HT - d + hi],
                              in_=b['hcout'][r][0, :, :, lo:hi], cond=cond, **kw)
            ins.then_inc(s_hx.h, 16)
        s_hx.n += 16

    def exchange_head(r, full):
        d = 1 << r
        ctx.w('g', s2v)
        ctx.w('g', s2a)
        ctx.w('g', s_hx)
        ctx.w('g', s_gd)
        kw = {'allow_slow_non_contiguous': True} if d < 2 else {}
        ctx.do('g', s_gd, 'dma_start', out=b['hcin'][r][:, :, :],
               in_=X[:, :, HT - d:HT], inc_k=16, **kw)
        R[('xtail', r)] = (s_gd, s_gd.n)
        ctx.w('g', s_gd)
        ctx.do('g', s_cc, 'collective_compute', "AllGather", ALU.bypass,
               ins=[b['hcin'][r][:, :, :]], outs=[b['hcout'][r][:, :, :, :]],
               replica_groups=RGROUPS)
        ctx.w('g', s_cc)
        split = 0 if full else d - (d >> 1)
        cond_land(r, 0, d - split)
        if full:
            R[('hz', r)] = (s_hx, s_hx.n)

    def exchange_tail(r):
        d = 1 << r
        ctx.w('g', s2v)
        cond_land(r, d - (d >> 1), d)
        R[('hz', r)] = (s_hx, s_hx.n)

    def front_s(rc):
        d, par, c0, bd = geom(rc)
        w = [(s2v, s2v.n)]
        pk = chunk_at(seqi(rc) - 1)
        if pk and ('vtokmm', pk) in R:
            w.append(R[('vtokmm', pk)])
        if bd > 0 and ('hz', rc[0]) in R:
            w.append(R[('hz', rc[0])])
        first = True
        for p in range(PT):
            if bd > 0:
                la, _ = lslice(p, rc, 0, bd)
                ctx.do('v', s2v, 'tensor_add',
                       out=sbb[:, p, 0:bd], in0=la, in1=X[:, p, c0:c0 + bd],
                       waits=w if first else ())
                first = False
            if bd < 512:
                la, _ = lslice(p, rc, bd, 512)
                ctx.do('v', s2v, 'tensor_add',
                       out=sbb[:, p, bd:512], in0=la,
                       in1=X[:, p, c0 + bd:c0 + 512],
                       waits=w if first else ())
                first = False
        R[('s', rc)] = (s2v, s2v.n)

    def front_vtok(rc):
        d, par, c0, bd = geom(rc)
        ctx.w('t', *R[('s', rc)])
        pk = chunk_at(seqi(rc) - 1)
        if pk and ('vtokevac', pk) in R:
            ctx.w('t', *R[('vtokevac', pk)])
        pk2 = chunk_at(seqi(rc) - 2)
        aw = []
        if pk2 and ('z', pk2) in R:
            aw.append(R[('z', pk2)])
        for g in range(4):
            gb0 = 2 * (g % 2)
            if g >= 2:
                ctx.w('t', *R[('vtokevacg', rc, g - 2)])
            for jc in range(2):
                ins = None
                for kt in range(PT):
                    ins = ctx.op('t', 'matmul', psh[:, gb0 + jc, 0:512],
                                 sbb[:, kt, g * 128:(g + 1) * 128],
                                 Wb[:, kt, jc * 512:(jc + 1) * 512],
                                 start=(kt == 0), stop=(kt == PT - 1))
                if ins is not None:
                    ins.then_inc(s2p.h, 1)
                s2p.n += 1
            R[('vtokg', rc, g)] = (s2p, s2p.n)
            ctx.w('a', *R[('vtokg', rc, g)])
            for jc in range(2):
                ctx.do('a', s2a, 'copy',
                       out=vtok[:, par, g, jc * 512:(jc + 1) * 512],
                       in_=psh[:, gb0 + jc, 0:512],
                       waits=(aw + [(s2a, s2a.n)]) if (g == 0 and jc == 0)
                       else [(s2a, s2a.n)])
            R[('vtokevacg', rc, g)] = (s2a, s2a.n)
        R[('vtokmm', rc)] = (s2p, s2p.n)
        R[('vtokevac', rc)] = (s2a, s2a.n)

    def front_prod(rc, bi):
        d, par, c0, bd = geom(rc)
        gbi = seqi(rc) * 4 + bi
        pb = (prodb, prodb2)[gbi % 2]
        l = bi
        w = []
        if ('scoremm', gbi - 2) in R:
            w.append(R[('scoremm', gbi - 2)])
        first = True
        for m in range(4):
            for h in range(2):
                tix = m * 2 + h
                qp, kp = 2 * l + h, 2 * m + h
                if bd > 0:
                    la, _ = lslice(qp, rc, 0, bd)
                    ctx.do('v', s2v, 'tensor_mul',
                           out=pb[:, tix, 0:bd], in0=la,
                           in1=X[:, kp, c0:c0 + bd],
                           waits=w if first else ())
                    first = False
                if bd < 512:
                    la, _ = lslice(qp, rc, bd, 512)
                    ctx.do('v', s2v, 'tensor_mul',
                           out=pb[:, tix, bd:512], in0=la,
                           in1=X[:, kp, c0 + bd:c0 + 512],
                           waits=w if first else ())
                    first = False
        R[('prodv', gbi)] = (s2v, s2v.n)

    def front_score(rc, bi):
        gbi = seqi(rc) * 4 + bi
        pb = (prodb, prodb2)[gbi % 2]
        ctx.w('t', *R[('prodv', gbi)])
        if ('scsevac', gbi - 1) in R:
            ctx.w('t', *R[('scsevac', gbi - 1)])
        for m in range(4):
            base = 32 * m
            ins = None
            for h in range(2):
                tix = m * 2 + h
                kw = {'tile_position': (0, 96)} if base == 96 else {}
                ins = ctx.op('t', 'matmul', psh[base:base + 32, 4, 0:512],
                             ones32b[:, :], pb[:, tix, 0:512],
                             start=(h == 0), stop=(h == 1), **kw)
            if ins is not None:
                ins.then_inc(s2p.h, 1)
            s2p.n += 1
        R[('scoremm', gbi)] = (s2p, s2p.n)
        ctx.w('a', *R[('scoremm', gbi)])
        if ('attT', gbi - 1) in R:
            ctx.w('a', *R[('attT', gbi - 1)])
        ctx.do('a', s2a, 'copy', out=scs32[:, :], in_=psh[:, 4, 0:512],
               waits=[(s2a, s2a.n)])
        R[('scsevac', gbi)] = (s2a, s2a.n)
        ctx.w('t', *R[('scsevac', gbi)])
        if ('smaxcp', gbi - 1) in R:
            ctx.w('t', *R[('smaxcp', gbi - 1)])
        for g in range(4):
            ctx.do('t', s2p, 'transpose', psh[:, 5, g * 128:(g + 1) * 128],
                   scs32[:, g * 128:(g + 1) * 128], b['identf'][:, :])
        R[('attT', gbi)] = (s2p, s2p.n)

    def front_smax(rc, bi):
        gbi = seqi(rc) * 4 + bi
        l = bi
        ps5v = psh[:, 5, 0:512].rearrange("p (g m rep) -> p g m rep",
                                          m=4, rep=32)[:, :, :, 0]
        ctx.do('v', s2v, 'tensor_copy', out=esr[:, l * 16:(l + 1) * 16],
               in_=ps5v, waits=[R[('attT', gbi)], (s2v, s2v.n)])
        R[('smaxcp', gbi)] = (s2v, s2v.n)

    def front_smax_batch(rc):
        d, par, c0, bd = geom(rc)
        e_all = esr[:, 0:64].rearrange("p (l g m) -> p l g m", g=4, m=4)
        xall = eex[:, 0:64].rearrange("p (l g m) -> p l g m", g=4, m=4)
        nm3 = nmx[:, 0:16].rearrange("p (l g) -> p l g", g=4)
        ctx.do('v', s2v, 'reduce_max', out=nm3, in_=e_all, axis=AX.X,
               negate=True, waits=[(s2v, s2v.n)])
        for m in range(4):
            ctx.do('v', s2v, 'tensor_add', out=xall[:, :, :, m],
                   in0=e_all[:, :, :, m], in1=nm3,
                   waits=[(s2v, s2v.n)] if m == 0 else ())
        eo = junk256[:, 64:128]
        ctx.do('a', s2a, 'activation', out=eo, in_=eex[:, 0:64],
               func=AF.Exp, scale=b['cc'][:, 2:3],
               waits=[(s2v, s2v.n), (s2a, s2a.n)])
        eo4 = eo.rearrange("p (l g m) -> p l g m", g=4, m=4)
        ctx.do('v', s2v, 'reduce_sum',
               out=ssum[:, 0:16].rearrange("p (l g) -> p l g", g=4),
               in_=eo4, axis=AX.X, waits=[(s2a, s2a.n), (s2v, s2v.n)])
        ctx.do('v', s2v, 'reciprocal', out=rcs[:, 0:16], in_=ssum[:, 0:16],
               waits=[(s2v, s2v.n)])
        av4 = attb2[:, par, :].rearrange("p (g l m) -> p g l m", l=4, m=4)
        eo_glm = eo.rearrange("p (l g m) -> p g l m", g=4, m=4)
        rc3 = rcs[:, 0:16].rearrange("p (l g) -> p g l", g=4)
        for m in range(4):
            ctx.do('v', s2v, 'tensor_mul', out=av4[:, :, :, m],
                   in0=eo_glm[:, :, :, m], in1=rc3,
                   waits=[(s2v, s2v.n)] if m == 0 else ())
        R[('smax', rc)] = (s2v, s2v.n)

    def back_group(rc, g):
        d, par, c0, bd = geom(rc)
        att = attb2[:, par, :]
        iw = [(s2v, s2v.n)]
        if g == 0:
            iw.append(R[('vtokevac', rc)])
        for l in range(4):
            cA = 16 * g + 4 * l
            ctx.do('v', s2v, 'tensor_scalar_mul',
                   out=zb1[:, l * 256:(l + 1) * 256],
                   in0=vtok[:, par, g, 0:256], scalar1=att[:, cA:cA + 1],
                   waits=iw if l == 0 else ())
        for mi, (src_, dst_) in enumerate(((zb1, zb2), (zb2, zb1), (zb1, zb2))):
            m = mi + 1
            ctx.w('v', s2v)
            for l in range(4):
                cA = 16 * g + 4 * l + m
                ctx.do('v', s2v, 'scalar_tensor_tensor',
                       out=dst_[:, l * 256:(l + 1) * 256],
                       in0=vtok[:, par, g, m * 256:(m + 1) * 256],
                       scalar=att[:, cA:cA + 1],
                       in1=src_[:, l * 256:(l + 1) * 256],
                       op0=ALU.mult, op1=ALU.add)
        R[('chainv',)] = (s2v, s2v.n)
        R[('z', rc)] = (s2v, s2v.n)
        for l in range(4):
            ctx.do('a', s2a, 'activation', out=junk256[:, :],
                   in_=zb2[:, l * 256:(l + 1) * 256], func=AF.Square,
                   accum_out=msb[:, 4 * g + l:4 * g + l + 1],
                   waits=[R[('chainv',)], (s2a, s2a.n)])
        ctx.do('a', s2a, 'activation', out=sdv[:, 4 * g:4 * g + 4],
               in_=msb[:, 4 * g:4 * g + 4], func=AF.Sqrt,
               scale=b['cc'][:, 3:4], bias=b['cc'][:, 1:2],
               waits=[(s2a, s2a.n)])
        ctx.do('v', s2v, 'reciprocal', out=rs0[:, 4 * g:4 * g + 4],
               in_=sdv[:, 4 * g:4 * g + 4], waits=[(s2a, s2a.n), (s2v, s2v.n)])
        for l in range(4):
            ctx.do('v', s2v, 'tensor_scalar_mul',
                   out=rsv[:, 4 * g + l:4 * g + l + 1],
                   in0=rs0[:, 4 * g + l:4 * g + l + 1],
                   scalar1=1.0 / (1.0 + l),
                   waits=[(s2v, s2v.n)] if l == 0 else ())
        ctx.w('v', s2v)
        for l in range(4):
            ctx.do('v', s2v, 'tensor_scalar_mul',
                   out=zpg[:, l * 256:(l + 1) * 256],
                   in0=zb2[:, l * 256:(l + 1) * 256],
                   scalar1=rsv[:, 4 * g + l:4 * g + l + 1])
        R[('zpg', rc, g)] = (s2v, s2v.n)
        ctx.w('t', *R[('zpg', rc, g)])
        if ('zTevacg', rc, g - 1) in R:
            ctx.w('t', *R[('zTevacg', rc, g - 1)])
        for p in range(PT):
            bank = 6 + (p // 4)
            ctx.do('t', s2p, 'transpose',
                   psb[:, bank, (p % 4) * 128:(p % 4 + 1) * 128],
                   zpg[:, p * 128:(p + 1) * 128], b['identb'][:, :])
        R[('zT', rc, g)] = (s2p, s2p.n)

    def back_group_evac(rc, g):
        ctx.w('a', *R[('zT', rc, g)])
        if g == 0 and ('finallast',) in R:
            ctx.w('a', *R[('finallast',)])
        for p in range(PT):
            bank = 6 + (p // 4)
            ctx.do('a', s2a, 'copy', out=zTs[:, p, g * 128:(g + 1) * 128],
                   in_=psb[:, bank, (p % 4) * 128:(p % 4 + 1) * 128],
                   waits=[(s2a, s2a.n)])
        R[('zTevacg', rc, g)] = (s2a, s2a.n)
        if g == 3:
            R[('zTevac', rc)] = (s2a, s2a.n)

    def back_final(rc):
        d, par, c0, bd = geom(rc)
        ctx.w('v', *R[('zTevac', rc)])
        ctx.w('v', s2v)
        ctx.w('v', s_gd)
        direct = (bd == 0 and d >= 512)
        if direct:
            for p in range(PT):
                la, _ = lslice(p, rc, 0, 512)
                ctx.do('v', s2v, 'tensor_add', out=X[:, p, c0:c0 + 512],
                       in0=zTs[:, p, 0:512], in1=la)
        else:
            for half in (1, 0):
                lo, hi = half * 256, half * 256 + 256
                lo2 = max(lo, bd)
                for p in range(PT):
                    if bd > lo:
                        e = min(bd, hi)
                        la, _ = lslice(p, rc, lo, e)
                        ctx.do('v', s2v, 'tensor_add',
                               out=tmp8[:, p, 0:e - lo], in0=zTs[:, p, lo:e],
                               in1=la)
                    if lo2 < hi:
                        la, _ = lslice(p, rc, lo2, hi)
                        ctx.do('v', s2v, 'tensor_add',
                               out=tmp8[:, p, lo2 - lo:hi - lo],
                               in0=zTs[:, p, lo2:hi], in1=la)
                ctx.w('v', s2v)
                for p in range(PT):
                    ctx.do('v', s2v, 'tensor_copy', out=X[:, p, c0 + lo:c0 + hi],
                           in_=tmp8[:, p, 0:256])
                ctx.w('v', s2v)
        R[('final', rc)] = (s2v, s2v.n)
        R[('finallast',)] = (s2v, s2v.n)

    def back_all(rc):
        for g in range(4):
            back_group(rc, g)
            back_group_evac(rc, g)
        back_final(rc)

    order = [chunk_at(i) for i in range(nrounds * NCH)]
    s_done = set()
    for i, rc in enumerate(order):
        new_round = (rc[1] == NCH - 1)
        r = rc[0]
        need_late = new_round and i >= 1 and (1 << r) > 512 * (NCH - 1)
        pv = order[i - 1] if i >= 1 else None
        if need_late:
            back_all(pv)
            pv = None
        if new_round:
            exchange_head(r, full=(r == 0 or need_late))
        if rc not in s_done:
            front_s(rc)
        front_vtok(rc)
        front_prod(rc, 0)
        front_prod(rc, 1)
        for bi in range(4):
            front_score(rc, bi)
            if pv is not None:
                back_group(pv, bi)
            if bi < 2:
                front_prod(rc, bi + 2)
            if bi == 2 and i + 1 < len(order):
                nxt = order[i + 1]
                if (nxt[0] == r or (nxt[0] == r + 1 and nxt[0] <= 9)) \
                        and nxt != (10, 2):
                    front_s(nxt)
                    s_done.add(nxt)
            front_smax(rc, bi)
            if bi == 3:
                front_smax_batch(rc)
            if pv is not None:
                back_group_evac(pv, bi)
        if pv is not None:
            back_final(pv)
        if new_round and not (r == 0 or need_late):
            exchange_tail(r)
    back_all(order[-1])

    if ctx.dbg == 'scan':
        sd = ctx.sems['g_dma']
        ctx.w('g', s2v)
        ctx.w('g', sd)
        for ci in range(PT):
            ctx.do('g', sd, 'dma_start', out=b['out_dbg'][ci * 128:(ci + 1) * 128, 0:HT],
                   in_=X[:, ci, 0:HT], inc_k=16)
        ctx.w('g', sd)


def emit_p3(ctx):
    b = ctx.b
    s_dma, s_gd, s_v, s_a, s_pe = (ctx.sems[k] for k in
                                   ('sy_dma', 'g_dma', 'v', 'a', 'pe'))
    X = b['xscan']

    emit_ln(ctx, b['in_xh'], 0, 16, b['xnh'], 0, b['xt3'], b['xnt3'])
    for ci in range(PT):
        ctx.do('g', s_gd, 'dma_start', out=b['wvbf'][:, ci, :],
               in_=b['in_wvT'][ci * 128:(ci + 1) * 128, :], inc_k=16)
    ctx.w('t', s_gd)
    ctx.w('t', s_a)
    ctx.w('t', s_v)
    pe_b = s_pe.n
    a_b = s_a.n
    gi = 0
    for jt in range(PT):
        for tc in range(4):
            bank = 2 + (gi % 4)
            if gi >= 4:
                ctx.w('t', s_a, a_b + gi - 4 + 1)
            ins = None
            for kt in range(PT):
                ins = ctx.op('t', 'matmul', b['ps'][:, bank, 0:512],
                             b['wvbf'][:, kt, jt * 128:(jt + 1) * 128],
                             b['xnh'][:, kt, tc * 512:(tc + 1) * 512],
                             start=(kt == 0), stop=(kt == PT - 1))
            if ins is not None:
                ins.then_inc(s_pe.h, 1)
            s_pe.n += 1
            ctx.do('a', s_a, 'copy', out=b['vbf'][:, jt, tc * 512:(tc + 1) * 512],
                   in_=b['ps'][:, bank, 0:512],
                   waits=[(s_pe, pe_b + gi + 1), (s_a, None)])
            gi += 1
    ctx.w('v', s_a)
    ctx.w('v', s_v)
    for ci in range(PT):
        ctx.do('v', s_v, 'tensor_mul', out=b['ubf'][:, ci, :], in0=X[:, ci, :],
               in1=b['vbf'][:, ci, :])
    ctx.w('g', s_v)
    for ci in range(PT):
        ctx.do('g', s_gd, 'dma_start', out=b['wprojbf'][:, ci, :],
               in_=b['in_wprojT'][ci * 128:(ci + 1) * 128, :], inc_k=16)
    ctx.w('t', s_gd)
    ctx.w('t', s_v)
    pe_b, v_b = s_pe.n, s_v.n
    for tg in range(16):
        par = tg % 2
        xgp = ctx.sems['xg0'] if par == 0 else ctx.sems['xg1']
        ctx.do('sy', xgp, 'dma_start', out=b['xtg'][:, par, :],
               in_=b['in_xh'][tg * 128:(tg + 1) * 128, :],
               waits=[(s_v, v_b + (tg - 2) * 2 + 2)] if tg >= 2 else (), inc_k=16)
        if tg >= 2:
            ctx.w('t', s_v, v_b + (tg - 2) * 2 + 2)
        for jc in range(2):
            ins = None
            for kt in range(PT):
                ins = ctx.op('t', 'matmul', b['ps'][:, 2 * par + jc, 0:512],
                             b['ubf'][:, kt, tg * 128:(tg + 1) * 128],
                             b['wprojbf'][:, kt, jc * 512:(jc + 1) * 512],
                             start=(kt == 0), stop=(kt == PT - 1))
            if ins is not None:
                ins.then_inc(s_pe.h, 1)
            s_pe.n += 1
        ctx.w('v', s_pe, pe_b + 2 * (tg + 1))
        ctx.w('v', xgp)
        ctx.w('v', s_gd)
        for jc in range(2):
            ctx.do('v', s_v, 'tensor_add', out=b['oat'][:, jc * 512:(jc + 1) * 512],
                   in0=b['ps'][:, 2 * par + jc, 0:512],
                   in1=b['xtg'][:, par, jc * 512:(jc + 1) * 512])
        ctx.do('g', s_gd, 'dma_start', out=b['oasp'][tg * 128:(tg + 1) * 128, :],
               in_=b['oat'][:, :], waits=[(s_v, None)], inc_k=16)
    ctx.w('g', s_a)
    ctx.do('g', s_gd, 'dma_start', out=b['xnsp'][:, :, :], in_=b['xnh'][:, :, :],
           inc_k=16)
    ctx.w('g', s_gd)


def emit_p4(ctx):
    b = ctx.b
    s_dma, s_gd, s_v, s_a, s_pe = (ctx.sems[k] for k in
                                   ('sy_dma', 'g_dma', 'v', 'a', 'pe'))
    s_out = ctx.sems['outd']
    psh = b['ps']
    abuf, bbuf, w3v = b['abuf'], b['bbuf'], b['w3bf']
    xns = b['xnhsub']
    ctx.w('sy', s_dma)
    ctx.do('sy', s_dma, 'dma_start', out=b['s1c'][:, :], in_=b['in_s1c'][:, :], inc_k=16)
    ctx.do('sy', s_dma, 'dma_start', out=b['s2c'][:, :], in_=b['in_s2c'][:, :], inc_k=16)
    ctx.do('sy', s_dma, 'dma_start', out=b['sh1c'][:, :], in_=b['in_sh1c'][:, :], inc_k=16)
    ctx.do('sy', s_dma, 'dma_start', out=b['sh2c'][:, :], in_=b['in_sh2c'][:, :], inc_k=16)
    for e in ('v', 'a', 't'):
        ctx.w(e, s_dma)
    R = {}

    for sh in range(2):
        t0 = sh * 1024
        ctx.w('sy', s_dma)
        if ('w12mm', sh - 1) in R:
            ctx.w('sy', *R[('w12mm', sh - 1)])
        for ci in range(PT):
            ctx.do('sy', s_dma, 'dma_start', out=xns[:, ci, :],
                   in_=b['xnsp'][:, ci, t0:t0 + 1024], inc_k=16)
        for js in range(8):
            par = js % 2
            stp = ctx.sems['st0'] if par == 0 else ctx.sems['st1']
            w = [R[('w12s', sh, js - 2)]] if ('w12s', sh, js - 2) in R else \
                ([R[('hmm_last',)]] if ('hmm_last',) in R else [])
            ctx.do('g', stp, 'dma_start', out=b['w1st'][:, par],
                   in_=b['in_w1s'][js].rearrange("(a b) j -> b a j", b=128),
                   waits=w, inc_k=16)
            ctx.do('g', stp, 'dma_start', out=b['w2st'][:, par],
                   in_=b['in_w2s'][js].rearrange("(a b) j -> b a j", b=128),
                   inc_k=16)
            ctx.w('t', stp)
            if js == 0:
                ctx.w('t', s_dma)
                ctx.w('t', s_a)
                ctx.w('t', s_v)
            gi0 = js * 8
            for jt in range(4):
                for tc in range(2):
                    gi = gi0 + jt * 2 + tc
                    b1, b2 = 2 * (gi % 2), 2 * (gi % 2) + 1
                    if ('aevac', sh, gi - 2) in R:
                        ctx.w('t', *R[('aevac', sh, gi - 2)])
                        ctx.w('t', *R[('vstt', sh, gi - 2)])
                    for (wst, bank) in ((b['w1st'], b1), (b['w2st'], b2)):
                        ins = None
                        for kt in range(PT):
                            ins = ctx.op('t', 'matmul', psh[:, bank, 0:512],
                                         wst[:, par, kt, jt * 128:(jt + 1) * 128],
                                         xns[:, kt, tc * 512:(tc + 1) * 512],
                                         start=(kt == 0), stop=(kt == PT - 1))
                        if ins is not None:
                            ins.then_inc(s_pe.h, 1)
                        s_pe.n += 1
                    R[('w12g', sh, gi)] = (s_pe, s_pe.n)
                    jg = js * 4 + jt
                    ctx.w('a', *R[('w12g', sh, gi)])
                    ta = b['tmpa'][:, (gi % 2) * 512:(gi % 2) * 512 + 512]
                    wl = [(s_a, s_a.n)]
                    if ('vstt', sh, gi - 2) in R:
                        wl.append(R[('vstt', sh, gi - 2)])
                    if gi == 0:
                        wl.append((s_v, s_v.n))
                    ctx.do('a', s_a, 'activation', out=ta,
                           in_=psh[:, b1, 0:512], func=AF.Silu,
                           scale=b['s1c'][:, jg:jg + 1], waits=wl)
                    R[('aevac', sh, gi)] = (s_a, s_a.n)
                    ctx.do('v', s_v, 'scalar_tensor_tensor',
                           out=abuf[:, jg, tc * 512:(tc + 1) * 512],
                           in0=psh[:, b2, 0:512], scalar=b['s2c'][:, jg:jg + 1],
                           in1=ta, op0=ALU.mult, op1=ALU.mult,
                           waits=[R[('aevac', sh, gi)], (s_v, s_v.n)])
                    R[('vstt', sh, gi)] = (s_v, s_v.n)
            R[('w12s', sh, js)] = (s_pe, s_pe.n)
        R[('w12mm', sh)] = (s_pe, s_pe.n)
        emit_h_layer(ctx, b['in_h1s'], b['h1st'], abuf, bbuf, b['sh1c'], R,
                     lastv=R[('vstt', sh, 63)])
        emit_h_layer(ctx, b['in_h2s'], b['h1st'], bbuf, abuf, b['sh2c'], R,
                     lastv=None)
        for kt2 in range(4):
            ctx.do('g', s_gd, 'dma_start', out=w3v[:, kt2 * 8:(kt2 + 1) * 8, :],
                   in_=b['in_w3T'].rearrange("(a b) j -> b a j", b=128)
                   [:, kt2 * 8:(kt2 + 1) * 8, :],
                   waits=[(s_gd, s_gd.n), (s_pe, s_pe.n), (s_a, s_a.n)]
                   if kt2 == 0 else (), inc_k=16)
        ctx.w('sy', s_dma)
        ctx.do('sy', s_dma, 'dma_start', out=b['s3r'][0:1, :], in_=b['in_s3r'][:, :],
               waits=[(s_pe, s_pe.n)] if sh == 0 else (), inc_k=16)
        ctx.do('t', s_pe, 'matmul', psh[:, 6, 0:512], b['onesr'][0:1, :],
               b['s3r'][0:1, 0:512], start=True, stop=True,
               waits=[(s_dma, s_dma.n), (s_a, s_a.n), (s_v, s_v.n)])
        ctx.do('t', s_pe, 'matmul', psh[:, 7, 0:512], b['onesr'][0:1, :],
               b['s3r'][0:1, 512:1024], start=True, stop=True)
        ctx.do('a', s_a, 'copy', out=b['s3bc'][:, 0:512], in_=psh[:, 6, 0:512],
               waits=[(s_pe, s_pe.n), (s_a, s_a.n)])
        ctx.do('a', s_a, 'copy', out=b['s3bc'][:, 512:1024], in_=psh[:, 7, 0:512],
               waits=[(s_a, s_a.n)])
        ctx.w('t', s_gd)
        ctx.w('t', s_a)
        pe_b3, v_b3 = s_pe.n, s_v.n
        for tg in range(8):
            par = tg % 2
            xgp = ctx.sems['xg0'] if par == 0 else ctx.sems['xg1']
            ctx.do('sy', xgp, 'dma_start', out=b['oatile'][:, par, :],
                   in_=b['oasp'][t0 + tg * 128:t0 + (tg + 1) * 128, :],
                   waits=([(s_v, v_b3 + (tg - 2) * 3 + 3)] if tg >= 2 else [])
                   + [(s_out, s_out.n)],
                   inc_k=16)
            if tg >= 2:
                ctx.w('t', s_v, v_b3 + (tg - 2) * 3 + 3)
            for jc in range(2):
                ins = None
                for kt in range(32):
                    ins = ctx.op('t', 'matmul', psh[:, 2 * par + jc, 0:512],
                                 abuf[:, kt, tg * 128:(tg + 1) * 128],
                                 w3v[:, kt, jc * 512:(jc + 1) * 512],
                                 start=(kt == 0), stop=(kt == 31))
                if ins is not None:
                    ins.then_inc(s_pe.h, 1)
                s_pe.n += 1
            ctx.w('v', s_pe, pe_b3 + 2 * (tg + 1))
            ctx.w('v', xgp)
            ctx.w('v', s_out)
            ctx.w('v', s_v)
            for jc in range(2):
                ctx.do('v', s_v, 'tensor_mul', out=b['tmpa'][:, jc * 512:(jc + 1) * 512],
                       in0=psh[:, 2 * par + jc, 0:512],
                       in1=b['s3bc'][:, jc * 512:(jc + 1) * 512])
            ctx.w('v', s_v)
            ctx.do('v', s_v, 'tensor_add', out=b['oatile'][:, par, :],
                   in0=b['tmpa'][:, :], in1=b['oatile'][:, par, :])
            ctx.do('g', s_out, 'dma_start',
                   out=b['out_h'][t0 + tg * 128:t0 + (tg + 1) * 128, :],
                   in_=b['oatile'][:, par, :], waits=[(s_v, s_v.n)], inc_k=16)
    ctx.w('g', s_out)
    ctx.w('sy', s_out)


def emit_h_layer(ctx, in_strips, stbuf, src, dst, scol, R, lastv=None):
    s_gd, s_v, s_a, s_pe = (ctx.sems[k] for k in ('g_dma', 'v', 'a', 'pe'))
    psh = ctx.b['ps']
    key = ctx.nc.next_id() if False else id(in_strips)
    for js in range(16):
        par = js % 2
        stp = ctx.sems['st0'] if par == 0 else ctx.sems['st1']
        w = [R[('hs', key, js - 2)]] if ('hs', key, js - 2) in R else [(s_pe, s_pe.n)]
        ctx.do('g', stp, 'dma_start', out=stbuf[:, par],
               in_=in_strips[js].rearrange("(a b) j -> b a j", b=128),
               waits=w, inc_k=16)
        ctx.w('t', stp)
        if js == 0:
            ctx.w('t', s_a)
            if lastv is not None:
                ctx.w('t', *lastv)
            else:
                ctx.w('t', s_v)
        for jt in range(2):
            for tc in range(2):
                gi = js * 4 + jt * 2 + tc
                bank = gi % 6
                if ('hg', key, gi - 6) in R:
                    ctx.w('t', *R[('hg', key, gi - 6)])
                ins = None
                for kt in range(32):
                    ins = ctx.op('t', 'matmul', psh[:, bank, 0:512],
                                 stbuf[:, par, kt, jt * 128:(jt + 1) * 128],
                                 src[:, kt, tc * 512:(tc + 1) * 512],
                                 start=(kt == 0), stop=(kt == 31))
                if ins is not None:
                    ins.then_inc(s_pe.h, 1)
                s_pe.n += 1
                jg = js * 2 + jt
                ctx.w('a', s_pe, s_pe.n)
                ctx.do('a', s_a, 'activation', out=dst[:, jg, tc * 512:(tc + 1) * 512],
                       in_=psh[:, bank, 0:512], func=AF.Gelu,
                       scale=scol[:, jg:jg + 1], waits=[(s_a, s_a.n)])
                R[('hg', key, gi)] = (s_a, s_a.n)
        R[('hs', key, js)] = (s_pe, s_pe.n)
    R[('hmm_last',)] = (s_pe, s_pe.n)


def build(dbg=None, dbg_rounds=12):
    nc = bass.Bass("TRN2", target_bir_lowering=False)
    din = {}
    D = nc.dram_tensor
    din['in_gsr'] = D("gsr", [C, C], F32, kind="ExternalInput").ap()
    din['in_xh'] = D("xh", [HT, C], F32, kind="ExternalInput").ap()
    din['in_idh'] = D("idh", [128, PT, HT], F32, kind="ExternalInput").ap()
    din['in_wupT'] = D("wupT", [C, C], F32, kind="ExternalInput").ap()
    din['in_wvT'] = D("wvT", [C, C], F32, kind="ExternalInput").ap()
    din['in_wprojT'] = D("wprojT", [C, C], F32, kind="ExternalInput").ap()
    din['in_identc'] = D("identc", [128, PT], F32, kind="ExternalInput").ap()
    din['in_eye'] = D("eye128", [128, 128], F32, kind="ExternalInput").ap()
    din['in_eyeb'] = D("eye128b", [128, 128], BF16, kind="ExternalInput").ap()
    din['in_onesc'] = D("onescol", [128, 1], F32, kind="ExternalInput").ap()
    din['in_onesb'] = D("onescolb", [128, 1], BF16, kind="ExternalInput").ap()
    din['in_onesr'] = D("onesrow", [1, 128], F32, kind="ExternalInput").ap()
    din['in_hoff'] = D("hoff", [1, 1], I32, kind="ExternalInput").ap()
    din['in_w1s'] = [D(f"w1s{j}", [C, 512], F32, kind="ExternalInput").ap() for j in range(8)]
    din['in_w2s'] = [D(f"w2s{j}", [C, 512], F32, kind="ExternalInput").ap() for j in range(8)]
    din['in_h1s'] = [D(f"h1s{j}", [DH, 256], F32, kind="ExternalInput").ap() for j in range(16)]
    din['in_h2s'] = [D(f"h2s{j}", [DH, 256], F32, kind="ExternalInput").ap() for j in range(16)]
    din['in_w3T'] = D("w3T", [DH, C], F32, kind="ExternalInput").ap()
    din['in_s1c'] = D("s1c_i", [128, 32], F32, kind="ExternalInput").ap()
    din['in_s2c'] = D("s2c_i", [128, 32], F32, kind="ExternalInput").ap()
    din['in_sh1c'] = D("sh1c_i", [128, 32], F32, kind="ExternalInput").ap()
    din['in_sh2c'] = D("sh2c_i", [128, 32], F32, kind="ExternalInput").ap()
    din['in_s3r'] = D("s3r_i", [1, C], F32, kind="ExternalInput").ap()
    out_h = D("out_h", [HT, C], F32, kind="ExternalOutput").ap()
    out_dbg = None
    if dbg in ('ns', 'q', 'scan'):
        out_dbg = D("dbgout", [C, 4096], F32, kind="ExternalOutput").ap()
    oasp = D("oasp", [HT, C], F32, kind="Internal").ap()
    hcin = [D(f"hcin{r}", [128, PT, 1 << r], BF16, kind="Internal").ap()
            for r in range(12)]
    hcout = [D(f"hcout{r}", [2, 128, PT, 1 << r], BF16, kind="Internal").ap()
             for r in range(12)]
    xnsp_h = D("xnsp", [128, PT, HT], BF16, kind="Internal")
    xnsp = xnsp_h.ap()

    st = {}

    def salloc(name, shape, dtype):
        st[name] = nc.alloc_sbuf_tensor(name, shape, dtype).ap()
        return st[name]

    identf = salloc("identf", [128, 128], F32)
    identb = salloc("identb", [128, 128], BF16)
    onesc = salloc("onesc", [128, 1], F32)
    onesb = salloc("onesb", [128, 1], BF16)
    onesr = salloc("onesr", [1, 128], F32)
    identcol = salloc("identcol", [128, PT], F32)
    lncols = salloc("lncols", [128, 12], F32)
    fcol = salloc("fcol", [128, 16], F32)
    sc_col = salloc("sc_col", [128, 2], F32)
    hoffs = salloc("hoffs", [1, 1], I32)
    s1c = salloc("s1c", [128, 32], F32)
    s2c = salloc("s2c", [128, 32], F32)
    sh1c = salloc("sh1c", [128, 32], F32)
    sh2c = salloc("sh2c", [128, 32], F32)
    nmx = salloc("nmx", [128, 16], F32)
    esr = salloc("esr", [128, 64], F32)
    eex = salloc("eex", [128, 64], F32)

    ssum = salloc("ssum", [128, 16], F32)
    rcs = salloc("rcs", [128, 16], F32)
    msb = salloc("msb", [128, 16], F32)
    sdv = salloc("sdv", [128, 16], F32)
    rs0 = salloc("rs0", [128, 16], F32)
    rsv = salloc("rsv", [128, 16], F32)
    cc = salloc("cc", [128, 4], F32)
    ones32b = salloc("ones32b", [128, 32], BF16)
    scs32 = salloc("scs32", [128, 512], F32)
    attb2 = salloc("attb2", [128, 2, 64], F32)
    junk256 = salloc("junk256", [128, 256], F32)

    arena_h = nc.alloc_sbuf_tensor("arena", [128, (nc.sbuf_bytes_remaining // 64) * 64 - 128], U8)
    ar = arena_h.ap()
    ARN = ar.shape[1]
    assert ARN >= 180 * KB, ARN

    def V2(off, dtype, d1, d2):
        esz = {F32: 4, F32R: 4, BF16: 2, I32: 4}[dtype]
        n = d1 * d2 * esz
        assert off % 64 == 0 and off + n <= ARN, (off, n, ARN)
        return ar[:, off:off + n].bitcast(dtype).rearrange("p (a b) -> p a b", b=d2)

    def V1(off, dtype, d1):
        esz = {F32: 4, F32R: 4, BF16: 2}[dtype]
        n = d1 * esz
        assert off % 64 == 0 and off + n <= ARN, (off, n, ARN)
        return ar[:, off:off + n].bitcast(dtype)

    KBy = 1024
    nsA = V2(0, F32R, PT, C)
    nsB = V2(32 * KBy, F32R, PT, C)
    nsC = V2(64 * KBy, F32R, PT, C)
    nsD = V2(96 * KBy, F32R, PT, C)
    nsE = V2(128 * KBy, F32R, PT, C)
    nsE32 = V2(128 * KBy, F32, PT, C)
    Wbf = V2(128 * KBy, BF16, PT, C)
    xscan = V2(0, F32, PT, HT)
    hz = V2(64 * KBy, BF16, PT, HT)
    xnfq = V2(96 * KBy, BF16, PT, 512)
    xt1 = V2(112 * KBy, F32, 2, 1024)
    xnt1 = V2(120 * KBy, BF16, 2, 1024)
    wupst = V2(124 * KBy, BF16, 2 * PT, 128).rearrange("p (a b) c -> p a b c", b=PT)
    xnful = V2(160 * KBy, BF16, PT, HT)
    xtl = V2(192 * KBy, F32, 1, 1024)
    xntl = V2(196 * KBy, BF16, 2, 1024)
    wupq = V2(96 * KBy, BF16, PT, C)
    sbb = V2(96 * KBy, BF16, PT, 512)
    vtok = V2(104 * KBy, BF16, 2 * 4, 1024).rearrange("p (a b) c -> p a b c", b=4)
    zTs = V2(120 * KBy, BF16, PT, 512)
    prodb = V2(144 * KBy, BF16, 8, 512)
    prodb2 = V2(166 * KBy, BF16, 8, 512)
    tmp8 = V2(152 * KBy, F32, PT, 256)
    zb1 = V1(160 * KBy, BF16, 1024)
    zb2 = V1(162 * KBy, BF16, 1024)
    zpg = V1(164 * KBy, BF16, 1024)
    vbf = V2(64 * KBy, BF16, PT, HT)
    ubf = V2(96 * KBy, BF16, PT, HT)
    xnh = V2(128 * KBy, BF16, PT, HT)
    wvbf = V2(160 * KBy, BF16, PT, C)
    wprojbf = V2(160 * KBy, BF16, PT, C)
    xt3 = V2(176 * KBy, F32, 2, 1024)
    xnt3 = V2(184 * KBy, BF16, 2, 1024)
    oat = V1(188 * KBy, F32, 1024)
    xtg = V2(192 * KBy, F32, 2, 1024)
    Wbf = V2(128 * KBy, BF16, PT, C)
    abuf = V2(0, BF16, 32, 1024)
    bbuf = V2(64 * KBy, BF16, 32, 1024)
    w3bf = V2(64 * KBy, BF16, 32, 1024)
    w1st = V2(128 * KBy, BF16, 2 * PT, 512).rearrange("p (a b) c -> p a b c", b=PT)
    w2st = V2(144 * KBy, BF16, 2 * PT, 512).rearrange("p (a b) c -> p a b c", b=PT)
    h1st = V2(128 * KBy, BF16, 2 * 32, 256).rearrange("p (a b) c -> p a b c", b=32)
    xnhsub = V2(160 * KBy, BF16, PT, 1024)
    s3bc = V1(176 * KBy, F32, 1024)
    tmpa = V1(180 * KBy, F32, 1024)
    s3r = V2(184 * KBy, F32, 1, 1024)[0:1, 0, :].rearrange("p a -> p a")
    oatile = V2(188 * KBy, F32, 2, 1024)

    with (
        nc.psum_tensor("ps", [128, PT, 512], F32) as ps,
        nc.semaphore("sy_dma") as h_sd,
        nc.semaphore("g_dma") as h_gd,
        nc.semaphore("pe") as h_pe,
        nc.semaphore("v") as h_v,
        nc.semaphore("a") as h_a,
        nc.semaphore("outd") as h_od,
        nc.semaphore("ln0") as h_l0,
        nc.semaphore("ln1") as h_l1,
        nc.semaphore("st0") as h_s0,
        nc.semaphore("st1") as h_s1,
        nc.semaphore("xg0") as h_x0,
        nc.semaphore("xg1") as h_x1,
        nc.semaphore("cc") as h_cc,
        nc.semaphore("hx") as h_hx,
        nc.Block() as block,
    ):
        sems = {'sy_dma': Sem(h_sd), 'g_dma': Sem(h_gd), 'pe': Sem(h_pe),
                'v': Sem(h_v), 'a': Sem(h_a), 'outd': Sem(h_od),
                'ln0': Sem(h_l0), 'ln1': Sem(h_l1), 'st0': Sem(h_s0),
                'st1': Sem(h_s1), 'xg0': Sem(h_x0), 'xg1': Sem(h_x1),
                'cc': Sem(h_cc), 'hx': Sem(h_hx)}
        bufs = dict(
            nsA=nsA, nsB=nsB, nsC=nsC, nsD=nsD, nsE=nsE, nsE32=nsE32, Wbf=Wbf,
            xscan=xscan, hz=hz, xnfq=xnfq, xt1=xt1, xnt1=xnt1, wupst=wupst,
            xnful=xnful, xtl=xtl, xntl=xntl, wupq=wupq,
            sbb=sbb, vtok=vtok, tmp8=tmp8, zTs=zTs, prodb=prodb, prodb2=prodb2,
            zb1=zb1, zb2=zb2, zpg=zpg,
            vbf=vbf, ubf=ubf, xnh=xnh, wvbf=wvbf, wprojbf=wprojbf,
            xt3=xt3, xnt3=xnt3, oat=oat, xtg=xtg,
            abuf=abuf, bbuf=bbuf, w3bf=w3bf, w1st=w1st, w2st=w2st, h1st=h1st,
            xnhsub=xnhsub, s3bc=s3bc, tmpa=tmpa, s3r=s3r, oatile=oatile,
            ps=ps.ap(), psb=ps.ap().bitcast(BF16), oasp=oasp, xnsp=xnsp,
            hcin=hcin, hcout=hcout, out_h=out_h,
            out_dbg=out_dbg, dbg_rounds=dbg_rounds,
            **st, **din,
        )

        def emit_all(ctx):
            s_dma = ctx.sems['sy_dma']
            for (dst, src) in ((identf[:, :], din['in_eye'][:, :]),
                               (identb[:, :], din['in_eyeb'][:, :]),
                               (onesc[:, :], din['in_onesc'][:, :]),
                               (onesb[:, :], din['in_onesb'][:, :]),
                               (onesr[:, :], din['in_onesr'][:, :]),
                               (identcol[:, :], din['in_identc'][:, :]),
                               (hoffs[:, :], din['in_hoff'][:, :])):
                ctx.do('sy', s_dma, 'dma_start', out=dst, in_=src, inc_k=16)
            s_v0 = ctx.sems['v']
            for (col, val) in ((0, 1e-5), (1, 1e-8), (2, 1.0 / 16.0), (3, 1.0 / 256.0)):
                ctx.do('v', s_v0, 'memset', st['cc'][:, col:col + 1], val)
            ctx.do('v', s_v0, 'memset', st['ones32b'][:, :], 1.0)
            for e in ('v', 'a', 't', 'g'):
                ctx.w(e, s_dma)
            for e in ('a', 't', 'g'):
                ctx.w(e, s_v0)
            emit_ns(ctx)
            if ctx.dbg == 'ns':
                return
            ctx.sync_all()
            emit_p1(ctx)
            if ctx.dbg == 'q':
                return
            ctx.sync_all()
            emit_scan(ctx)
            if ctx.dbg == 'scan':
                return
            ctx.sync_all()
            emit_p3(ctx)
            ctx.sync_all()
            emit_p4(ctx)

        engines = {'v': nc.vector, 'a': nc.scalar, 't': nc.tensor,
                   'sy': nc.sync, 'g': nc.gpsimd}

        def begin():
            for s in sems.values():
                s.n = 0

        @block.sync
        def _(sync):
            begin(); emit_all(Ctx(nc, 'sy', engines, sems, bufs, dbg))

        @block.vector
        def _(vector):
            begin(); emit_all(Ctx(nc, 'v', engines, sems, bufs, dbg))

        @block.scalar
        def _(scalar):
            begin(); emit_all(Ctx(nc, 'a', engines, sems, bufs, dbg))

        @block.tensor
        def _(tensor):
            begin(); emit_all(Ctx(nc, 't', engines, sems, bufs, dbg))

        @block.gpsimd
        def _(gpsimd):
            begin(); emit_all(Ctx(nc, 'g', engines, sems, bufs, dbg))

    return nc


def host_inputs(z, core):
    import ml_dtypes
    seq, hh = core // 2, core % 2
    A = np.ascontiguousarray
    x = np.asarray(z['L0_attn_x'])
    identc = np.asarray(z['L0_attn_identity']).reshape(PT, 128).T
    d = {
        'gsr': A(np.asarray(z['L0_attn_W_scan_raw'])),
        'idh': A(np.broadcast_to(identc[:, :, None], (128, PT, HT)).astype(np.float32)),
        'xh': A(x[seq, hh * HT:(hh + 1) * HT]),
        'wupT': A(np.asarray(z['L0_attn_W_up']).T),
        'wvT': A(np.asarray(z['L0_attn_W_v']).T),
        'wprojT': A(np.asarray(z['L0_attn_W_proj']).T),
        'identc': A(np.asarray(z['L0_attn_identity']).reshape(PT, 128).T),
        'eye128': np.eye(128, dtype=np.float32),
        'eye128b': np.eye(128, dtype=np.float32).astype(ml_dtypes.bfloat16),
        'onescol': np.ones((128, 1), np.float32),
        'onescolb': np.ones((128, 1), np.float32).astype(ml_dtypes.bfloat16),
        'onesrow': np.ones((1, 128), np.float32),
        'hoff': np.array([[hh * HT]], np.int32),
        'w3T': A(np.asarray(z['L0_ffn_w3_w']).T),
        's1c_i': A(np.asarray(z['L0_ffn_w1_s']).reshape(32, 128).T),
        's2c_i': A(np.asarray(z['L0_ffn_w2_s']).reshape(32, 128).T),
        'sh1c_i': A(np.asarray(z['L0_ffn_h1_s']).reshape(32, 128).T),
        'sh2c_i': A(np.asarray(z['L0_ffn_h2_s']).reshape(32, 128).T),
        's3r_i': A(np.asarray(z['L0_ffn_w3_s']).reshape(1, C)),
    }
    w1T = np.asarray(z['L0_ffn_w1_w']).T
    w2T = np.asarray(z['L0_ffn_w2_w']).T
    h1T = np.asarray(z['L0_ffn_h1_w']).T
    h2T = np.asarray(z['L0_ffn_h2_w']).T
    for j in range(8):
        d[f'w1s{j}'] = A(w1T[:, j * 512:(j + 1) * 512])
        d[f'w2s{j}'] = A(w2T[:, j * 512:(j + 1) * 512])
    for j in range(16):
        d[f'h1s{j}'] = A(h1T[:, j * 256:(j + 1) * 256])
        d[f'h2s{j}'] = A(h2T[:, j * 256:(j + 1) * 256])
    return d


def _patch_walrus_verifier():
    import concourse.bass_utils as _bu

    def bir_verify_and_optimise(tmpdir, inp="bir.json", outp="file.neff",
                                arch=None, *, dve_root=None):
        from pathlib import Path as _P
        cmd = [
            _bu.get_walrus_driver(),
            "--pass",
            ",".join([
                "runtime_memory_reservation",
                "lower_act",
                "lower_dve",
                "lower_ap_offset",
                "codegen",
                "neff_packager",
            ]),
            "-i", inp,
        ]
        if _bu.checkenv("CONCOURSE_WALRUS_GENERATE_MANIFEST"):
            cmd += ["-o", "walrus_bir.out.json"]
        cmd += [
            "--neff-output-filename", outp,
            "--enable-birsim=true",
            "--mem-mode=physical",
            "--policy=0",
            "--enable-ldw-opt=false",
            "--assign-static-dmas-to-sp=false",
            f"--dram-page-size={_bu.aot_getenv('NEURON_SCRATCHPAD_PAGE_SIZE', '256')}",
            f"--enable-neff-debug-info={'false' if _bu.aot_checkenv('CONCOURSE_SCRUB_NEFF_DEBUG_INFO') else 'true'}",
            "--jobs", "8",
            *_bu.get_walrus_args(
                _bu.get_bir_arch(tmpdir, inp) if arch is None else arch,
                tmpdir, dve_root=dve_root),
        ]
        result = _bu.run_command(cmd, cwd=tmpdir)
        if result is not None:
            (_P(tmpdir) / "log.txt").write_text(result.stdout)
        return f"{tmpdir}/{outp}"

    _bu.bir_verify_and_optimise = bir_verify_and_optimise


_patch_walrus_verifier()


_NC_CACHE = {}


def kernel(**inputs):
    if 'nc' not in _NC_CACHE:
        _NC_CACHE['nc'] = build(dbg=None)
    nc = _NC_CACHE['nc']
    in_maps = [host_inputs(inputs, core) for core in range(8)]
    res = run_bass_kernel_spmd(nc, in_maps, core_ids=list(range(8)))
    out = np.empty((B, T, C), np.float32)
    for core in range(8):
        seq, hh = core // 2, core % 2
        out[seq, hh * HT:(hh + 1) * HT] = np.asarray(res.results[core]['out_h'])
    return out
